# version 02
# baseline (speedup 1.0000x reference)
import numpy as np
import ml_dtypes
from contextlib import ExitStack

import concourse.bass as bass
import concourse.mybir as mybir
from concourse.bass_utils import run_bass_kernel_spmd

F32 = mybir.dt.float32
BF16 = mybir.dt.bfloat16
I32 = mybir.dt.int32
AF = mybir.ActivationFunctionType
ALU = mybir.AluOpType
AX = mybir.AxisListType

MAGIC = float(np.float32(1.5 * 2 ** 23))
TWO_PI_HI = float(np.float32(2.0 * np.pi))
TWO_PI_LO = float(np.float32(2.0 * np.pi - np.float64(np.float32(2.0 * np.pi))))
INV_2PI = float(np.float32(1.0 / (2.0 * np.pi)))
PIO2_HI = float(np.float32(0.5 * np.pi))
PIO2_LO = float(np.float32(0.5 * np.pi - np.float64(np.float32(0.5 * np.pi))))
PI_F = float(np.float32(np.pi))

NCORES = 8
B, N, C = 16, 2048, 1024
BPC = B // NCORES
TPC = BPC * N
NT = TPC // 128
NTS = N // 128
H, D = 16, 64
KCH = C // 128

BF = ml_dtypes.bfloat16


class Sem:
    def __init__(self, h):
        self.h = h
        self.n = 0


class Sched:

    ENGS = ("sync", "vector", "scalar", "tensor", "gpsimd", "gdma")
    AUTO = ("vector", "scalar", "gpsimd")

    def __init__(self, esems):
        self.ev = {e: [] for e in self.ENGS}
        self.esems = esems

    def step(self, eng, waits, fn, inc=None):
        waits = [(s, v) for (s, v) in waits if v > 0]
        ret = None
        if eng in self.AUTO:
            assert inc is None or inc[0] is self.esems[eng]
            es = self.esems[eng]
            if es.n > 0:
                waits.append((es, es.n))
            es.n += 1
            inc = (es, 1)
            ret = es.n
        elif inc is not None:
            sem, amt = inc
            sem.n += amt
            ret = sem.n

        def run(h, waits=tuple(waits), fn=fn, inc=inc):
            for s, v in waits:
                h.wait_ge(s.h, v)
            r = fn(h)
            if inc is not None:
                sem, amt = inc
                if r is None:
                    r = h.nop()
                r.then_inc(sem.h, amt)
        self.ev["gpsimd" if eng == "gdma" else eng].append(run)
        return ret

    def emit(self, block):
        merged = dict(self.ev)
        for eng in self.ENGS:
            if eng == "gdma":
                continue
            fns = self.ev[eng]
            if not fns:
                continue
            def make(fns):
                def body(h):
                    for f in fns:
                        f(h)
                return body
            getattr(block, eng)(make(fns))



def build_stage_a(phase=9, sub=9, ntiles=NTS):
    nc = bass.Bass("TRN2", target_bir_lowering=False)
    xA = nc.dram_tensor("xA", [TPC, C], F32, kind="ExternalInput").ap()
    wqT = nc.dram_tensor("wqT", [C, C], F32, kind="ExternalInput").ap()
    wkT = nc.dram_tensor("wkT", [C, C], F32, kind="ExternalInput").ap()
    wvT = nc.dram_tensor("wvT", [C, C], F32, kind="ExternalInput").ap()
    bqkv = nc.dram_tensor("bqkv", [3, C], F32, kind="ExternalInput").ap()
    gb = nc.dram_tensor("gb", [2, C], F32, kind="ExternalInput").ap()
    wg = nc.dram_tensor("wg", [C, 8], F32, kind="ExternalInput").ap()
    posT = nc.dram_tensor("posT", [1, N], F32, kind="ExternalInput").ap()
    jfrac = nc.dram_tensor("jfrac", [1, C // 2], F32, kind="ExternalInput").ap()
    x2_out = nc.dram_tensor("x2_out", [TPC, C], F32, kind="ExternalOutput").ap()
    probs_out = nc.dram_tensor("probs_out", [TPC, 8], F32, kind="ExternalOutput").ap()

    with ExitStack() as ctx:
        sbt = lambda n, s, d=F32: ctx.enter_context(nc.sbuf_tensor(n, s, d))
        pst = lambda n, s, d=F32: ctx.enter_context(nc.psum_tensor(n, s, d))

        wq_b = sbt("wq_b", [128, KCH * C], BF16)
        wk_b = sbt("wk_b", [128, KCH * C], BF16)
        wv_b = sbt("wv_b", [128, KCH * C], BF16)
        wg_b = sbt("wg_b", [128, KCH * 8], BF16)
        bq_b = sbt("bq_b", [1, C], BF16)
        bk_b = sbt("bk_b", [1, C], BF16)
        bv_b = sbt("bv_b", [1, C], BF16)
        bias_rows = [bq_b, bk_b, bv_b]
        gam_b = sbt("gam_b", [1, C], BF16)
        bet_b = sbt("bet_b", [1, C], BF16)
        gamB = sbt("gamB", [128, C], F32)
        betB = sbt("betB", [128, C], F32)
        ones_row = sbt("ones_row", [1, 128], BF16)
        ones_col = sbt("ones_col", [128, 1], BF16)
        ident = sbt("ident", [128, 128], BF16)
        cos_t = sbt("cos_t", [128, NTS * 512], BF16)
        sin_t = sbt("sin_t", [128, NTS * 512], BF16)
        posr = sbt("posr", [1, N], F32)
        jfr = sbt("jfr", [1, C // 2], F32)
        jl = sbt("jl", [1, C // 2], F32)
        invf_r = sbt("invf_r", [1, C // 2], F32)
        lnv = sbt("lnv", [1, 1], F32)
        tc1 = sbt("tc1", [128, 512], F32)
        tc2 = sbt("tc2", [128, 512], F32)

        mu1 = sbt("mu1", [128, NT], F32)
        rstd1 = sbt("rstd1", [128, NT], F32)
        sumc = sbt("sumc", [128, NT], F32)
        sqc = sbt("sqc", [128, NT], F32)
        sum2c = sbt("sum2c", [128, NT], F32)
        sq2c = sbt("sq2c", [128, NT], F32)
        mu2ln = sbt("mu2ln", [128, NT], F32)
        rstd2ln = sbt("rstd2ln", [128, NT], F32)

        xt = [sbt(f"xt{i}", [128, C]) for i in range(2)]
        xn = sbt("xn", [128, C])
        xn_b = sbt("xn_b", [128, C], BF16)
        xnT = sbt("xnT", [128, KCH * 128], BF16)
        kg = sbt("kg", [128, C])
        kg_b = sbt("kg_b", [128, C], BF16)
        kr_b = sbt("kr_b", [128, C], BF16)
        vg_b = sbt("vg_b", [128, C], BF16)
        qg = sbt("qg", [128, C])
        qg_b = sbt("qg_b", [128, C], BF16)
        qr_b = sbt("qr_b", [128, C], BF16)
        qrT = sbt("qrT", [64, H * 128], BF16)
        rp1 = sbt("rp1", [128, 512], BF16)
        rp2 = sbt("rp2", [128, 512], BF16)
        kv_s = sbt("kv_s", [64, H * 64], BF16)
        ksum_row = sbt("ksum_row", [1, C], F32)
        kmb_row = sbt("kmb_row", [1, C], BF16)
        kmeanB = sbt("kmeanB", [128, C], BF16)
        zd = sbt("zd", [128, C], BF16)
        zcol = sbt("zcol", [128, H], F32)
        zinv = sbt("zinv", [128, H], F32)
        res2 = sbt("res2", [128, C], F32)
        scr = sbt("scr", [128, C], F32)
        scr2 = sbt("scr2", [128, C], F32)
        rs_cols = sbt("rs_cols", [128, H], F32)
        colA = sbt("colA", [128, 1], F32)
        colB = sbt("colB", [128, 1], F32)
        colC = sbt("colC", [128, 1], F32)
        colD = sbt("colD", [128, 1], F32)
        x2t = [sbt(f"x2t{i}", [128, C]) for i in range(2)]
        probs_t = sbt("probs_t", [128, 8], F32)
        eps5_col = sbt("eps5_col", [128, 1], F32)
        eps12_col = sbt("eps12_col", [128, 1], F32)
        negmax = sbt("negmax", [128, 1], F32)
        smsum = sbt("smsum", [128, 1], F32)

        ps_t = [pst(f"ps_t{i}", [128, 128], BF16) for i in range(3)]
        ps_a = pst("ps_a", [128, 512])
        ps_b = pst("ps_b", [128, 512])
        ps_e = pst("ps_e", [128, 512])
        ps_f = pst("ps_f", [128, 512])
        ps_g = pst("ps_g", [128, 512])

        S = {nm: Sem(ctx.enter_context(nc.semaphore(nm)))
             for nm in ("LD", "LDB", "W", "V", "A", "T", "G", "ST", "STB", "STP")}
        (LD, LDB, W, V, A, T, G, ST, STB, STP) = (
            S[n] for n in ("LD", "LDB", "W", "V", "A", "T", "G", "ST", "STB", "STP"))
        LDp = [LD, LDB]
        STp = [ST, STB]
        CN = Sem(ctx.enter_context(nc.semaphore("CN")))
        block = ctx.enter_context(nc.Block())
        sch = Sched({"vector": V, "scalar": A, "gpsimd": G})

        for cch in range(KCH):
            for dst, src in ((wq_b, wqT), (wk_b, wkT), (wv_b, wvT)):
                sch.step("gdma", [], lambda g, dst=dst, src=src, cch=cch: g.dma_start(
                    out=dst[:, cch * C:(cch + 1) * C],
                    in_=src[cch * 128:(cch + 1) * 128, :]), inc=(W, 16))
            sch.step("gdma", [], lambda g, cch=cch: g.dma_start(
                out=wg_b[:, cch * 8:(cch + 1) * 8],
                in_=wg[cch * 128:(cch + 1) * 128, :]), inc=(W, 16))
        for i, row in enumerate(bias_rows):
            sch.step("gdma", [], lambda g, i=i, row=row: g.dma_start(
                out=row[:], in_=bqkv[i:i + 1, :]), inc=(W, 16))
        sch.step("gdma", [], lambda g: g.dma_start(out=gam_b[:], in_=gb[0:1, :]), inc=(W, 16))
        sch.step("gdma", [], lambda g: g.dma_start(out=bet_b[:], in_=gb[1:2, :]), inc=(W, 16))
        sch.step("sync", [], lambda s: s.dma_start(out=posr[:], in_=posT), inc=(CN, 16))
        sch.step("sync", [], lambda s: s.dma_start(out=jfr[:], in_=jfrac), inc=(CN, 16))
        w_all = W.n
        w_all_ld = CN.n

        sch.step("vector", [], lambda v: v.memset(ones_row[:], 1.0))
        sch.step("vector", [], lambda v: v.memset(ones_col[:], 1.0))
        sch.step("vector", [], lambda v: v.memset(eps5_col[:], 1e-5))
        sch.step("vector", [], lambda v: v.memset(eps12_col[:], 1e-12))
        consts_v = sch.step("vector", [], lambda v: v.memset(lnv[:], 10000.0))
        id_ms = sch.step("gpsimd", [], lambda g: g.memset(ident[:], 1.0))
        id_done = sch.step("gpsimd", [(G, id_ms)], lambda g: g.affine_select(
            ident[:], ident[:], pattern=[[-1, 128]], compare_op=ALU.is_equal,
            fill=0.0, base=0, channel_multiplier=1))

        for half in range(2):
            for row, dstps in ((gam_b, ps_a), (bet_b, ps_b)):
                tmm = sch.step("tensor", [(W, w_all), (CN, w_all_ld), (V, consts_v)],
                               lambda t, row=row, half=half, dstps=dstps:
                               t.matmul(dstps[:, 0:512], ones_row[:],
                                        row[0:1, half * 512:(half + 1) * 512],
                                        start=True, stop=True), inc=(T, 1))
            sch.step("scalar", [(T, T.n)], lambda s, half=half: s.copy(
                gamB[:, half * 512:(half + 1) * 512], ps_a[:, 0:512]))
            sch.step("scalar", [], lambda s, half=half: s.copy(
                betB[:, half * 512:(half + 1) * 512], ps_b[:, 0:512]))
            if half == 0:
                ev = A.n
                sch.step("tensor", [(A, ev)], lambda t: None)

        gb_bcast_done = A.n

        lnv_done = sch.step("scalar", [(V, 1)], lambda s: s.activation(
            lnv[:], lnv[:], AF.Ln))
        jl_done = sch.step("vector", [(W, w_all), (CN, w_all_ld), (A, lnv_done)], lambda v: v.tensor_scalar(
            jl[:], jfr[:], lnv[0:1, 0:1], None, ALU.mult))
        invf_done = sch.step("scalar", [(V, jl_done)], lambda s: s.activation(
            invf_r[:], jl[:], AF.Exp, bias=0.0, scale=-1.0))
        pse_free = 0
        prev_s_done = 0
        prev_c_done = 0
        tc3 = sbt("tc3", [128, 512], F32)
        for chunk in range(NTS):
            mm = sch.step("tensor", [(A, invf_done), (V, pse_free)],
                          lambda t, chunk=chunk: t.matmul(
                              ps_e[:, 0:512],
                              posr[0:1, chunk * 128:(chunk + 1) * 128],
                              invf_r[0:1, :], start=True, stop=True), inc=(T, 1))
            sch.step("vector", [(T, mm)], lambda v: v.tensor_scalar(
                tc1[:], ps_e[:, 0:512], INV_2PI, None, ALU.mult))
            sch.step("vector", [], lambda v: v.tensor_scalar(
                tc1[:], tc1[:], MAGIC, MAGIC, ALU.add, ALU.subtract))
            sch.step("vector", [(A, prev_s_done)], lambda v: v.scalar_tensor_tensor(
                tc2[:], tc1[:], -TWO_PI_HI, ps_e[:, 0:512], ALU.mult, ALU.add))
            sch.step("vector", [], lambda v: v.scalar_tensor_tensor(
                tc2[:], tc1[:], -TWO_PI_LO, tc2[:], ALU.mult, ALU.add))
            sch.step("vector", [], lambda v: v.tensor_single_scalar(
                tc1[:], tc2[:], PI_F, ALU.is_gt))
            sch.step("vector", [], lambda v: v.scalar_tensor_tensor(
                tc2[:], tc1[:], -TWO_PI_HI, tc2[:], ALU.mult, ALU.add))
            sch.step("vector", [], lambda v: v.tensor_single_scalar(
                tc1[:], tc2[:], -PI_F, ALU.is_lt))
            sin_in = sch.step("vector", [], lambda v: v.scalar_tensor_tensor(
                tc2[:], tc1[:], TWO_PI_HI, tc2[:], ALU.mult, ALU.add))
            s_done = sch.step("scalar", [(V, sin_in)], lambda s, chunk=chunk: s.activation(
                sin_t[:, chunk * 512:(chunk + 1) * 512], tc2[:], AF.Sin))
            prev_s_done = s_done
            pse_free = sch.step("vector", [(A, prev_c_done)], lambda v: v.tensor_scalar(
                tc3[:], ps_e[:, 0:512], PIO2_HI, None, ALU.add))
            sch.step("vector", [], lambda v: v.tensor_scalar(
                tc1[:], tc3[:], INV_2PI, None, ALU.mult))
            sch.step("vector", [], lambda v: v.tensor_scalar(
                tc1[:], tc1[:], MAGIC, MAGIC, ALU.add, ALU.subtract))
            sch.step("vector", [], lambda v: v.scalar_tensor_tensor(
                tc3[:], tc1[:], -TWO_PI_HI, tc3[:], ALU.mult, ALU.add))
            sch.step("vector", [], lambda v: v.scalar_tensor_tensor(
                tc3[:], tc1[:], -TWO_PI_LO, tc3[:], ALU.mult, ALU.add))
            sch.step("vector", [], lambda v: v.tensor_scalar(
                tc3[:], tc3[:], PIO2_LO, None, ALU.add))
            sch.step("vector", [], lambda v: v.tensor_single_scalar(
                tc1[:], tc3[:], PI_F, ALU.is_gt))
            sch.step("vector", [], lambda v: v.scalar_tensor_tensor(
                tc3[:], tc1[:], -TWO_PI_HI, tc3[:], ALU.mult, ALU.add))
            sch.step("vector", [], lambda v: v.tensor_single_scalar(
                tc1[:], tc3[:], -PI_F, ALU.is_lt))
            cos_in = sch.step("vector", [(A, s_done)], lambda v: v.scalar_tensor_tensor(
                tc3[:], tc1[:], TWO_PI_HI, tc3[:], ALU.mult, ALU.add))
            c_done = sch.step("scalar", [(V, cos_in)], lambda s, chunk=chunk: s.activation(
                cos_t[:, chunk * 512:(chunk + 1) * 512], tc3[:], AF.Sin))
            prev_c_done = c_done
        tables_done_a = A.n
        tables_done_v = V.n

        x_rel = []
        for ti in range(NT):
            w = [(V, x_rel[ti - 2])] if ti >= 2 else []
            ld = sch.step("sync", w, lambda s, ti=ti: s.dma_start(
                out=xt[ti % 2][:], in_=xA[ti * 128:(ti + 1) * 128, :]), inc=(LDp[ti % 2], 16))
            sch.step("vector", [(LDp[ti % 2], ld)], lambda v, ti=ti: v.tensor_reduce(
                sumc[:, ti:ti + 1], xt[ti % 2][:], AX.X, ALU.add))
            rel = sch.step("vector", [], lambda v, ti=ti: v.scalar_tensor_tensor(
                scr[:], xt[ti % 2][:], 0.0, xt[ti % 2][:], ALU.add, ALU.mult,
                accum_out=sqc[:, ti:ti + 1]))
            x_rel.append(rel)
        sch.step("vector", [], lambda v: v.tensor_scalar(
            mu1[:], sumc[:], 1.0 / C, None, ALU.mult))
        sch.step("vector", [], lambda v: v.tensor_scalar(
            sqc[:], sqc[:], 1.0 / C, None, ALU.mult))
        sch.step("vector", [], lambda v: v.tensor_tensor(
            scr[:, 0:NT], mu1[:], mu1[:], ALU.mult))
        var_done = sch.step("vector", [], lambda v: v.tensor_tensor(
            sqc[:], sqc[:], scr[:, 0:NT], ALU.subtract))
        std_done = sch.step("scalar", [(V, var_done)], lambda s: s.activation(
            rstd1[:], sqc[:], AF.Sqrt, bias=eps5_col[:, 0:1], scale=1.0))
        stats1_done = sch.step("vector", [(A, std_done)], lambda v: v.reciprocal(
            rstd1[:], rstd1[:]))

        psbf_last = [0]

        def transposes(src_b, dst, nchunk, width, wait_v, base_tag):
            evA = []
            for c2 in range(nchunk):
                slot = c2 % 3
                waits = [(V, wait_v)] if c2 == 0 else []
                if c2 >= 3:
                    waits.append((A, evA[c2 - 3]))
                if c2 == 0:
                    waits.append((G, id_done))
                    waits.append((A, psbf_last[0]))
                sch.step("tensor", waits, lambda t, c2=c2, slot=slot: t.transpose(
                    ps_t[slot][0:width, 0:128],
                    src_b[:, c2 * width:(c2 + 1) * width], ident[:]), inc=(T, 1))
                ev = sch.step("scalar", [(T, T.n)], lambda s, c2=c2, slot=slot: s.copy(
                    dst[0:width, c2 * 128:(c2 + 1) * 128],
                    ps_t[slot][0:width, 0:128]))
                evA.append(ev)
            psbf_last[0] = evA[-1]
            return evA[-1]

        def proj(dst_ps, wmat, bias_row, xnT_done):
            last = None
            for half in range(2):
                ps = dst_ps[half]
                for c2 in range(KCH):
                    sch.step("tensor", [(A, xnT_done)] if c2 == 0 and half == 0 else [],
                             lambda t, ps=ps, c2=c2, half=half: t.matmul(
                                 ps[:, 0:512],
                                 xnT[:, c2 * 128:(c2 + 1) * 128],
                                 wmat[:, c2 * C + half * 512: c2 * C + half * 512 + 512],
                                 start=(c2 == 0), stop=False))
                brow = bias_rows[bias_row]
                last = sch.step("tensor", [], lambda t, ps=ps, half=half, brow=brow: t.matmul(
                    ps[:, 0:512], ones_row[:],
                    brow[0:1, half * 512:(half + 1) * 512],
                    start=False, stop=True), inc=(T, 1))
            return last

        def rope(src_b, dst_b, tile_in_seq, wait_v):
            co = cos_t[:, tile_in_seq * 512:(tile_in_seq + 1) * 512]
            si = sin_t[:, tile_in_seq * 512:(tile_in_seq + 1) * 512]
            x1 = src_b[:, 0:C:2]
            x2 = src_b[:, 1:C:2]
            o1 = dst_b[:, 0:C:2]
            o2 = dst_b[:, 1:C:2]
            sch.step("vector", [(V, wait_v)], lambda v: v.tensor_tensor(rp1[:], x1, co, ALU.mult))
            sch.step("vector", [], lambda v: v.tensor_tensor(rp2[:], x2, si, ALU.mult))
            sch.step("vector", [], lambda v: v.tensor_tensor(o1, rp1[:], rp2[:], ALU.subtract))
            sch.step("vector", [], lambda v: v.tensor_tensor(rp1[:], x2, co, ALU.mult))
            sch.step("vector", [], lambda v: v.tensor_tensor(rp2[:], x1, si, ALU.mult))
            return sch.step("vector", [], lambda v: v.tensor_tensor(
                o2, rp1[:], rp2[:], ALU.add))

        x2_store = {}
        for s_ix in range(BPC if phase >= 3 else (1 if phase >= 1 else 0)):
            ksum_reset = sch.step("vector", [], lambda v: v.memset(ksum_row[:], 0.0))
            for t_ix in range(min(NTS, ntiles)):
                ti = s_ix * NTS + t_ix
                w = [(V, x_rel[-2])] if len(x_rel) >= 2 else []
                ld = sch.step("sync", w, lambda s, ti=ti: s.dma_start(
                    out=xt[ti % 2][:], in_=xA[ti * 128:(ti + 1) * 128, :]), inc=(LDp[ti % 2], 16))
                nrm = sch.step("vector", [(LDp[ti % 2], ld), (V, stats1_done)],
                               lambda v, ti=ti: v.tensor_scalar(
                                   xn_b[:], xt[ti % 2][:], mu1[:, ti:ti + 1],
                                   rstd1[:, ti:ti + 1], ALU.subtract, ALU.mult))
                x_rel.append(nrm)
                if sub < 1:
                    continue
                xnT_done = transposes(xn_b, xnT, KCH, 128, nrm, f"p1t{ti}")
                if sub < 2:
                    continue
                kproj = proj((ps_a, ps_b), wk_b, 1, xnT_done)
                kg_done = sch.step("scalar", [(T, kproj)], lambda s: s.activation(
                    kg[:, 0:512], ps_a[:, 0:512], AF.Gelu))
                kg_done = sch.step("scalar", [], lambda s: s.activation(
                    kg[:, 512:1024], ps_b[:, 0:512], AF.Gelu))
                vproj = proj((ps_a, ps_b), wv_b, 2, kg_done)
                vg_done = sch.step("scalar", [(T, vproj)], lambda s: s.copy(
                    vg_b[:, 0:512], ps_a[:, 0:512]))
                vg_done = sch.step("scalar", [], lambda s: s.copy(
                    vg_b[:, 512:1024], ps_b[:, 0:512]))
                kgb_done = sch.step("vector", [(A, kg_done)], lambda v: v.tensor_scalar(
                    kg_b[:], kg[:], 0.21, None, ALU.add))
                if sub < 3:
                    continue
                kr_done = rope(kg_b, kr_b, t_ix, kgb_done)
                if sub < 4:
                    continue
                def kv_mm(t, t_ix=t_ix):
                    r = None
                    for h in range(H):
                        psd = ps_e if h < 8 else ps_f
                        r = t.matmul(psd[0:64, (h % 8) * 64:(h % 8) * 64 + 64],
                                     kr_b[:, h * D:(h + 1) * D],
                                     vg_b[:, h * D:(h + 1) * D],
                                     start=(t_ix == 0 and h % 8 == 0),
                                     stop=(t_ix == NTS - 1 and h % 8 == 7))
                    return r
                kvw = [(V, kr_done), (A, vg_done)]
                if t_ix == 0 and s_ix > 0:
                    kvw.append((A, kv_evac_prev))
                    kvw.append((V, ksum_reset))
                kv_done = sch.step("tensor", kvw, kv_mm, inc=(T, 1))
                if sub < 5:
                    continue
                ks1 = sch.step("tensor", [(V, kgb_done), (T, kv_done)], lambda t: t.matmul(
                    ps_g[0:1, 0:512], ones_col[:], kg_b[:, 0:512],
                    start=True, stop=True), inc=(T, 1))
                ksa = sch.step("vector", [(T, ks1)], lambda v: v.scalar_tensor_tensor(
                    ksum_row[0:1, 0:512], ps_g[0:1, 0:512], 0.0,
                    ksum_row[0:1, 0:512], ALU.add, ALU.add))
                ks2 = sch.step("tensor", [(V, ksa)], lambda t: t.matmul(
                    ps_g[0:1, 0:512], ones_col[:], kg_b[:, 512:1024],
                    start=True, stop=True), inc=(T, 1))
                ksb = sch.step("vector", [(T, ks2)], lambda v: v.scalar_tensor_tensor(
                    ksum_row[0:1, 512:1024], ps_g[0:1, 0:512], 0.0,
                    ksum_row[0:1, 512:1024], ALU.add, ALU.add))
            if sub < 5 or ntiles < NTS:
                break
            kv_ev = sch.step("scalar", [(T, kv_done)], lambda s: s.activation(
                kv_s[0:64, 0:512], ps_e[0:64, 0:512],
                AF.Copy, bias=0.0, scale=1.0 / N))
            kv_ev = sch.step("scalar", [], lambda s: s.activation(
                kv_s[0:64, 512:1024], ps_f[0:64, 0:512],
                AF.Copy, bias=0.0, scale=1.0 / N))
            km_ev = kv_ev
            kmb_done = sch.step("vector", [(A, km_ev), (V, ksb)], lambda v: v.tensor_scalar(
                kmb_row[:], ksum_row[:], 1.0 / N, None, ALU.mult))
            for half in range(2):
                psd = (ps_a, ps_b)[half]
                sch.step("tensor", [(V, kmb_done)], lambda t, half=half, psd=psd: t.matmul(
                    psd[:, 0:512], ones_row[:],
                    kmb_row[0:1, half * 512:(half + 1) * 512],
                    start=True, stop=True), inc=(T, 1))
            kmB_done = sch.step("scalar", [(T, T.n)], lambda s: (
                s.copy(kmeanB[:, 0:512], ps_a[:, 0:512]),
                s.copy(kmeanB[:, 512:1024], ps_b[:, 0:512]))[-1])
            kv_evac_prev = kmB_done

            for t_ix in range(NTS if phase >= 2 else 0):
                ti = s_ix * NTS + t_ix
                w = [(V, x_rel[-2])] if len(x_rel) >= 2 else []
                ld = sch.step("sync", w, lambda s, ti=ti: s.dma_start(
                    out=xt[ti % 2][:], in_=xA[ti * 128:(ti + 1) * 128, :]), inc=(LDp[ti % 2], 16))
                nrm = sch.step("vector", [(LDp[ti % 2], ld)], lambda v, ti=ti: v.tensor_scalar(
                    xn[:], xt[ti % 2][:], mu1[:, ti:ti + 1], rstd1[:, ti:ti + 1],
                    ALU.subtract, ALU.mult))
                nrm = sch.step("vector", [], lambda v: v.tensor_copy(xn_b[:], xn[:]),
                               inc=(V, 1))
                xnT_done = transposes(xn_b, xnT, KCH, 128, nrm, f"p2t{ti}")
                qproj = proj((ps_a, ps_b), wq_b, 0, xnT_done)
                qg_done = sch.step("scalar", [(T, qproj)], lambda s: s.activation(
                    qg[:, 0:512], ps_a[:, 0:512], AF.Gelu))
                qg_done = sch.step("scalar", [], lambda s: s.activation(
                    qg[:, 512:1024], ps_b[:, 0:512], AF.Gelu))
                qgb_done = sch.step("vector", [(A, qg_done)], lambda v: v.tensor_scalar(
                    qg_b[:], qg[:], 0.21, None, ALU.add))
                sch.step("vector", [(A, kmB_done)], lambda v: v.tensor_tensor(
                    zd[:], qg_b[:], kmeanB[:], ALU.mult))
                sch.step("vector", [], lambda v: v.tensor_reduce(
                    zcol[:], zd[:].rearrange("p (h d) -> p h d", d=D), AX.X, ALU.add))
                sch.step("vector", [], lambda v: v.tensor_scalar(
                    zinv[:], zcol[:], 1e-6, None, ALU.add))
                z_done = sch.step("vector", [], lambda v: v.reciprocal(
                    zinv[:], zinv[:]))
                qr_done = rope(qg_b, qr_b, t_ix, qgb_done)
                evA = []
                for h in range(H):
                    slot = h % 3
                    waits = [(V, qr_done)] if h == 0 else []
                    if h >= 3:
                        waits.append((A, evA[h - 3]))
                    if h == 0:
                        waits.append((A, psbf_last[0]))
                    sch.step("tensor", waits, lambda t, h=h, slot=slot: t.transpose(
                        ps_t[slot][0:64, 0:128],
                        qr_b[:, h * D:(h + 1) * D], ident[:]), inc=(T, 1))
                    ev = sch.step("scalar", [(T, T.n)], lambda s, h=h, slot=slot: s.copy(
                        qrT[0:64, h * 128:(h + 1) * 128],
                        ps_t[slot][0:64, 0:128]))
                    evA.append(ev)
                qrT_done = evA[-1]
                psbf_last[0] = qrT_done
                def out_mm(t):
                    r = None
                    for h in range(H):
                        psd = ps_e if h < 8 else ps_f
                        off = (h % 8) * 64
                        r = t.matmul(psd[:, off:off + 64],
                                     qrT[0:64, h * 128:(h + 1) * 128],
                                     kv_s[0:64, h * 64:(h + 1) * 64],
                                     start=(h % 8 == 0), stop=(h % 8 == 7))
                    return r
                omm = sch.step("tensor", [(A, qrT_done)], out_mm, inc=(T, 1))
                def zres(v):
                    r = None
                    for h in range(H):
                        psd = ps_e if h < 8 else ps_f
                        off = (h % 8) * 64
                        r = v.scalar_tensor_tensor(
                            res2[:, h * D:(h + 1) * D], psd[:, off:off + 64],
                            zinv[:, h:h + 1], xn[:, h * D:(h + 1) * D],
                            ALU.mult, ALU.add, accum_out=rs_cols[:, h:h + 1])
                    return r
                res_done = sch.step("vector", [(T, omm), (V, z_done)], zres)
                sch.step("vector", [], lambda v: v.tensor_reduce(
                    colA[:], rs_cols[:], AX.X, ALU.add))
                sq_done = sch.step("vector", [], lambda v: v.scalar_tensor_tensor(
                    scr[:], res2[:], 0.0, res2[:], ALU.add, ALU.mult,
                    accum_out=colB[:]))
                sch.step("vector", [], lambda v: v.tensor_scalar(
                    colA[:], colA[:], 1.0 / C, None, ALU.mult))
                sch.step("vector", [], lambda v: v.tensor_scalar(
                    colB[:], colB[:], 1.0 / C, None, ALU.mult))
                sch.step("vector", [], lambda v: v.tensor_tensor(
                    colC[:], colA[:], colA[:], ALU.mult))
                var2_done = sch.step("vector", [], lambda v: v.tensor_tensor(
                    colB[:], colB[:], colC[:], ALU.subtract))
                sd2 = sch.step("scalar", [(V, var2_done)], lambda s: s.activation(
                    colD[:], colB[:], AF.Sqrt, bias=eps12_col[:, 0:1], scale=1.0))
                rs2 = sch.step("vector", [(A, sd2)], lambda v: v.reciprocal(
                    colD[:], colD[:]))
                sch.step("vector", [], lambda v: v.tensor_scalar(
                    scr2[:], res2[:], colA[:, 0:1], colD[:, 0:1], ALU.subtract, ALU.mult))
                sch.step("vector", [(A, gb_bcast_done)], lambda v: v.tensor_tensor(
                    scr2[:], scr2[:], gamB[:], ALU.mult))
                sch.step("vector", [], lambda v: v.tensor_tensor(
                    scr2[:], scr2[:], betB[:], ALU.add))
                x2w = [(STp[ti % 2], x2_store[ti - 2])] if (ti - 2) in x2_store else []
                x2_done = sch.step("vector", x2w, lambda v, ti=ti: v.tensor_tensor(
                    x2t[ti % 2][:], scr2[:], xt[ti % 2][:], ALU.add))
                x_rel.append(x2_done)
                sch.step("vector", [], lambda v, ti=ti: v.tensor_reduce(
                    sum2c[:, ti:ti + 1], x2t[ti % 2][:], AX.X, ALU.add))
                st2 = sch.step("vector", [], lambda v, ti=ti: v.scalar_tensor_tensor(
                    scr[:], x2t[ti % 2][:], 0.0, x2t[ti % 2][:], ALU.add, ALU.mult,
                    accum_out=sq2c[:, ti:ti + 1]))
                stv = sch.step("sync", [(V, x2_done)], lambda s, ti=ti: s.dma_start(
                    out=x2_out[ti * 128:(ti + 1) * 128, :], in_=x2t[ti % 2][:]),
                    inc=(STp[ti % 2], 16))
                x2_store[ti] = stv

        sch.step("vector", [], lambda v: v.tensor_scalar(
            mu2ln[:], sum2c[:], 1.0 / C, None, ALU.mult))
        sch.step("vector", [], lambda v: v.tensor_scalar(
            sq2c[:], sq2c[:], 1.0 / C, None, ALU.mult))
        sch.step("vector", [], lambda v: v.tensor_tensor(
            scr[:, 0:NT], mu2ln[:], mu2ln[:], ALU.mult))
        v2_done = sch.step("vector", [], lambda v: v.tensor_tensor(
            sq2c[:], sq2c[:], scr[:, 0:NT], ALU.subtract))
        sd_ln2 = sch.step("scalar", [(V, v2_done)], lambda s: s.activation(
            rstd2ln[:], sq2c[:], AF.Sqrt, bias=eps5_col[:, 0:1], scale=1.0))
        stats2_done = sch.step("vector", [(A, sd_ln2)], lambda v: v.reciprocal(
            rstd2ln[:], rstd2ln[:]))

        for ti in range(NT if phase >= 4 else 0):
            w = [(V, x_rel[-2])] if len(x_rel) >= 2 else []
            w = w + [(ST, ST.n), (STB, STB.n)]
            ld = sch.step("sync", w, lambda s, ti=ti: s.dma_start(
                out=xt[ti % 2][:], in_=x2_out[ti * 128:(ti + 1) * 128, :]), inc=(LDp[ti % 2], 16))
            nrm = sch.step("vector", [(LDp[ti % 2], ld), (V, stats2_done)],
                           lambda v, ti=ti: v.tensor_scalar(
                               xn_b[:], xt[ti % 2][:], mu2ln[:, ti:ti + 1],
                               rstd2ln[:, ti:ti + 1], ALU.subtract, ALU.mult))
            x_rel.append(nrm)
            xnT_done = transposes(xn_b, xnT, KCH, 128, nrm, f"rt{ti}")
            def lg_mm(t):
                r = None
                for c2 in range(KCH):
                    r = t.matmul(ps_a[:, 0:8], xnT[:, c2 * 128:(c2 + 1) * 128],
                                 wg_b[:, c2 * 8:(c2 + 1) * 8],
                                 start=(c2 == 0), stop=(c2 == KCH - 1))
                return r
            lg = sch.step("tensor", [(A, xnT_done)], lg_mm, inc=(T, 1))
            sch.step("vector", [(T, lg)], lambda v: v.tensor_reduce(
                negmax[:], ps_a[:, 0:8], AX.X, ALU.max))
            ngm = sch.step("vector", [], lambda v: v.tensor_scalar(
                negmax[:], negmax[:], -1.0, None, ALU.mult))
            exw = [(V, ngm)]
            if ti > 0:
                exw.append((STP, pr_store))
            ex = sch.step("scalar", exw, lambda s: s.activation(
                probs_t[:], ps_a[:, 0:8], AF.Exp, bias=negmax[:, 0:1], scale=1.0,
                accum_out=smsum[:, 0:1]))
            sch.step("vector", [(A, ex)], lambda v: v.reciprocal(smsum[:], smsum[:]))
            pr = sch.step("vector", [], lambda v: v.tensor_scalar(
                probs_t[:], probs_t[:], smsum[:, 0:1], None, ALU.mult))
            pr_store = sch.step("sync", [(V, pr)], lambda s, ti=ti: s.dma_start(
                out=probs_out[ti * 128:(ti + 1) * 128, :], in_=probs_t[:]), inc=(STP, 16))

        sch.step("gpsimd", [(ST, ST.n), (STB, STB.n), (STP, STP.n)],
                 lambda g: None)

        sch.emit(block)
    return nc



def build_stage_b1():
    nc = bass.Bass("TRN2", target_bir_lowering=False)
    x2 = nc.dram_tensor("x2", [TPC, C], F32, kind="ExternalInput").ap()
    probs_all = nc.dram_tensor("probs_all", [B * N, 8], F32, kind="ExternalInput").ap()
    probs_loc = nc.dram_tensor("probs_loc", [TPC, 8], F32, kind="ExternalInput").ap()
    w1_in = nc.dram_tensor("w1_in", [8, C, 2048], F32, kind="ExternalInput").ap()
    w2_in = nc.dram_tensor("w2_in", [8, 2048, C], F32, kind="ExternalInput").ap()
    x3_out = nc.dram_tensor("x3_out", [TPC, C], F32, kind="ExternalOutput").ap()

    NHALF = 2
    TPH = NT // NHALF
    NFC = 16
    CAP = float(B * N // 8)

    with ExitStack() as ctx:
        sbt = lambda n, s, d=F32: ctx.enter_context(nc.sbuf_tensor(n, s, d))
        pst = lambda n, s, d=F32: ctx.enter_context(nc.psum_tensor(n, s, d))

        probs_il = sbt("probs_il", [128, 2048])
        ones_rowf = sbt("ones_rowf", [1, 128], F32)
        ones_colf = sbt("ones_colf", [128, 1], F32)
        ident = sbt("ident", [128, 128], BF16)
        lo_r = sbt("lo_r", [1, 8])
        hi_r = sbt("hi_r", [1, 8])
        mid_r = sbt("mid_r", [1, 8])
        cnt_r = sbt("cnt_r", [1, 8])
        cge_r = sbt("cge_r", [1, 8])
        dd_r = sbt("dd_r", [1, 8])
        ge_t = sbt("ge_t", [128, 2048])
        colr8 = sbt("colr8", [128, 8])
        thrB = sbt("thrB", [128, 8])
        gates_all = sbt("gates_all", [128, NT * 8])
        p_t = sbt("p_t", [128, 8])
        g_m1 = sbt("g_m1", [128, 8])
        g_msk = sbt("g_msk", [128, 8])
        g_t1 = sbt("g_t1", [128, 8])
        g_u = sbt("g_u", [128, 8])
        g_eq = sbt("g_eq", [128, 8])
        g_ind = sbt("g_ind", [128, 8])
        colA = sbt("colA", [128, 1])
        colP = sbt("colP", [128, 1])
        colW = sbt("colW", [128, 1])
        cola = sbt("cola", [128, 1])
        colna = sbt("colna", [128, 1])
        colpref = sbt("colpref", [128, 1])
        colt = sbt("colt", [128, 1])
        mu2 = sbt("mu2", [128, NT])
        rstd2 = sbt("rstd2", [128, NT])
        sumc = sbt("sumc", [128, NT])
        sqc = sbt("sqc", [128, NT])
        scr_nt = sbt("scr_nt", [128, NT])
        eps5_col = sbt("eps5_col", [128, 1])
        xt = [sbt(f"xt{i}", [128, C]) for i in range(2)]
        x3t = [sbt(f"x3t{i}", [128, C]) for i in range(2)]
        xn_b = sbt("xn_b", [128, C], BF16)
        scr = sbt("scr", [128, C])
        xnT_half = sbt("xnT_half", [128, KCH * (TPH * 128)], BF16)
        out_acc = sbt("out_acc", [128, TPH * C])
        w1_b = sbt("w1_b", [128, KCH * 2048], BF16)
        w2_b = sbt("w2_b", [128, NFC * C], BF16)
        a_bt = [sbt(f"a_bt{i}", [128, 128], BF16) for i in range(3)]

        ps_t = [pst(f"ps_t{i}", [128, 128], BF16) for i in range(3)]
        ps_q = pst("ps_q", [128, 512])
        ps_s = pst("ps_s", [128, 512])
        ps_h1 = pst("ps_h1", [128, 512])
        ps_h2 = pst("ps_h2", [128, 512])
        ps_g = pst("ps_g", [128, 512])

        S = {nm: Sem(ctx.enter_context(nc.semaphore(nm)))
             for nm in ("LD", "LDB", "W", "WA", "WB", "WP", "V", "A", "T", "G", "ST", "STB")}
        (LD, LDB, W, WA, WB, WP, V, A, T, G, ST, STB) = (
            S[n] for n in ("LD", "LDB", "W", "WA", "WB", "WP", "V", "A", "T", "G", "ST", "STB"))
        LDp = [LD, LDB]
        STp = [ST, STB]
        block = ctx.enter_context(nc.Block())
        sch = Sched({"vector": V, "scalar": A, "gpsimd": G})

        sch.step("sync", [], lambda s: s.dma_start(
            out=probs_il[:], in_=probs_all.rearrange("(p r) e -> p r e", p=128)),
            inc=(W, 16))
        w_all = W.n
        consts_v = sch.step("vector", [], lambda v: (
            v.memset(ones_rowf[:], 1.0), v.memset(ones_colf[:], 1.0),
            v.memset(eps5_col[:], 1e-5), v.memset(colpref[:], 0.0),
            v.memset(lo_r[:], 0.0), v.memset(hi_r[:], 2.0))[-1])
        id_ms = sch.step("gpsimd", [], lambda g: g.memset(ident[:], 1.0))
        id_done = sch.step("gpsimd", [(G, id_ms)], lambda g: g.affine_select(
            ident[:], ident[:], pattern=[[-1, 128]], compare_op=ALU.is_equal,
            fill=0.0, base=0, channel_multiplier=1))

        x_rel = []
        for ti in range(NT):
            w = [(V, x_rel[-2])] if len(x_rel) >= 2 else []
            ld = sch.step("sync", w, lambda s, ti=ti: s.dma_start(
                out=xt[ti % 2][:], in_=x2[ti * 128:(ti + 1) * 128, :]),
                inc=(LDp[ti % 2], 16))
            sch.step("vector", [(LDp[ti % 2], ld)], lambda v, ti=ti: v.tensor_reduce(
                sumc[:, ti:ti + 1], xt[ti % 2][:], AX.X, ALU.add))
            rel = sch.step("vector", [], lambda v, ti=ti: v.scalar_tensor_tensor(
                scr[:], xt[ti % 2][:], 0.0, xt[ti % 2][:], ALU.add, ALU.mult,
                accum_out=sqc[:, ti:ti + 1]))
            x_rel.append(rel)
        sch.step("vector", [], lambda v: v.tensor_scalar(
            mu2[:], sumc[:], 1.0 / C, None, ALU.mult))
        sch.step("vector", [], lambda v: v.tensor_scalar(
            sqc[:], sqc[:], 1.0 / C, None, ALU.mult))
        sch.step("vector", [], lambda v: v.tensor_tensor(
            scr_nt[:], mu2[:], mu2[:], ALU.mult))
        var_done = sch.step("vector", [], lambda v: v.tensor_tensor(
            sqc[:], sqc[:], scr_nt[:], ALU.subtract))
        std_done = sch.step("scalar", [(V, var_done)], lambda s: s.activation(
            rstd2[:], sqc[:], AF.Sqrt, bias=eps5_col[:, 0:1], scale=1.0))
        stats_done = sch.step("vector", [(A, std_done)], lambda v: v.reciprocal(
            rstd2[:], rstd2[:]))

        prev_ge_v = 0
        for it in range(40):
            w0 = [(W, w_all)] if it == 0 else []
            sch.step("vector", w0, lambda v: v.tensor_tensor(
                mid_r[:], lo_r[:], hi_r[:], ALU.add))
            mid_done = sch.step("vector", [], lambda v: v.tensor_scalar(
                mid_r[:], mid_r[:], 0.5, None, ALU.mult))
            mid_rep = bass.AP(mid_r[:].tensor, 0, [[8, 1], [0, 64], [1, 8]])
            for blk in range(4):
                mm = sch.step("tensor", [(V, mid_done)], lambda t: t.matmul(
                    ps_g[:, 0:512], ones_rowf[:], mid_rep,
                    start=True, stop=True), inc=(T, 1))
                geb = sch.step("vector", [(T, mm)], lambda v, blk=blk: v.tensor_tensor(
                    ge_t[:, blk * 512:(blk + 1) * 512],
                    probs_il[:, blk * 512:(blk + 1) * 512],
                    ps_g[:, 0:512], ALU.is_ge))
                if blk < 3:
                    sch.step("tensor", [(V, geb)], lambda t: None)
            red = sch.step("vector", [], lambda v: v.tensor_reduce(
                colr8[:], ge_t[:].rearrange("p (r e) -> p e r", e=8), AX.X, ALU.add))
            cm = sch.step("tensor", [(V, red)], lambda t: t.matmul(
                ps_g[0:1, 0:8], ones_colf[:], colr8[:], start=True, stop=True),
                inc=(T, 1))
            sch.step("vector", [(T, cm)], lambda v: v.tensor_copy(
                cnt_r[:], ps_g[0:1, 0:8]))
            sch.step("vector", [], lambda v: v.tensor_single_scalar(
                cge_r[:], cnt_r[:], CAP, ALU.is_ge))
            sch.step("vector", [], lambda v: v.tensor_tensor(
                dd_r[:], mid_r[:], lo_r[:], ALU.subtract))
            sch.step("vector", [], lambda v: v.tensor_tensor(
                dd_r[:], dd_r[:], cge_r[:], ALU.mult))
            sch.step("vector", [], lambda v: v.tensor_tensor(
                lo_r[:], lo_r[:], dd_r[:], ALU.add))
            sch.step("vector", [], lambda v: v.tensor_tensor(
                dd_r[:], hi_r[:], mid_r[:], ALU.subtract))
            sch.step("vector", [], lambda v: v.tensor_tensor(
                dd_r[:], dd_r[:], cge_r[:], ALU.mult))
            prev_ge_v = sch.step("vector", [], lambda v: v.tensor_tensor(
                hi_r[:], mid_r[:], dd_r[:], ALU.add))
        tb = sch.step("tensor", [(V, prev_ge_v)], lambda t: t.matmul(
            ps_g[:, 0:8], ones_rowf[:], lo_r[:], start=True, stop=True), inc=(T, 1))
        thr_done = sch.step("vector", [(T, tb)], lambda v: v.tensor_copy(
            thrB[:], ps_g[:, 0:8]))

        gv = 0
        for ti in range(NT):
            ldg = sch.step("sync", [(V, gv)] if gv else [], lambda s, ti=ti: s.dma_start(
                out=p_t[:], in_=probs_loc[ti * 128:(ti + 1) * 128, :]), inc=(WP, 16))
            gw = [(WP, WP.n), (V, thr_done)] if ti == 0 else [(WP, WP.n)]
            sch.step("vector", gw, lambda v: v.tensor_tensor(
                g_eq[:], p_t[:], thrB[:], ALU.is_ge))
            sch.step("vector", [], lambda v: v.scalar_tensor_tensor(
                g_m1[:], g_eq[:], 30.0, p_t[:], ALU.mult, ALU.add))
            sch.step("vector", [], lambda v: v.tensor_scalar(
                g_msk[:], g_m1[:], 30.0, None, ALU.subtract))
            sch.step("vector", [], lambda v: v.tensor_reduce(
                colA[:], g_msk[:], AX.X, ALU.max))
            sch.step("vector", [], lambda v: v.tensor_reduce(
                colP[:], p_t[:], AX.X, ALU.max))
            sch.step("vector", [], lambda v: v.tensor_single_scalar(
                cola[:], colA[:], -1.0, ALU.is_ge))
            sch.step("vector", [], lambda v: v.tensor_tensor(
                colt[:], colA[:], colP[:], ALU.subtract))
            sch.step("vector", [], lambda v: v.scalar_tensor_tensor(
                colW[:], colt[:], cola[:, 0:1], colP[:], ALU.mult, ALU.add))
            sch.step("vector", [], lambda v: v.tensor_scalar(
                colna[:], cola[:], -1.0, 1.0, ALU.mult, ALU.add))
            sch.step("vector", [], lambda v: v.tensor_scalar(
                g_t1[:], g_eq[:], -30.0, 30.0, ALU.mult, ALU.add))
            sch.step("vector", [], lambda v: v.scalar_tensor_tensor(
                g_u[:], g_t1[:], colna[:, 0:1], g_msk[:], ALU.mult, ALU.add))
            sch.step("vector", [], lambda v: v.tensor_reduce(
                colt[:], g_u[:], AX.X, ALU.max))
            sch.step("vector", [], lambda v: v.tensor_scalar(
                g_eq[:], g_u[:], colt[:, 0:1], None, ALU.is_equal))
            sch.step("vector", [], lambda v: v.memset(colpref[:], 0.0))
            for e in range(8):
                sch.step("vector", [], lambda v: v.tensor_scalar(
                    colt[:], colpref[:], -1.0, 1.0, ALU.mult, ALU.add))
                sch.step("vector", [], lambda v, e=e: v.tensor_tensor(
                    g_ind[:, e:e + 1], g_eq[:, e:e + 1], colt[:], ALU.mult))
                sch.step("vector", [], lambda v, e=e: v.tensor_tensor(
                    colpref[:], colpref[:], g_eq[:, e:e + 1], ALU.max))
            gv = sch.step("vector", [], lambda v, ti=ti: v.tensor_scalar(
                gates_all[:, ti * 8:(ti + 1) * 8], g_ind[:], colW[:, 0:1],
                None, ALU.mult))
        gates_done = gv

        psbf_last = [0]
        xnb_T = [0]
        wa_last = [0]
        wb_last = [0]
        x3_store = {}
        for hf in range(NHALF):
            xnT_done = 0
            for tt in range(TPH):
                ti = hf * TPH + tt
                w = [(V, x_rel[-2])] if len(x_rel) >= 2 else []
                ld = sch.step("sync", w, lambda s, ti=ti: s.dma_start(
                    out=xt[ti % 2][:], in_=x2[ti * 128:(ti + 1) * 128, :]),
                    inc=(LDp[ti % 2], 16))
                nw = [(LDp[ti % 2], ld)]
                if xnb_T[0]:
                    nw.append((T, xnb_T[0]))
                nrm = sch.step("vector", nw,
                               lambda v, ti=ti: v.tensor_scalar(
                                   xn_b[:], xt[ti % 2][:], mu2[:, ti:ti + 1],
                                   rstd2[:, ti:ti + 1], ALU.subtract, ALU.mult))
                x_rel.append(nrm)
                evA = []
                for c2 in range(KCH):
                    slot = c2 % 3
                    waits = [(V, nrm)] if c2 == 0 else []
                    if c2 >= 3:
                        waits.append((A, evA[c2 - 3]))
                    if c2 == 0:
                        waits.append((G, id_done))
                        waits.append((A, psbf_last[0]))
                    sch.step("tensor", waits, lambda t, c2=c2, slot=slot, tt=tt:
                             t.transpose(ps_t[slot][0:128, 0:128],
                                         xn_b[:, c2 * 128:(c2 + 1) * 128],
                                         ident[:]), inc=(T, 1))
                    ev = sch.step("scalar", [(T, T.n)],
                                  lambda s, c2=c2, slot=slot, tt=tt: s.copy(
                                      xnT_half[0:128,
                                               c2 * (TPH * 128) + tt * 128:
                                               c2 * (TPH * 128) + tt * 128 + 128],
                                      ps_t[slot][0:128, 0:128]))
                    evA.append(ev)
                psbf_last[0] = evA[-1]
                xnT_done = evA[-1]
                xnb_T[0] = T.n

            if hf == 0:
                prev_hev = 0
            for e in range(8):
                for c2 in range(KCH):
                    wv = [(T, wa_last[0])] if c2 == 0 else []
                    sch.step("gdma", wv, lambda g, e=e, c2=c2: g.dma_start(
                        out=w1_b[:, c2 * 2048:(c2 + 1) * 2048],
                        in_=w1_in[e, c2 * 128:(c2 + 1) * 128, :]), inc=(WA, 16))
                for fc in range(NFC):
                    wv = [(T, wb_last[0])] if fc == 0 else []
                    sch.step("gdma", wv, lambda g, e=e, fc=fc: g.dma_start(
                        out=w2_b[:, fc * C:(fc + 1) * C],
                        in_=w2_in[e, fc * 128:(fc + 1) * 128, :]), inc=(WB, 16))
                wa_tot, wb_tot = WA.n, WB.n

                for tt in range(TPH):
                    ti = hf * TPH + tt
                    aev = {}
                    for fc in range(NFC):
                        psA = ps_q if fc % 2 == 0 else ps_s
                        bslot = fc % 3
                        def a_mm(t, fc=fc, tt=tt, psA=psA, e=e):
                            r = None
                            for c2 in range(KCH):
                                r = t.matmul(
                                    psA[:, 0:128],
                                    w1_b[:, c2 * 2048 + fc * 128:
                                         c2 * 2048 + fc * 128 + 128],
                                    xnT_half[:, c2 * (TPH * 128) + tt * 128:
                                             c2 * (TPH * 128) + tt * 128 + 128],
                                    start=(c2 == 0), stop=(c2 == KCH - 1))
                            return r
                        waits = []
                        if fc == 0 and tt == 0:
                            waits = [(WA, wa_tot), (A, xnT_done)]
                        if fc < 2 and not (fc == 0 and tt == 0 and e == 0):
                            waits.append((A, A.n))
                        if fc >= 2:
                            waits.append((A, aev[fc - 2]))
                        if fc >= 3:
                            waits.append((T, hmm_done[fc - 3]))
                        amm = sch.step("tensor", waits, a_mm, inc=(T, 1))
                        aev[fc] = sch.step("scalar", [(T, amm)],
                                           lambda s, fc=fc, bslot=bslot, psA=psA:
                                           s.activation(a_bt[bslot][:],
                                                        psA[:, 0:128], AF.Gelu))
                        def h_mm(t, fc=fc, bslot=bslot):
                            t.matmul(ps_h1[:, 0:512], a_bt[bslot][:],
                                     w2_b[:, fc * C:fc * C + 512],
                                     start=(fc == 0), stop=(fc == NFC - 1))
                            return t.matmul(ps_h2[:, 0:512], a_bt[bslot][:],
                                            w2_b[:, fc * C + 512:fc * C + 1024],
                                            start=(fc == 0), stop=(fc == NFC - 1))
                        hw = [(A, aev[fc])]
                        if fc == 0:
                            hw.append((WB, wb_tot))
                            if prev_hev:
                                hw.append((V, prev_hev))
                        hmm = sch.step("tensor", hw, h_mm, inc=(T, 1))
                        if fc == 0:
                            hmm_done = {}
                        hmm_done[fc] = hmm
                    gcol = gates_all[:, ti * 8 + e:ti * 8 + e + 1]
                    hv = [(T, hmm_done[NFC - 1]), (V, gates_done)]
                    if e == 0:
                        ev1 = sch.step("vector", hv, lambda v, tt=tt, gcol=gcol:
                                       v.tensor_scalar(
                                           out_acc[:, tt * C:tt * C + 512],
                                           ps_h1[:, 0:512], gcol, None, ALU.mult))
                        ev2 = sch.step("vector", [], lambda v, tt=tt, gcol=gcol:
                                       v.tensor_scalar(
                                           out_acc[:, tt * C + 512:tt * C + 1024],
                                           ps_h2[:, 0:512], gcol, None, ALU.mult))
                    else:
                        ev1 = sch.step("vector", hv, lambda v, tt=tt, gcol=gcol:
                                       v.scalar_tensor_tensor(
                                           out_acc[:, tt * C:tt * C + 512],
                                           ps_h1[:, 0:512], gcol,
                                           out_acc[:, tt * C:tt * C + 512],
                                           ALU.mult, ALU.add))
                        ev2 = sch.step("vector", [], lambda v, tt=tt, gcol=gcol:
                                       v.scalar_tensor_tensor(
                                           out_acc[:, tt * C + 512:tt * C + 1024],
                                           ps_h2[:, 0:512], gcol,
                                           out_acc[:, tt * C + 512:tt * C + 1024],
                                           ALU.mult, ALU.add))
                    prev_hev = ev2
                wa_last[0] = T.n
                wb_last[0] = T.n

            for tt in range(TPH):
                ti = hf * TPH + tt
                w = [(V, x_rel[-2])] if len(x_rel) >= 2 else []
                ld = sch.step("sync", w, lambda s, ti=ti: s.dma_start(
                    out=xt[ti % 2][:], in_=x2[ti * 128:(ti + 1) * 128, :]),
                    inc=(LDp[ti % 2], 16))
                x3w = [(LDp[ti % 2], ld), (V, prev_hev)]
                if (ti - 2) in x3_store:
                    x3w.append((STp[ti % 2], x3_store[ti - 2]))
                x3v = sch.step("vector", x3w, lambda v, ti=ti, tt=tt: v.tensor_tensor(
                    x3t[ti % 2][:], xt[ti % 2][:], out_acc[:, tt * C:(tt + 1) * C],
                    ALU.add))
                x_rel.append(x3v)
                stv = sch.step("sync", [(V, x3v)], lambda s, ti=ti: s.dma_start(
                    out=x3_out[ti * 128:(ti + 1) * 128, :], in_=x3t[ti % 2][:]),
                    inc=(STp[ti % 2], 16))
                x3_store[ti] = stv

        sch.step("gpsimd", [(ST, ST.n), (STB, STB.n)], lambda g: None)
        sch.emit(block)
    return nc


_STAGE_B1 = None


def run_stage_b1(x2_full, probs_full, inputs):
    global _STAGE_B1
    if _STAGE_B1 is None:
        _STAGE_B1 = build_stage_b1()
    w1 = np.ascontiguousarray(np.asarray(inputs["L0_ffn_W1"], np.float32))
    w2 = np.ascontiguousarray(np.asarray(inputs["L0_ffn_W2"], np.float32))
    in_maps = []
    for c in range(NCORES):
        x2c = np.ascontiguousarray(x2_full[c * BPC:(c + 1) * BPC].reshape(TPC, C))
        in_maps.append(dict(x2=x2c, probs_all=probs_full,
                            probs_loc=np.ascontiguousarray(
                                probs_full[c * TPC:(c + 1) * TPC]),
                            w1_in=w1, w2_in=w2))
    res = run_bass_kernel_spmd(_STAGE_B1, in_maps, core_ids=list(range(NCORES)))
    x3 = np.stack([res.results[c]["x3_out"].reshape(BPC, N, C)
                   for c in range(NCORES)]).reshape(B, N, C)
    return x3



def build_stage_b2():
    nc = bass.Bass("TRN2", target_bir_lowering=False)
    x3 = nc.dram_tensor("x3", [TPC, C], F32, kind="ExternalInput").ap()
    k123T = nc.dram_tensor("k123T", [C, 48], F32, kind="ExternalInput").ap()
    abR = nc.dram_tensor("abR", [1, 2 * H * 64], F32, kind="ExternalInput").ap()
    bT = nc.dram_tensor("bT", [64, 2 * H], F32, kind="ExternalInput").ap()
    ccT = nc.dram_tensor("ccT", [64, H], F32, kind="ExternalInput").ap()
    vwF = nc.dram_tensor("vwF", [C, H * 64], F32, kind="ExternalInput").ap()
    owS = nc.dram_tensor("owS", [H * 64, C], F32, kind="ExternalInput").ap()
    posT = nc.dram_tensor("posT", [1, N], F32, kind="ExternalInput").ap()
    utri = nc.dram_tensor("utri", [128, 128], F32, kind="ExternalInput").ap()
    x4_out = nc.dram_tensor("x4_out", [TPC, C], F32, kind="ExternalOutput").ap()
    m0_out = nc.dram_tensor("m0_out", [1, 1], F32, kind="ExternalOutput").ap()
    mu4_out = nc.dram_tensor("mu4_out", [128, NT], F32, kind="ExternalOutput").ap()
    rstd4_out = nc.dram_tensor("rstd4_out", [128, NT], F32, kind="ExternalOutput").ap()

    with ExitStack() as ctx:
        sbt = lambda n, s, d=F32: ctx.enter_context(nc.sbuf_tensor(n, s, d))
        pst = lambda n, s, d=F32: ctx.enter_context(nc.psum_tensor(n, s, d))

        ones_row = sbt("ones_row", [1, 128], BF16)
        ones_col = sbt("ones_col", [128, 1], BF16)
        ident = sbt("ident", [128, 128], BF16)
        identf = sbt("identf", [128, 128], F32)
        uptri = sbt("uptri", [128, 128], BF16)
        utri_f = sbt("utri_f", [128, 128], F32)
        k123_b = sbt("k123_b", [128, KCH * 48], BF16)
        vw_b = sbt("vw_b", [128, KCH * (H * 64)], BF16)
        ow_b = sbt("ow_b", [128, 8 * C], BF16)
        abR_s = sbt("abR_s", [1, 2 * H * 64], F32)
        bT_s = sbt("bT_s", [64, 2 * H], F32)
        ccT_s = sbt("ccT_s", [64, H], F32)
        posr = sbt("posr", [1, N], F32)
        nhat = sbt("nhat", [1, N], F32)
        sinm = sbt("sinm", [64, 512], F32)
        p1T = sbt("p1T", [128, NTS * H], F32)
        p2eT = sbt("p2eT", [128, NTS * H], F32)
        mu3 = sbt("mu3", [128, NT], F32)
        rstd3 = sbt("rstd3", [128, NT], F32)
        mu4 = sbt("mu4", [128, NT], F32)
        rstd4 = sbt("rstd4", [128, NT], F32)
        sumc = sbt("sumc", [128, NT], F32)
        sqc = sbt("sqc", [128, NT], F32)
        scr_nt = sbt("scr_nt", [128, NT], F32)
        eps5_col = sbt("eps5_col", [128, 1], F32)
        xt = [sbt(f"xt{i}", [128, C]) for i in range(2)]
        x4t = [sbt(f"x4t{i}", [128, C]) for i in range(2)]
        xn_b = sbt("xn_b", [128, C], BF16)
        xnT = sbt("xnT", [128, C], BF16)
        scr = sbt("scr", [128, C])
        v_b = sbt("v_b", [128, C], BF16)
        cde_s = sbt("cde_s", [128, 64])
        cdeE = sbt("cdeE", [128, 48])
        crE_b = sbt("crE_b", [128, H], BF16)
        c1t = sbt("c1t", [128, H])
        denp = sbt("denp", [128, H])
        dent = sbt("dent", [128, H])
        s_b = sbt("s_b", [128, C], BF16)
        vdiag = sbt("vdiag", [128, C])
        numt = sbt("numt", [128, C])
        o_b = sbt("o_b", [128, C], BF16)
        oTp = [sbt(f"oTp{j}", [128, 128], BF16) for j in range(8)]
        carry_sb = sbt("carry_sb", [1, C], F32)
        carry_b = sbt("carry_b", [1, C], BF16)
        cden_sb = sbt("cden_sb", [1, H], F32)
        cden_b = sbt("cden_b", [1, H], BF16)
        m0c = sbt("m0c", [128, 1], F32)
        m0row = sbt("m0row", [1, 128], F32)
        m0s = sbt("m0s", [1, 1], F32)
        colA = sbt("colA", [128, 1])

        ps_t = [pst(f"ps_t{i}", [128, 128], BF16) for i in range(3)]
        ps_v1 = pst("ps_v1", [128, 512])
        ps_v2 = pst("ps_v2", [128, 512])
        ps_p1 = pst("ps_p1", [128, 512])
        ps_p2 = pst("ps_p2", [128, 512])
        ps_c = pst("ps_c", [128, 512])

        S = {nm: Sem(ctx.enter_context(nc.semaphore(nm)))
             for nm in ("LD", "LDB", "W", "CN", "V", "A", "T", "G", "ST", "STB")}
        (LD, LDB, W, CN, V, A, T, G, ST, STB) = (
            S[n] for n in ("LD", "LDB", "W", "CN", "V", "A", "T", "G", "ST", "STB"))
        LDp = [LD, LDB]
        STp = [ST, STB]
        block = ctx.enter_context(nc.Block())
        sch = Sched({"vector": V, "scalar": A, "gpsimd": G})

        for c2 in range(KCH):
            sch.step("gdma", [], lambda g, c2=c2: g.dma_start(
                out=k123_b[:, c2 * 48:(c2 + 1) * 48],
                in_=k123T[c2 * 128:(c2 + 1) * 128, :]), inc=(W, 16))
            sch.step("gdma", [], lambda g, c2=c2: g.dma_start(
                out=vw_b[:, c2 * (H * 64):(c2 + 1) * (H * 64)],
                in_=vwF[c2 * 128:(c2 + 1) * 128, :]), inc=(W, 16))
            sch.step("gdma", [], lambda g, c2=c2: g.dma_start(
                out=ow_b[:, c2 * C:(c2 + 1) * C],
                in_=owS[c2 * 128:(c2 + 1) * 128, :]), inc=(W, 16))
        sch.step("sync", [], lambda s: s.dma_start(out=abR_s[:], in_=abR), inc=(CN, 16))
        sch.step("sync", [], lambda s: s.dma_start(out=bT_s[:], in_=bT), inc=(CN, 16))
        sch.step("sync", [], lambda s: s.dma_start(out=ccT_s[:], in_=ccT), inc=(CN, 16))
        sch.step("sync", [], lambda s: s.dma_start(out=posr[:], in_=posT), inc=(CN, 16))
        w_all, cn_all = W.n, CN.n

        consts_v = sch.step("vector", [], lambda v: (
            v.memset(ones_row[:], 1.0), v.memset(ones_col[:], 1.0),
            v.memset(eps5_col[:], 1e-5), v.memset(m0c[:], 0.0),
            v.memset(identf[:], 1.0))[-1])
        id_ms = sch.step("gpsimd", [], lambda g: g.memset(ident[:], 1.0))
        id_done = sch.step("gpsimd", [(G, id_ms)], lambda g: g.affine_select(
            ident[:], ident[:], pattern=[[-1, 128]], compare_op=ALU.is_equal,
            fill=0.0, base=0, channel_multiplier=1))
        ut_ld = sch.step("sync", [], lambda s: s.dma_start(
            out=utri_f[:], in_=utri), inc=(CN, 16))
        ut_done = sch.step("vector", [(CN, CN.n)], lambda v: v.tensor_copy(
            uptri[:], utri_f[:]))
        idf_done = sch.step("gpsimd", [(G, id_done), (V, consts_v)], lambda g: g.affine_select(
            identf[:], identf[:], pattern=[[-1, 128]], compare_op=ALU.is_equal,
            fill=0.0, base=0, channel_multiplier=1))

        nh_done = sch.step("vector", [(CN, cn_all), (V, consts_v)],
                           lambda v: v.tensor_scalar(
                               nhat[:], posr[:], 1.0 / float(N - 1), None, ALU.mult))

        prev_sin_a = 0
        for pp in range(2):
            dstT = p1T if pp == 0 else p2eT
            for h in range(H):
                for chunk in range(4):
                    mm = sch.step("tensor",
                                  [(V, nh_done), (A, prev_sin_a)],
                                  lambda t, pp=pp, h=h, chunk=chunk: t.matmul(
                                      ps_c[0:64, 0:512],
                                      abR_s[0:1, pp * 1024 + h * 64:
                                            pp * 1024 + (h + 1) * 64],
                                      nhat[0:1, chunk * 512:(chunk + 1) * 512],
                                      start=True, stop=True), inc=(T, 1))
                    prev_sin_a = sch.step("scalar", [(T, mm)],
                                          lambda s, pp=pp, h=h: s.activation(
                                              sinm[:], ps_c[0:64, 0:512], AF.Sin,
                                              bias=bT_s[0:64, pp * H + h:pp * H + h + 1],
                                              scale=1.0))
                    for sub in range(4):
                        mm2 = sch.step("tensor", [(A, prev_sin_a)],
                                       lambda t, sub=sub, h=h: t.matmul(
                                           ps_v1[0:128, sub:sub + 1],
                                           sinm[0:64, sub * 128:(sub + 1) * 128],
                                           ccT_s[0:64, h:h + 1],
                                           start=True, stop=True), inc=(T, 1))
                    out3 = (dstT[:].rearrange("p (t hh) -> p t hh", hh=H)
                            [:, chunk * 4:chunk * 4 + 4, h:h + 1])
                    in3 = ps_v1[0:128, 0:4].rearrange("p (a b) -> p a b", b=1)
                    ev = sch.step("scalar", [(T, T.n)],
                                  lambda s, out3=out3, in3=in3: s.copy(out3, in3))
                    prev_sin_a = ev
        sch.step("vector", [(A, prev_sin_a)], lambda v: v.tensor_scalar(
            p2eT[:], p2eT[:], 20.0, None, ALU.min))
        p2c = sch.step("vector", [], lambda v: v.tensor_scalar(
            p2eT[:], p2eT[:], -20.0, None, ALU.max))
        p2e_done = sch.step("scalar", [(V, p2c)], lambda s: s.activation(
            p2eT[:], p2eT[:], AF.Exp))

        x_rel = []
        for ti in range(NT):
            w = [(V, x_rel[-2])] if len(x_rel) >= 2 else []
            ld = sch.step("sync", w, lambda s, ti=ti: s.dma_start(
                out=xt[ti % 2][:], in_=x3[ti * 128:(ti + 1) * 128, :]),
                inc=(LDp[ti % 2], 16))
            sch.step("vector", [(LDp[ti % 2], ld)], lambda v, ti=ti: v.tensor_reduce(
                sumc[:, ti:ti + 1], xt[ti % 2][:], AX.X, ALU.add))
            rel = sch.step("vector", [], lambda v, ti=ti: v.scalar_tensor_tensor(
                scr[:], xt[ti % 2][:], 0.0, xt[ti % 2][:], ALU.add, ALU.mult,
                accum_out=sqc[:, ti:ti + 1]))
            x_rel.append(rel)
        sch.step("vector", [], lambda v: v.tensor_scalar(
            mu3[:], sumc[:], 1.0 / C, None, ALU.mult))
        sch.step("vector", [], lambda v: v.tensor_scalar(
            sqc[:], sqc[:], 1.0 / C, None, ALU.mult))
        sch.step("vector", [], lambda v: v.tensor_tensor(
            scr_nt[:], mu3[:], mu3[:], ALU.mult))
        var_done = sch.step("vector", [], lambda v: v.tensor_tensor(
            sqc[:], sqc[:], scr_nt[:], ALU.subtract))
        std_done = sch.step("scalar", [(V, var_done)], lambda s: s.activation(
            rstd3[:], sqc[:], AF.Sqrt, bias=eps5_col[:, 0:1], scale=1.0))
        stats_done = sch.step("vector", [(A, std_done)], lambda v: v.reciprocal(
            rstd3[:], rstd3[:]))

        psbf_last = [0]
        xnb_T = [0]
        x4_store = {}
        prev_tile_v = 0
        prev_tile_a = 0
        prev_oT_T = 0
        for s_ix in range(BPC):
            cr_reset = sch.step("vector", [], lambda v: (
                v.memset(carry_sb[:], 0.0), v.memset(cden_sb[:], 0.0))[-1])
            for t_ix in range(NTS):
                ti = s_ix * NTS + t_ix
                w = [(V, x_rel[-2])] if len(x_rel) >= 2 else []
                ld = sch.step("sync", w, lambda s, ti=ti: s.dma_start(
                    out=xt[ti % 2][:], in_=x3[ti * 128:(ti + 1) * 128, :]),
                    inc=(LDp[ti % 2], 16))
                nw = [(LDp[ti % 2], ld), (V, stats_done)]
                if xnb_T[0]:
                    nw.append((T, xnb_T[0]))
                nrm = sch.step("vector", nw, lambda v, ti=ti: v.tensor_scalar(
                    xn_b[:], xt[ti % 2][:], mu3[:, ti:ti + 1], rstd3[:, ti:ti + 1],
                    ALU.subtract, ALU.mult))
                evA = []
                for c2 in range(KCH):
                    slot = c2 % 3
                    waits = [(V, nrm)] if c2 == 0 else []
                    if c2 >= 3:
                        waits.append((A, evA[c2 - 3]))
                    if c2 == 0:
                        waits.append((G, idf_done))
                        waits.append((A, psbf_last[0]))
                    sch.step("tensor", waits, lambda t, c2=c2, slot=slot: t.transpose(
                        ps_t[slot][0:128, 0:128],
                        xn_b[:, c2 * 128:(c2 + 1) * 128], ident[:]), inc=(T, 1))
                    ev = sch.step("scalar", [(T, T.n)], lambda s, c2=c2, slot=slot:
                                  s.copy(xnT[0:128, c2 * 128:(c2 + 1) * 128],
                                         ps_t[slot][0:128, 0:128]))
                    evA.append(ev)
                psbf_last[0] = evA[-1]
                xnT_done = evA[-1]
                xnb_T[0] = T.n

                def v_mm(t):
                    r = None
                    for c2 in range(KCH):
                        t.matmul(ps_v1[:, 0:512], xnT[:, c2 * 128:(c2 + 1) * 128],
                                 vw_b[:, c2 * 1024:c2 * 1024 + 512],
                                 start=(c2 == 0), stop=(c2 == KCH - 1))
                        r = t.matmul(ps_v2[:, 0:512], xnT[:, c2 * 128:(c2 + 1) * 128],
                                     vw_b[:, c2 * 1024 + 512:c2 * 1024 + 1024],
                                     start=(c2 == 0), stop=(c2 == KCH - 1))
                    return r
                vw_w = [(A, xnT_done), (W, w_all)]
                if prev_tile_v:
                    vw_w.append((V, prev_tile_v))
                vmm = sch.step("tensor", vw_w, v_mm, inc=(T, 1))
                vev = sch.step("scalar", [(T, vmm)], lambda s: s.copy(
                    v_b[:, 0:512], ps_v1[:, 0:512]))
                vev = sch.step("scalar", [], lambda s: s.copy(
                    v_b[:, 512:1024], ps_v2[:, 0:512]))

                def cde_mm(t):
                    r = None
                    for j in range(3):
                        for c2 in range(KCH):
                            r = t.matmul(ps_c[:, j * 16:(j + 1) * 16],
                                         xnT[:, c2 * 128:(c2 + 1) * 128],
                                         k123_b[:, c2 * 48 + j * 16:c2 * 48 + (j + 1) * 16],
                                         start=(j == 0 and c2 == 0),
                                         stop=(j == 2 and c2 == KCH - 1))
                    return r
                cw = [(A, xnT_done)]
                if prev_tile_v:
                    cw.append((V, prev_tile_v))
                cmm = sch.step("tensor", cw, cde_mm, inc=(T, 1))
                cev = sch.step("scalar", [(T, cmm)], lambda s: s.copy(
                    cde_s[:, 0:48], ps_c[:, 0:48]))

                sch.step("vector", [(A, cev)], lambda v, t_ix=t_ix: v.tensor_tensor(
                    cdeE[:, 0:16], cde_s[:, 0:16],
                    p1T[:, t_ix * H:(t_ix + 1) * H], ALU.add))
                sch.step("vector", [], lambda v: v.tensor_copy(
                    cdeE[:, 16:48], cde_s[:, 16:48]))
                sch.step("vector", [], lambda v: v.tensor_scalar(
                    cdeE[:], cdeE[:], 20.0, None, ALU.min))
                ce1 = sch.step("vector", [], lambda v: v.tensor_scalar(
                    cdeE[:], cdeE[:], -20.0, None, ALU.max))
                ce2 = sch.step("scalar", [(V, ce1), (A, p2e_done)], lambda s: s.activation(
                    cdeE[:], cdeE[:], AF.Exp))
                sv = None
                for h in range(H):
                    sv = sch.step("vector", [(A, ce2)] if h == 0 else [],
                                  lambda v, h=h: v.tensor_scalar(
                                      s_b[:, h * 64:(h + 1) * 64],
                                      v_b[:, h * 64:(h + 1) * 64],
                                      cdeE[:, h:h + 1], None, ALU.mult))
                crb = sch.step("vector", [], lambda v: v.tensor_copy(
                    crE_b[:], cdeE[:, 0:16]))
                cab = sch.step("vector", [], lambda v: v.tensor_copy(
                    carry_b[:], carry_sb[:]))
                cdb = sch.step("vector", [], lambda v: v.tensor_copy(
                    cden_b[:], cden_sb[:]))

                def pre_mm(t):
                    r = None
                    for h in range(H):
                        psd = ps_p1 if h < 8 else ps_p2
                        off = (h % 8) * 64
                        t.matmul(psd[:, off:off + 64], uptri[:],
                                 s_b[:, h * 64:(h + 1) * 64],
                                 start=(h % 8 == 0), stop=False)
                        r = t.matmul(psd[:, off:off + 64], ones_row[:],
                                     carry_b[0:1, h * 64:(h + 1) * 64],
                                     start=False, stop=(h % 8 == 7))
                    return r
                pw = [(V, cdb), (V, ut_done)]
                pmm = sch.step("tensor", pw, pre_mm, inc=(T, 1))
                def den_mm(t):
                    t.matmul(ps_c[:, 48:64], uptri[:], crE_b[:],
                             start=True, stop=False)
                    return t.matmul(ps_c[:, 48:64], ones_row[:], cden_b[0:1, :],
                                    start=False, stop=True)
                dmm = sch.step("tensor", [(V, cdb)], den_mm, inc=(T, 1))
                def csum_mm(t):
                    t.matmul(ps_v1[0:1, 0:512], ones_col[:], s_b[:, 0:512],
                             start=True, stop=True)
                    t.matmul(ps_v2[0:1, 0:512], ones_col[:], s_b[:, 512:1024],
                             start=True, stop=True)
                    return t.matmul(ps_c[0:1, 64:80], ones_col[:], crE_b[:],
                                    start=True, stop=True)
                smm = sch.step("tensor", [(T, pmm), (A, A.n)], csum_mm, inc=(T, 1))
                cup = sch.step("vector", [(T, smm)], lambda v: v.scalar_tensor_tensor(
                    carry_sb[0:1, 0:512], ps_v1[0:1, 0:512], 0.0,
                    carry_sb[0:1, 0:512], ALU.add, ALU.add))
                cup = sch.step("vector", [], lambda v: v.scalar_tensor_tensor(
                    carry_sb[0:1, 512:1024], ps_v2[0:1, 0:512], 0.0,
                    carry_sb[0:1, 512:1024], ALU.add, ALU.add))
                cup = sch.step("vector", [], lambda v: v.scalar_tensor_tensor(
                    cden_sb[0:1, :], ps_c[0:1, 64:80], 0.0,
                    cden_sb[0:1, :], ALU.add, ALU.add))

                sch.step("vector", [], lambda v, t_ix=t_ix: v.tensor_tensor(
                    c1t[:], p2eT[:, t_ix * H:(t_ix + 1) * H], cdeE[:, 32:48], ALU.mult))
                sch.step("vector", [(T, smm)], lambda v: v.tensor_copy(
                    denp[:], ps_c[:, 48:64]))
                sch.step("vector", [], lambda v: v.tensor_tensor(
                    dent[:], denp[:], c1t[:], ALU.mult))
                sch.step("vector", [], lambda v: v.tensor_tensor(
                    dent[:], dent[:], cdeE[:, 16:32], ALU.add))
                dr = sch.step("vector", [], lambda v: v.reciprocal(dent[:], dent[:]))
                ov = None
                for h in range(H):
                    psd = ps_p1 if h < 8 else ps_p2
                    off = (h % 8) * 64
                    sch.step("vector", [(T, pmm)] if h == 0 else [],
                             lambda v, h=h: v.tensor_scalar(
                                 vdiag[:, h * 64:(h + 1) * 64],
                                 v_b[:, h * 64:(h + 1) * 64],
                                 cdeE[:, 16 + h:17 + h], None, ALU.mult))
                    sch.step("vector", [], lambda v, h=h, psd=psd, off=off:
                             v.scalar_tensor_tensor(
                                 numt[:, h * 64:(h + 1) * 64],
                                 psd[:, off:off + 64], c1t[:, h:h + 1],
                                 vdiag[:, h * 64:(h + 1) * 64], ALU.mult, ALU.add))
                    ov = sch.step("vector", [], lambda v, h=h: v.tensor_scalar(
                        o_b[:, h * 64:(h + 1) * 64],
                        numt[:, h * 64:(h + 1) * 64],
                        dent[:, h:h + 1], None, ALU.mult))
                evO = []
                for h in range(H):
                    slot = h % 3
                    waits = [(V, ov)] if h == 0 else []
                    if h >= 3:
                        waits.append((A, evO[h - 3]))
                    if h == 0:
                        waits.append((A, psbf_last[0]))
                        if prev_oT_T:
                            waits.append((T, prev_oT_T))
                    sch.step("tensor", waits, lambda t, h=h, slot=slot: t.transpose(
                        ps_t[slot][0:64, 0:128],
                        o_b[:, h * 64:(h + 1) * 64], ident[:]), inc=(T, 1))
                    ev = sch.step("scalar", [(T, T.n)], lambda s, h=h, slot=slot:
                                  s.copy(oTp[h // 2][(h % 2) * 64:(h % 2) * 64 + 64, :],
                                         ps_t[slot][0:64, 0:128]))
                    evO.append(ev)
                psbf_last[0] = evO[-1]
                def o_mm(t):
                    r = None
                    for j in range(8):
                        t.matmul(ps_v1[:, 0:512], oTp[j][:],
                                 ow_b[:, j * C:j * C + 512],
                                 start=(j == 0), stop=(j == 7))
                        r = t.matmul(ps_v2[:, 0:512], oTp[j][:],
                                     ow_b[:, j * C + 512:j * C + 1024],
                                     start=(j == 0), stop=(j == 7))
                    return r
                omm = sch.step("tensor", [(A, evO[-1]), (V, cup)], o_mm, inc=(T, 1))
                prev_oT_T = omm
                x4w = [(T, omm)]
                if (ti - 2) in x4_store:
                    x4w.append((STp[ti % 2], x4_store[ti - 2]))
                sch.step("vector", x4w, lambda v, ti=ti: v.tensor_tensor(
                    x4t[ti % 2][:, 0:512], xt[ti % 2][:, 0:512],
                    ps_v1[:, 0:512], ALU.add))
                x4v = sch.step("vector", [], lambda v, ti=ti: v.tensor_tensor(
                    x4t[ti % 2][:, 512:1024], xt[ti % 2][:, 512:1024],
                    ps_v2[:, 0:512], ALU.add))
                x_rel.append(x4v)
                sch.step("vector", [], lambda v, ti=ti: v.tensor_reduce(
                    sumc[:, ti:ti + 1], x4t[ti % 2][:], AX.X, ALU.add))
                st2 = sch.step("vector", [], lambda v, ti=ti: v.scalar_tensor_tensor(
                    scr[:], x4t[ti % 2][:], 0.0, x4t[ti % 2][:], ALU.add, ALU.mult,
                    accum_out=sqc[:, ti:ti + 1]))
                prev_tile_v = st2
                stv = sch.step("sync", [(V, x4v)], lambda s, ti=ti: s.dma_start(
                    out=x4_out[ti * 128:(ti + 1) * 128, :], in_=x4t[ti % 2][:]),
                    inc=(STp[ti % 2], 16))
                x4_store[ti] = stv

        sch.step("vector", [], lambda v: v.tensor_scalar(
            mu4[:], sumc[:], 1.0 / C, None, ALU.mult))
        sch.step("vector", [], lambda v: v.tensor_scalar(
            sqc[:], sqc[:], 1.0 / C, None, ALU.mult))
        sch.step("vector", [], lambda v: v.tensor_tensor(
            scr_nt[:], mu4[:], mu4[:], ALU.mult))
        var4 = sch.step("vector", [], lambda v: v.tensor_tensor(
            sqc[:], sqc[:], scr_nt[:], ALU.subtract))
        sd4 = sch.step("scalar", [(V, var4)], lambda s: s.activation(
            rstd4[:], sqc[:], AF.Sqrt, bias=eps5_col[:, 0:1], scale=1.0))
        st4 = sch.step("vector", [(A, sd4)], lambda v: v.reciprocal(
            rstd4[:], rstd4[:]))
        for ti in range(NT):
            w = [(V, x_rel[-2])] if len(x_rel) >= 2 else []
            w = w + [(ST, ST.n), (STB, STB.n)]
            ld = sch.step("sync", w, lambda s, ti=ti: s.dma_start(
                out=xt[ti % 2][:], in_=x4_out[ti * 128:(ti + 1) * 128, :]),
                inc=(LDp[ti % 2], 16))
            nrm = sch.step("vector", [(LDp[ti % 2], ld), (V, st4)],
                           lambda v, ti=ti: v.tensor_scalar(
                               scr[:], xt[ti % 2][:], mu4[:, ti:ti + 1],
                               rstd4[:, ti:ti + 1], ALU.subtract, ALU.mult))
            x_rel.append(nrm)
            sch.step("vector", [], lambda v: v.tensor_reduce(
                colA[:], scr[:], AX.X, ALU.max, apply_absolute_value=True))
            m0u = sch.step("vector", [], lambda v: v.tensor_tensor(
                m0c[:], m0c[:], colA[:], ALU.max))
        m0t = sch.step("tensor", [(V, m0u), (G, idf_done)], lambda t: t.transpose(
            ps_c[0:1, 0:128], m0c[:], identf[:]), inc=(T, 1))
        m0e = sch.step("vector", [(T, m0t)], lambda v: v.tensor_copy(
            m0row[:], ps_c[0:1, 0:128]))
        m0r = sch.step("vector", [], lambda v: v.tensor_reduce(
            m0s[:], m0row[:], AX.X, ALU.max))
        sch.step("sync", [(V, m0r)], lambda s: s.dma_start(
            out=m0_out, in_=m0s[:]), inc=(ST, 16))
        sch.step("sync", [], lambda s: s.dma_start(
            out=mu4_out, in_=mu4[:]), inc=(ST, 16))
        sch.step("sync", [], lambda s: s.dma_start(
            out=rstd4_out, in_=rstd4[:]), inc=(ST, 16))
        sch.step("gpsimd", [(ST, ST.n), (STB, STB.n)], lambda g: None)
        sch.emit(block)
    return nc


_STAGE_B2 = None


def run_stage_b2(x3_full, inputs):
    global _STAGE_B2
    if _STAGE_B2 is None:
        _STAGE_B2 = build_stage_b2()
    k1 = np.asarray(inputs["L1_attn_k1"], np.float32)
    k2 = np.asarray(inputs["L1_attn_k2"], np.float32)
    k3 = np.asarray(inputs["L1_attn_k3"], np.float32)
    k123T = np.ascontiguousarray(np.concatenate([k1.T, k2.T, k3.T], axis=1))
    a1 = np.asarray(inputs["L1_attn_a1"], np.float32).reshape(H, 64)
    a2 = np.asarray(inputs["L1_attn_a2"], np.float32).reshape(H, 64)
    b1 = np.asarray(inputs["L1_attn_b1"], np.float32).reshape(H, 64)
    b2 = np.asarray(inputs["L1_attn_b2"], np.float32).reshape(H, 64)
    cc = np.asarray(inputs["L1_attn_c"], np.float32).reshape(H, 64)
    abR = np.zeros((1, 2 * H * 64), np.float32)
    abR[0, :H * 64] = a1.reshape(-1)
    abR[0, H * 64:] = a2.reshape(-1)
    bT = np.ascontiguousarray(np.concatenate([b1.T, b2.T], axis=1))
    ccT = np.ascontiguousarray(cc.T)
    vw = np.asarray(inputs["L1_attn_value_weight"], np.float32)
    vwF = np.ascontiguousarray(vw.transpose(1, 0, 2).reshape(C, H * 64))
    ow = np.asarray(inputs["L1_attn_output_weight"], np.float32)
    owS = np.ascontiguousarray(ow.transpose(0, 2, 1).reshape(H * 64, C))
    posT = np.arange(N, dtype=np.float32).reshape(1, N)
    in_maps = []
    for c in range(NCORES):
        x3c = np.ascontiguousarray(x3_full[c * BPC:(c + 1) * BPC].reshape(TPC, C))
        in_maps.append(dict(x3=x3c, k123T=k123T, abR=abR, bT=bT, ccT=ccT,
                            vwF=vwF, owS=owS, posT=posT,
                            utri=np.triu(np.ones((128, 128), np.float32))))
    res = run_bass_kernel_spmd(_STAGE_B2, in_maps, core_ids=list(range(NCORES)))
    x4 = np.stack([res.results[c]["x4_out"].reshape(BPC, N, C)
                   for c in range(NCORES)]).reshape(B, N, C)
    m0 = np.stack([res.results[c]["m0_out"].reshape(1) for c in range(NCORES)])
    mu4 = np.stack([res.results[c]["mu4_out"] for c in range(NCORES)])
    rstd4 = np.stack([res.results[c]["rstd4_out"] for c in range(NCORES)])
    return x4, m0, mu4, rstd4



def build_stage_l1ffn(mode):
    nc = bass.Bass("TRN2", target_bir_lowering=False)
    x4 = nc.dram_tensor("x4", [TPC, C], F32, kind="ExternalInput").ap()
    mu4c = nc.dram_tensor("mu4c", [128, NT], F32, kind="ExternalInput").ap()
    rstd4c = nc.dram_tensor("rstd4c", [128, NT], F32, kind="ExternalInput").ap()
    m_all = nc.dram_tensor("m_all", [1, 64], F32, kind="ExternalInput").ap()
    wu_nat = nc.dram_tensor("wu_nat", [3072, C], F32, kind="ExternalInput").ap()
    wg_nat = nc.dram_tensor("wg_nat", [3072, C], F32, kind="ExternalInput").ap()
    if mode == "c":
        m_out = nc.dram_tensor("m_out", [1, 2], F32, kind="ExternalOutput").ap()
    elif mode in ("d", "e"):
        m_out = nc.dram_tensor("m_out", [1, 1], F32, kind="ExternalOutput").ap()
    else:
        o_out = nc.dram_tensor("o_out", [TPC, 3072], BF16, kind="ExternalOutput").ap()

    QMAX = 32767.0
    NRC = 3072 // 128
    NDC = 6

    with ExitStack() as ctx:
        sbt = lambda n, s, d=F32: ctx.enter_context(nc.sbuf_tensor(n, s, d))
        pst = lambda n, s, d=F32: ctx.enter_context(nc.psum_tensor(n, s, d))

        ones_rowf = sbt("ones_rowf", [1, 128], F32)
        ident = sbt("ident", [128, 128], BF16)
        identf = sbt("identf", [128, 128], F32)
        m_s = sbt("m_s", [1, 64], F32)
        mrow = sbt("mrow", [1, 1], F32)
        sc_sc = sbt("sc_sc", [1, 8])
        rrow = sbt("rrow", [1, 8])
        scB = sbt("scB", [128, 8])
        scB2 = sbt("scB2", [128, 8])
        mu4s = sbt("mu4s", [128, NT], F32)
        rstd4s = sbt("rstd4s", [128, NT], F32)
        xt = [sbt(f"xt{i}", [128, C]) for i in range(2)]
        xq_b = sbt("xq_b", [128, C], BF16)
        scr = sbt("scr", [128, C])
        xqT = sbt("xqT", [128, KCH * (NT * 128)], BF16)
        wt_f = [sbt(f"wt_f{i}", [128, C]) for i in range(2)]
        wsc = sbt("wsc", [128, 32])
        wsc_r = sbt("wsc_r", [128, 32])
        wdq_b = sbt("wdq_b", [128, C], BF16)
        wuT = sbt("wuT", [128, KCH * 3072], BF16)
        wgT = sbt("wgT", [128, KCH * 3072], BF16)
        upf = sbt("upf", [128, 512])
        gtf = sbt("gtf", [128, 512])
        sigf = sbt("sigf", [128, 512])
        t3f = sbt("t3f", [128, 512])
        t4f = sbt("t4f", [128, 512])
        colm = [sbt(f"colm{i}", [128, 1]) for i in range(2)]
        m_run = sbt("m_run", [128, 2])
        mrow128 = sbt("mrow128", [1, 128])
        mres = sbt("mres", [1, 2])
        if mode == "f1":
            of_t = [sbt("of_t0", [128, 3072], BF16)] * 2

        ps_t = [pst(f"ps_t{i}", [128, 128], BF16) for i in range(3)]
        ps_u1 = pst("ps_u1", [128, 512])
        ps_u2 = pst("ps_u2", [128, 512])
        ps_g1 = pst("ps_g1", [128, 512])
        ps_g2 = pst("ps_g2", [128, 512])
        ps_m = pst("ps_m", [128, 512])

        S = {nm: Sem(ctx.enter_context(nc.semaphore(nm)))
             for nm in ("LD", "LDB", "WL", "WLB", "CN", "V", "A", "T", "G", "ST", "STB")}
        (LD, LDB, WL, WLB, CN, V, A, T, G, ST, STB) = (
            S[n] for n in ("LD", "LDB", "WL", "WLB", "CN", "V", "A", "T", "G", "ST", "STB"))
        LDp = [LD, LDB]
        WLp = [WL, WLB]
        STp = [ST, STB]
        block = ctx.enter_context(nc.Block())
        sch = Sched({"vector": V, "scalar": A, "gpsimd": G})

        sch.step("sync", [], lambda s: s.dma_start(out=m_s[:], in_=m_all), inc=(CN, 16))
        sch.step("sync", [], lambda s: s.dma_start(out=mu4s[:], in_=mu4c), inc=(CN, 16))
        sch.step("sync", [], lambda s: s.dma_start(out=rstd4s[:], in_=rstd4c), inc=(CN, 16))
        cn_all = CN.n
        consts_v = sch.step("vector", [], lambda v: (
            v.memset(ones_rowf[:], 1.0), v.memset(identf[:], 1.0),
            v.memset(sc_sc[:], 1.0), v.memset(rrow[:], 1.0),
            v.memset(m_run[:], 0.0))[-1])
        id_ms = sch.step("gpsimd", [], lambda g: g.memset(ident[:], 1.0))
        id_done = sch.step("gpsimd", [(G, id_ms)], lambda g: g.affine_select(
            ident[:], ident[:], pattern=[[-1, 128]], compare_op=ALU.is_equal,
            fill=0.0, base=0, channel_multiplier=1))
        idf_done = sch.step("gpsimd", [(G, id_done), (V, consts_v)],
                            lambda g: g.affine_select(
                                identf[:], identf[:], pattern=[[-1, 128]],
                                compare_op=ALU.is_equal, fill=0.0, base=0,
                                channel_multiplier=1))

        nsc = {"c": 1, "d": 3, "e": 4, "f1": 5}[mode]
        sc_done = None
        for j in range(nsc):
            sch.step("vector", [(CN, cn_all), (V, consts_v)] if j == 0 else [],
                     lambda v, j=j: v.tensor_reduce(
                         mrow[:], m_s[0:1, j * 8:(j + 1) * 8], AX.X, ALU.max))
            sch.step("vector", [], lambda v, j=j: v.tensor_scalar(
                sc_sc[0:1, j:j + 1], mrow[:], 1.0 / QMAX, None, ALU.mult))
            sc_done = sch.step("vector", [], lambda v, j=j: v.tensor_scalar(
                sc_sc[0:1, j:j + 1], sc_sc[0:1, j:j + 1], 1e-12, None, ALU.max))
        rc = sch.step("vector", [(V, sc_done)], lambda v: v.reciprocal(
            rrow[0:1, 0:nsc], sc_sc[0:1, 0:nsc]))
        bc1 = sch.step("tensor", [(V, rc)], lambda t: t.matmul(
            ps_m[:, 0:8], ones_rowf[:], sc_sc[:], start=True, stop=True), inc=(T, 1))
        bv1 = sch.step("vector", [(T, bc1)], lambda v: v.tensor_copy(
            scB[:], ps_m[:, 0:8]))
        bc2 = sch.step("tensor", [(V, bv1)], lambda t: t.matmul(
            ps_m[:, 0:8], ones_rowf[:], rrow[:], start=True, stop=True), inc=(T, 1))
        bv2 = sch.step("vector", [(T, bc2)], lambda v: v.tensor_copy(
            scB2[:], ps_m[:, 0:8]))
        s_col = lambda j: scB[:, j:j + 1]
        r_col = lambda j: scB2[:, j:j + 1]

        x_rel = []
        psbf_last = [0]
        xnb_T = [0]
        for ti in range(NT):
            w = [(V, x_rel[-2])] if len(x_rel) >= 2 else []
            ld = sch.step("sync", w, lambda s, ti=ti: s.dma_start(
                out=xt[ti % 2][:], in_=x4[ti * 128:(ti + 1) * 128, :]),
                inc=(LDp[ti % 2], 16))
            nw = [(LDp[ti % 2], ld), (V, bv2)]
            if xnb_T[0]:
                nw.append((T, xnb_T[0]))
            sch.step("vector", nw, lambda v, ti=ti: v.tensor_scalar(
                scr[:], xt[ti % 2][:], mu4s[:, ti:ti + 1], rstd4s[:, ti:ti + 1],
                ALU.subtract, ALU.mult))
            sch.step("vector", [], lambda v: v.tensor_scalar(
                scr[:], scr[:], r_col(0), MAGIC, ALU.mult, ALU.add))
            sch.step("vector", [], lambda v: v.tensor_scalar(
                scr[:], scr[:], MAGIC, QMAX, ALU.subtract, ALU.min))
            nrm = sch.step("vector", [], lambda v: v.tensor_scalar(
                xq_b[:], scr[:], -QMAX - 1.0, s_col(0), ALU.max, ALU.mult))
            x_rel.append(nrm)
            evA = []
            for c2 in range(KCH):
                slot = c2 % 3
                waits = [(V, nrm)] if c2 == 0 else []
                if c2 >= 3:
                    waits.append((A, evA[c2 - 3]))
                if c2 == 0:
                    waits.append((G, idf_done))
                    waits.append((A, psbf_last[0]))
                sch.step("tensor", waits, lambda t, c2=c2, slot=slot: t.transpose(
                    ps_t[slot][0:128, 0:128],
                    xq_b[:, c2 * 128:(c2 + 1) * 128], ident[:]), inc=(T, 1))
                ev = sch.step("scalar", [(T, T.n)], lambda s, c2=c2, slot=slot, ti=ti:
                              s.copy(xqT[0:128, c2 * (NT * 128) + ti * 128:
                                          c2 * (NT * 128) + ti * 128 + 128],
                                     ps_t[slot][0:128, 0:128]))
                evA.append(ev)
            psbf_last[0] = evA[-1]
            xnb_T[0] = T.n
        xqT_done = evA[-1]

        wdq_T = [0]
        def dequant_weight(w_nat, dstT, first_v):
            last_v = first_v
            last_ev = psbf_last[0]
            for rc_ in range(NRC):
                buf = wt_f[rc_ % 2]
                lw = [(V, last_v)] if last_v else []
                ldw = sch.step("sync", lw, lambda s, rc_=rc_, w_nat=w_nat, buf=buf:
                               s.dma_start(out=buf[:],
                                           in_=w_nat[rc_ * 128:(rc_ + 1) * 128, :]),
                               inc=(WLp[rc_ % 2], 16))
                dqw = [(WLp[rc_ % 2], ldw)]
                if wdq_T[0]:
                    dqw.append((T, wdq_T[0]))
                sch.step("vector", dqw, lambda v, buf=buf: v.tensor_reduce(
                    wsc[:], buf[:].rearrange("p (b k) -> p b k", k=32),
                    AX.X, ALU.max, apply_absolute_value=True))
                sch.step("vector", [], lambda v: v.tensor_scalar(
                    wsc[:], wsc[:], 1.0 / 7.0, 1e-12, ALU.mult, ALU.max))
                sch.step("vector", [], lambda v: v.reciprocal(wsc_r[:], wsc[:]))
                sc3 = bass.AP(wsc_r[:].tensor, 0, [[32, 128], [1, 32], [0, 32]])
                sc3b = bass.AP(wsc[:].tensor, 0, [[32, 128], [1, 32], [0, 32]])
                b3 = buf[:].rearrange("p (b k) -> p b k", k=32)
                sch.step("vector", [], lambda v, b3=b3, sc3=sc3: v.tensor_tensor(
                    b3, b3, sc3, ALU.mult))
                sch.step("vector", [], lambda v, buf=buf: v.tensor_scalar(
                    buf[:], buf[:], MAGIC, MAGIC, ALU.add, ALU.subtract))
                sch.step("vector", [], lambda v, buf=buf: v.tensor_scalar(
                    buf[:], buf[:], 7.0, -8.0, ALU.min, ALU.max))
                lv = sch.step("vector", [], lambda v, b3=b3, sc3b=sc3b: v.tensor_tensor(
                    wdq_b[:].rearrange("p (b k) -> p b k", k=32), b3, sc3b, ALU.mult))
                last_v = lv
                evB = []
                for c2 in range(KCH):
                    slot = c2 % 3
                    waits = [(V, lv)] if c2 == 0 else []
                    if c2 >= 3:
                        waits.append((A, evB[c2 - 3]))
                    if c2 == 0:
                        waits.append((A, psbf_last[0]))
                    sch.step("tensor", waits, lambda t, c2=c2, slot=slot: t.transpose(
                        ps_t[slot][0:128, 0:128],
                        wdq_b[:, c2 * 128:(c2 + 1) * 128], ident[:]), inc=(T, 1))
                    ev = sch.step("scalar", [(T, T.n)],
                                  lambda s, c2=c2, slot=slot, rc_=rc_, dstT=dstT:
                                  s.copy(dstT[0:128, c2 * 3072 + rc_ * 128:
                                              c2 * 3072 + rc_ * 128 + 128],
                                         ps_t[slot][0:128, 0:128]))
                    evB.append(ev)
                psbf_last[0] = evB[-1]
                wdq_T[0] = T.n
                last_ev = evB[-1]
            return last_v, last_ev

        vU, aU = dequant_weight(wu_nat, wuT, 0)
        vG, aG = dequant_weight(wg_nat, wgT, vU)
        wT_done = aG

        frag_hist = []
        o_store = {}
        fv = 0
        for ti in range(NT):
            for dc in range(NDC):
                pu = ps_u1 if dc % 2 == 0 else ps_u2
                pg = ps_g1 if dc % 2 == 0 else ps_g2
                def up_mm(t, ti=ti, dc=dc, pu=pu, pg=pg):
                    r = None
                    for c2 in range(KCH):
                        t.matmul(pu[:, 0:512],
                                 xqT[:, c2 * (NT * 128) + ti * 128:
                                     c2 * (NT * 128) + ti * 128 + 128],
                                 wuT[:, c2 * 3072 + dc * 512:c2 * 3072 + dc * 512 + 512],
                                 start=(c2 == 0), stop=(c2 == KCH - 1))
                        r = t.matmul(pg[:, 0:512],
                                     xqT[:, c2 * (NT * 128) + ti * 128:
                                         c2 * (NT * 128) + ti * 128 + 128],
                                     wgT[:, c2 * 3072 + dc * 512:c2 * 3072 + dc * 512 + 512],
                                     start=(c2 == 0), stop=(c2 == KCH - 1))
                    return r
                mw = []
                if ti == 0 and dc == 0:
                    mw = [(A, wT_done), (A, xqT_done)]
                if len(frag_hist) >= 2:
                    mw.append((V, frag_hist[-2]))
                umm = sch.step("tensor", mw, up_mm, inc=(T, 1))
                if mode == "c":
                    sch.step("vector", [(T, umm)], lambda v, pu=pu: v.tensor_reduce(
                        colm[0][:], pu[:, 0:512], AX.X, ALU.max,
                        apply_absolute_value=True))
                    sch.step("vector", [], lambda v: v.tensor_tensor(
                        m_run[:, 0:1], m_run[:, 0:1], colm[0][:], ALU.max))
                    sch.step("vector", [], lambda v, pg=pg: v.tensor_reduce(
                        colm[1][:], pg[:, 0:512], AX.X, ALU.max,
                        apply_absolute_value=True))
                    fv = sch.step("vector", [], lambda v: v.tensor_tensor(
                        m_run[:, 1:2], m_run[:, 1:2], colm[1][:], ALU.max))
                else:
                    sch.step("vector", [(T, umm)], lambda v, pu=pu: v.tensor_scalar(
                        upf[:], pu[:, 0:512], r_col(1), MAGIC, ALU.mult, ALU.add))
                    sch.step("vector", [], lambda v: v.tensor_scalar(
                        upf[:], upf[:], MAGIC, QMAX, ALU.subtract, ALU.min))
                    sch.step("vector", [], lambda v: v.tensor_scalar(
                        upf[:], upf[:], -QMAX - 1.0, s_col(1), ALU.max, ALU.mult))
                    sch.step("vector", [], lambda v, pg=pg: v.tensor_scalar(
                        gtf[:], pg[:, 0:512], r_col(2), MAGIC, ALU.mult, ALU.add))
                    sch.step("vector", [], lambda v: v.tensor_scalar(
                        gtf[:], gtf[:], MAGIC, QMAX, ALU.subtract, ALU.min))
                    gq = sch.step("vector", [], lambda v: v.tensor_scalar(
                        gtf[:], gtf[:], -QMAX - 1.0, s_col(2), ALU.max, ALU.mult))
                    sg = sch.step("scalar", [(V, gq)], lambda s: s.activation(
                        sigf[:], gtf[:], AF.Sigmoid))
                    sch.step("vector", [(A, sg)], lambda v: v.tensor_scalar(
                        sigf[:], sigf[:], 65536.0, MAGIC, ALU.mult, ALU.add))
                    sch.step("vector", [], lambda v: v.tensor_scalar(
                        sigf[:], sigf[:], MAGIC, 65535.0, ALU.subtract, ALU.min))
                    sch.step("vector", [], lambda v: v.tensor_scalar(
                        sigf[:], sigf[:], 0.0, 1.0 / 65536.0, ALU.max, ALU.mult))
                    t3v = sch.step("vector", [], lambda v: v.tensor_tensor(
                        t3f[:], gtf[:], sigf[:], ALU.mult))
                    if mode == "d":
                        sch.step("vector", [], lambda v: v.tensor_reduce(
                            colm[0][:], t3f[:], AX.X, ALU.max,
                            apply_absolute_value=True))
                        fv = sch.step("vector", [], lambda v: v.tensor_tensor(
                            m_run[:, 0:1], m_run[:, 0:1], colm[0][:], ALU.max))
                    else:
                        sch.step("vector", [], lambda v: v.tensor_scalar(
                            t3f[:], t3f[:], r_col(3), MAGIC, ALU.mult, ALU.add))
                        sch.step("vector", [], lambda v: v.tensor_scalar(
                            t3f[:], t3f[:], MAGIC, QMAX, ALU.subtract, ALU.min))
                        sch.step("vector", [], lambda v: v.tensor_scalar(
                            t3f[:], t3f[:], -QMAX - 1.0, s_col(3), ALU.max, ALU.mult))
                        t4v = sch.step("vector", [], lambda v: v.tensor_tensor(
                            t4f[:], t3f[:], upf[:], ALU.mult))
                        if mode == "e":
                            sch.step("vector", [], lambda v: v.tensor_reduce(
                                colm[0][:], t4f[:], AX.X, ALU.max,
                                apply_absolute_value=True))
                            fv = sch.step("vector", [], lambda v: v.tensor_tensor(
                                m_run[:, 0:1], m_run[:, 0:1], colm[0][:], ALU.max))
                        else:
                            sch.step("vector", [], lambda v: v.tensor_scalar(
                                t4f[:], t4f[:], r_col(4), MAGIC, ALU.mult, ALU.add))
                            sch.step("vector", [], lambda v: v.tensor_scalar(
                                t4f[:], t4f[:], MAGIC, QMAX, ALU.subtract, ALU.min))
                            ow_ = [(STp[(ti - 1) % 2], o_store[ti - 1])] \
                                if (dc == 0 and (ti - 1) in o_store) else []
                            fv = sch.step("vector", ow_, lambda v, dc=dc, ti=ti:
                                          v.tensor_scalar(
                                              of_t[ti % 2][:, dc * 512:(dc + 1) * 512],
                                              t4f[:], -QMAX - 1.0, s_col(4),
                                              ALU.max, ALU.mult))
                frag_hist.append(fv)
            if mode == "f1":
                stv = sch.step("sync", [(V, fv)], lambda s, ti=ti: s.dma_start(
                    out=o_out[ti * 128:(ti + 1) * 128, :], in_=of_t[ti % 2][:]),
                    inc=(STp[ti % 2], 16))
                o_store[ti] = stv

        if mode != "f1":
            nout = 2 if mode == "c" else 1
            last = fv
            for j in range(nout):
                tj = sch.step("tensor", [(V, last), (G, idf_done)],
                              lambda t, j=j: t.transpose(
                                  ps_m[0:1, 0:128], m_run[:, j:j + 1], identf[:]),
                              inc=(T, 1))
                sch.step("vector", [(T, tj)], lambda v: v.tensor_copy(
                    mrow128[:], ps_m[0:1, 0:128]))
                last = sch.step("vector", [], lambda v, j=j: v.tensor_reduce(
                    mres[0:1, j:j + 1], mrow128[:], AX.X, ALU.max))
            sch.step("sync", [(V, last)], lambda s: s.dma_start(
                out=m_out, in_=mres[0:1, 0:nout]), inc=(ST, 16))
        sch.step("gpsimd", [(ST, ST.n), (STB, STB.n)], lambda g: None)
        sch.emit(block)
    return nc



_STAGE_A = None


def _stage_a_nc():
    global _STAGE_A
    if _STAGE_A is None:
        _STAGE_A = build_stage_a()
    return _STAGE_A


def run_stage_a(inputs):
    x = np.asarray(inputs["L0_attn_x"], np.float32)
    wqT = np.ascontiguousarray(np.asarray(inputs["L0_attn_Wq"], np.float32).T)
    wkT = np.ascontiguousarray(np.asarray(inputs["L0_attn_Wk"], np.float32).T)
    wvT = np.ascontiguousarray(np.asarray(inputs["L0_attn_Wv"], np.float32).T)
    bqkv = np.stack([np.asarray(inputs["L0_attn_bq"], np.float32),
                     np.asarray(inputs["L0_attn_bk"], np.float32),
                     np.asarray(inputs["L0_attn_bv"], np.float32)])
    gb = np.stack([np.asarray(inputs["L0_attn_gamma"], np.float32),
                   np.asarray(inputs["L0_attn_beta"], np.float32)])
    wg = np.asarray(inputs["L0_ffn_Wg"], np.float32)
    posT = np.arange(N, dtype=np.float32).reshape(1, N)
    jfrac = (np.arange(0, C, 2, dtype=np.float32) / np.float32(C)).reshape(1, C // 2)

    nc = _stage_a_nc()
    in_maps = []
    for c in range(NCORES):
        xa = np.ascontiguousarray(x[c * BPC:(c + 1) * BPC].reshape(TPC, C))
        in_maps.append(dict(xA=xa, wqT=wqT, wkT=wkT, wvT=wvT, bqkv=bqkv, gb=gb,
                            wg=wg, posT=posT, jfrac=jfrac))
    res = run_bass_kernel_spmd(nc, in_maps, core_ids=list(range(NCORES)))
    x2 = np.stack([res.results[c]["x2_out"].reshape(BPC, N, C)
                   for c in range(NCORES)]).reshape(B, N, C)
    probs = np.stack([res.results[c]["probs_out"].reshape(BPC * N, 8)
                      for c in range(NCORES)]).reshape(B * N, 8)
    return x2, probs


def build_stage_f2():
    nc = bass.Bass("TRN2", target_bir_lowering=False)
    o_in = nc.dram_tensor("o_in", [TPC, 3072], BF16, kind="ExternalInput").ap()
    x4 = nc.dram_tensor("x4", [TPC, C], F32, kind="ExternalInput").ap()
    wd_nat = nc.dram_tensor("wd_nat", [C, 3072], F32, kind="ExternalInput").ap()
    y_out = nc.dram_tensor("y_out", [TPC, C], F32, kind="ExternalOutput").ap()

    NI3 = 24

    with ExitStack() as ctx:
        sbt = lambda n, s, d=F32: ctx.enter_context(nc.sbuf_tensor(n, s, d))
        pst = lambda n, s, d=F32: ctx.enter_context(nc.psum_tensor(n, s, d))
        ident = sbt("ident", [128, 128], BF16)
        wt_f = [sbt(f"wt_f{i}", [128, 3072]) for i in range(2)]
        wsc = sbt("wsc", [128, 96])
        wsc_r = sbt("wsc_r", [128, 96])
        wdq_b = sbt("wdq_b", [128, 3072], BF16)
        wdT = sbt("wdT", [128, NI3 * C], BF16)
        ot_b = [sbt(f"ot_b{i}", [128, 3072], BF16) for i in range(2)]
        oT = sbt("oT", [128, NI3 * 128], BF16)
        xt = [sbt(f"xt{i}", [128, C]) for i in range(2)]
        yt = [sbt(f"yt{i}", [128, C]) for i in range(2)]
        ps_t = [pst(f"ps_t{i}", [128, 128], BF16) for i in range(3)]
        ps_y1 = pst("ps_y1", [128, 512])
        ps_y2 = pst("ps_y2", [128, 512])

        S = {nm: Sem(ctx.enter_context(nc.semaphore(nm)))
             for nm in ("LD", "LDB", "LO", "LOB", "WL", "WLB", "V", "A", "T", "G", "ST", "STB")}
        (LD, LDB, LO, LOB, WL, WLB, V, A, T, G, ST, STB) = (
            S[n] for n in ("LD", "LDB", "LO", "LOB", "WL", "WLB", "V", "A", "T", "G",
                           "ST", "STB"))
        WLp = [WL, WLB]
        LDp = [LD, LDB]
        LOp = [LO, LOB]
        STp = [ST, STB]
        block = ctx.enter_context(nc.Block())
        sch = Sched({"vector": V, "scalar": A, "gpsimd": G})

        id_ms = sch.step("gpsimd", [], lambda g: g.memset(ident[:], 1.0))
        id_done = sch.step("gpsimd", [(G, id_ms)], lambda g: g.affine_select(
            ident[:], ident[:], pattern=[[-1, 128]], compare_op=ALU.is_equal,
            fill=0.0, base=0, channel_multiplier=1))

        psbf_last = [0]
        wdq_T = [0]
        last_v = 0
        for rc_ in range(KCH):
            buf = wt_f[rc_ % 2]
            lw = [(V, last_v)] if rc_ >= 2 else []
            ldw = sch.step("sync", lw, lambda s, rc_=rc_, buf=buf: s.dma_start(
                out=buf[:], in_=wd_nat[rc_ * 128:(rc_ + 1) * 128, :]),
                inc=(WLp[rc_ % 2], 16))
            dqw = [(WLp[rc_ % 2], ldw)]
            if wdq_T[0]:
                dqw.append((T, wdq_T[0]))
            sch.step("vector", dqw, lambda v, buf=buf: v.tensor_reduce(
                wsc[:], buf[:].rearrange("p (b k) -> p b k", k=32),
                AX.X, ALU.max, apply_absolute_value=True))
            sch.step("vector", [], lambda v: v.tensor_scalar(
                wsc[:], wsc[:], 1.0 / 7.0, 1e-12, ALU.mult, ALU.max))
            sch.step("vector", [], lambda v: v.reciprocal(wsc_r[:], wsc[:]))
            sc3 = bass.AP(wsc_r[:].tensor, 0, [[96, 128], [1, 96], [0, 32]])
            sc3b = bass.AP(wsc[:].tensor, 0, [[96, 128], [1, 96], [0, 32]])
            b3 = buf[:].rearrange("p (b k) -> p b k", k=32)
            sch.step("vector", [], lambda v, b3=b3, sc3=sc3: v.tensor_tensor(
                b3, b3, sc3, ALU.mult))
            sch.step("vector", [], lambda v, buf=buf: v.tensor_scalar(
                buf[:], buf[:], MAGIC, MAGIC, ALU.add, ALU.subtract))
            sch.step("vector", [], lambda v, buf=buf: v.tensor_scalar(
                buf[:], buf[:], 7.0, -8.0, ALU.min, ALU.max))
            lv = sch.step("vector", [], lambda v, b3=b3, sc3b=sc3b: v.tensor_tensor(
                wdq_b[:].rearrange("p (b k) -> p b k", k=32), b3, sc3b, ALU.mult))
            last_v = lv
            evB = []
            for c2 in range(NI3):
                slot = c2 % 3
                waits = [(V, lv)] if c2 == 0 else []
                if c2 >= 3:
                    waits.append((A, evB[c2 - 3]))
                if c2 == 0:
                    waits.append((G, id_done))
                    waits.append((A, psbf_last[0]))
                sch.step("tensor", waits, lambda t, c2=c2, slot=slot: t.transpose(
                    ps_t[slot][0:128, 0:128],
                    wdq_b[:, c2 * 128:(c2 + 1) * 128], ident[:]), inc=(T, 1))
                ev = sch.step("scalar", [(T, T.n)],
                              lambda s, c2=c2, slot=slot, rc_=rc_: s.copy(
                                  wdT[0:128, c2 * C + rc_ * 128:
                                      c2 * C + rc_ * 128 + 128],
                                  ps_t[slot][0:128, 0:128]))
                evB.append(ev)
            psbf_last[0] = evB[-1]
            wdq_T[0] = T.n
        wT_done = psbf_last[0]

        o_rel = []
        x_rel = []
        oT_T = [0]
        y_store = {}
        for ti in range(NT):
            w = [(T, o_rel[-2])] if len(o_rel) >= 2 else []
            ldo = sch.step("sync", w, lambda s, ti=ti: s.dma_start(
                out=ot_b[ti % 2][:], in_=o_in[ti * 128:(ti + 1) * 128, :]),
                inc=(LOp[ti % 2], 16))
            wx = [(V, x_rel[-2])] if len(x_rel) >= 2 else []
            ldx = sch.step("sync", wx, lambda s, ti=ti: s.dma_start(
                out=xt[ti % 2][:], in_=x4[ti * 128:(ti + 1) * 128, :]),
                inc=(LDp[ti % 2], 16))
            evA = []
            for c2 in range(NI3):
                slot = c2 % 3
                waits = [(LOp[ti % 2], ldo)] if c2 == 0 else []
                if c2 >= 3:
                    waits.append((A, evA[c2 - 3]))
                if c2 == 0:
                    waits.append((A, psbf_last[0]))
                    if oT_T[0]:
                        waits.append((T, oT_T[0]))
                sch.step("tensor", waits, lambda t, c2=c2, slot=slot, ti=ti:
                         t.transpose(ps_t[slot][0:128, 0:128],
                                     ot_b[ti % 2][:, c2 * 128:(c2 + 1) * 128],
                                     ident[:]), inc=(T, 1))
                ev = sch.step("scalar", [(T, T.n)], lambda s, c2=c2, slot=slot:
                              s.copy(oT[0:128, c2 * 128:(c2 + 1) * 128],
                                     ps_t[slot][0:128, 0:128]))
                evA.append(ev)
            psbf_last[0] = evA[-1]
            o_rel.append(T.n)
            def y_mm(t, ti=ti):
                r = None
                for c2 in range(NI3):
                    t.matmul(ps_y1[:, 0:512], oT[:, c2 * 128:(c2 + 1) * 128],
                             wdT[:, c2 * C:c2 * C + 512],
                             start=(c2 == 0), stop=(c2 == NI3 - 1))
                    r = t.matmul(ps_y2[:, 0:512], oT[:, c2 * 128:(c2 + 1) * 128],
                                 wdT[:, c2 * C + 512:c2 * C + 1024],
                                 start=(c2 == 0), stop=(c2 == NI3 - 1))
                return r
            yw = [(A, evA[-1])]
            if ti > 0:
                yw.append((V, prev_yv))
            ymm = sch.step("tensor", yw, y_mm, inc=(T, 1))
            oT_T[0] = ymm
            y1w = [(T, ymm), (LDp[ti % 2], ldx)]
            if (ti - 2) in y_store:
                y1w.append((STp[ti % 2], y_store[ti - 2]))
            sch.step("vector", y1w, lambda v, ti=ti: v.tensor_tensor(
                yt[ti % 2][:, 0:512], xt[ti % 2][:, 0:512], ps_y1[:, 0:512], ALU.add))
            yv = sch.step("vector", [], lambda v, ti=ti: v.tensor_tensor(
                yt[ti % 2][:, 512:1024], xt[ti % 2][:, 512:1024],
                ps_y2[:, 0:512], ALU.add))
            prev_yv = yv
            x_rel.append(yv)
            stv = sch.step("sync", [(V, yv)], lambda s, ti=ti: s.dma_start(
                out=y_out[ti * 128:(ti + 1) * 128, :], in_=yt[ti % 2][:]),
                inc=(STp[ti % 2], 16))
            y_store[ti] = stv
        sch.step("gpsimd", [(ST, ST.n), (STB, STB.n)], lambda g: None)
        sch.emit(block)
    return nc



_L1FFN = {}
_STAGE_F2 = None


def run_stage_l1ffn(mode, x4_full, mu4, rstd4, m_parts, inputs):
    if mode not in _L1FFN:
        _L1FFN[mode] = build_stage_l1ffn(mode)
    wu = np.ascontiguousarray(np.asarray(inputs["L1_ffn_w_up"], np.float32))
    wg = np.ascontiguousarray(np.asarray(inputs["L1_ffn_w_gate"], np.float32))
    m_all = np.zeros((1, 64), np.float32)
    for j, mp in enumerate(m_parts):
        m_all[0, j * 8:j * 8 + len(mp)] = np.asarray(mp).reshape(-1)
    in_maps = []
    for c in range(NCORES):
        x4c = np.ascontiguousarray(x4_full[c * BPC:(c + 1) * BPC].reshape(TPC, C))
        in_maps.append(dict(x4=x4c, mu4c=np.ascontiguousarray(mu4[c]),
                            rstd4c=np.ascontiguousarray(rstd4[c]),
                            m_all=m_all, wu_nat=wu, wg_nat=wg))
    res = run_bass_kernel_spmd(_L1FFN[mode], in_maps, core_ids=list(range(NCORES)))
    if mode == "f1":
        o = np.stack([np.asarray(res.results[c]["o_out"]) for c in range(NCORES)])
        return o
    return np.stack([np.asarray(res.results[c]["m_out"]).reshape(-1)
                     for c in range(NCORES)])


def run_stage_f2(o_shards, x4_full, inputs):
    global _STAGE_F2
    if _STAGE_F2 is None:
        _STAGE_F2 = build_stage_f2()
    wd = np.ascontiguousarray(np.asarray(inputs["L1_ffn_w_down"], np.float32))
    in_maps = []
    for c in range(NCORES):
        x4c = np.ascontiguousarray(x4_full[c * BPC:(c + 1) * BPC].reshape(TPC, C))
        in_maps.append(dict(o_in=np.ascontiguousarray(o_shards[c]), x4=x4c, wd_nat=wd))
    res = run_bass_kernel_spmd(_STAGE_F2, in_maps, core_ids=list(range(NCORES)))
    y = np.stack([res.results[c]["y_out"].reshape(BPC, N, C)
                  for c in range(NCORES)]).reshape(B, N, C)
    return y


def kernel(**inputs):
    x2, probs = run_stage_a(inputs)
    x3 = run_stage_b1(x2, probs, inputs)
    x4, m0, mu4, rstd4 = run_stage_b2(x3, inputs)
    m12 = run_stage_l1ffn("c", x4, mu4, rstd4, [m0], inputs)
    m1 = m12[:, 0]
    m2 = m12[:, 1]
    m3 = run_stage_l1ffn("d", x4, mu4, rstd4, [m0, m1, m2], inputs)[:, 0]
    m4 = run_stage_l1ffn("e", x4, mu4, rstd4, [m0, m1, m2, m3], inputs)[:, 0]
    o = run_stage_l1ffn("f1", x4, mu4, rstd4, [m0, m1, m2, m3, m4], inputs)
    y = run_stage_f2(o, x4, inputs)
    return y


# version 03
# speedup vs baseline: 1.2310x; 1.2310x over previous
import numpy as np
import ml_dtypes
from contextlib import ExitStack

import concourse.bass as bass
import concourse.mybir as mybir
from concourse.bass_utils import run_bass_kernel_spmd

F32 = mybir.dt.float32
BF16 = mybir.dt.bfloat16
I32 = mybir.dt.int32
AF = mybir.ActivationFunctionType
ALU = mybir.AluOpType
AX = mybir.AxisListType

MAGIC = float(np.float32(1.5 * 2 ** 23))
TWO_PI_HI = float(np.float32(2.0 * np.pi))
TWO_PI_LO = float(np.float32(2.0 * np.pi - np.float64(np.float32(2.0 * np.pi))))
INV_2PI = float(np.float32(1.0 / (2.0 * np.pi)))
PIO2_HI = float(np.float32(0.5 * np.pi))
PIO2_LO = float(np.float32(0.5 * np.pi - np.float64(np.float32(0.5 * np.pi))))
PI_F = float(np.float32(np.pi))

NCORES = 8
B, N, C = 16, 2048, 1024
BPC = B // NCORES
TPC = BPC * N
NT = TPC // 128
NTS = N // 128
H, D = 16, 64
KCH = C // 128

BF = ml_dtypes.bfloat16


class Sem:
    def __init__(self, h):
        self.h = h
        self.n = 0


class Sched:

    ENGS = ("sync", "vector", "scalar", "tensor", "gpsimd", "gdma")
    AUTO = ("vector", "scalar", "gpsimd")

    def __init__(self, esems):
        self.ev = {e: [] for e in self.ENGS}
        self.esems = esems

    def step(self, eng, waits, fn, inc=None):
        waits = [(s, v) for (s, v) in waits if v > 0]
        ret = None
        if eng in self.AUTO:
            assert inc is None or inc[0] is self.esems[eng]
            es = self.esems[eng]
            if es.n > 0:
                waits.append((es, es.n))
            es.n += 1
            inc = (es, 1)
            ret = es.n
        elif inc is not None:
            sem, amt = inc
            sem.n += amt
            ret = sem.n

        def run(h, waits=tuple(waits), fn=fn, inc=inc):
            for s, v in waits:
                h.wait_ge(s.h, v)
            r = fn(h)
            if inc is not None:
                sem, amt = inc
                if r is None:
                    r = h.nop()
                r.then_inc(sem.h, amt)
        self.ev["gpsimd" if eng == "gdma" else eng].append(run)
        return ret

    def emit(self, block):
        merged = dict(self.ev)
        for eng in self.ENGS:
            if eng == "gdma":
                continue
            fns = self.ev[eng]
            if not fns:
                continue
            def make(fns):
                def body(h):
                    for f in fns:
                        f(h)
                return body
            getattr(block, eng)(make(fns))



def build_stage_a(phase=9, sub=9, ntiles=NTS):
    nc = bass.Bass("TRN2", target_bir_lowering=False)
    xA = nc.dram_tensor("xA", [TPC, C], F32, kind="ExternalInput").ap()
    wqT = nc.dram_tensor("wqT", [C, C], F32, kind="ExternalInput").ap()
    wkT = nc.dram_tensor("wkT", [C, C], F32, kind="ExternalInput").ap()
    wvT = nc.dram_tensor("wvT", [C, C], F32, kind="ExternalInput").ap()
    bqkv = nc.dram_tensor("bqkv", [3, C], F32, kind="ExternalInput").ap()
    gb = nc.dram_tensor("gb", [2, C], F32, kind="ExternalInput").ap()
    wg = nc.dram_tensor("wg", [C, 8], F32, kind="ExternalInput").ap()
    posT = nc.dram_tensor("posT", [1, N], F32, kind="ExternalInput").ap()
    jfrac = nc.dram_tensor("jfrac", [1, C // 2], F32, kind="ExternalInput").ap()
    x2_out = nc.dram_tensor("x2_out", [TPC, C], F32, kind="ExternalOutput").ap()
    probs_out = nc.dram_tensor("probs_out", [TPC, 8], F32, kind="ExternalOutput").ap()

    with ExitStack() as ctx:
        sbt = lambda n, s, d=F32: ctx.enter_context(nc.sbuf_tensor(n, s, d))
        pst = lambda n, s, d=F32: ctx.enter_context(nc.psum_tensor(n, s, d))

        wq_b = sbt("wq_b", [128, KCH * C], BF16)
        wk_b = sbt("wk_b", [128, KCH * C], BF16)
        wv_b = sbt("wv_b", [128, KCH * C], BF16)
        wg_b = sbt("wg_b", [128, KCH * 8], BF16)
        bq_b = sbt("bq_b", [1, C], BF16)
        bk_b = sbt("bk_b", [1, C], BF16)
        bv_b = sbt("bv_b", [1, C], BF16)
        bias_rows = [bq_b, bk_b, bv_b]
        gam_b = sbt("gam_b", [1, C], BF16)
        bet_b = sbt("bet_b", [1, C], BF16)
        gamB = sbt("gamB", [128, C], F32)
        betB = sbt("betB", [128, C], F32)
        ones_row = sbt("ones_row", [1, 128], BF16)
        ones_col = sbt("ones_col", [128, 1], BF16)
        ident = sbt("ident", [128, 128], BF16)
        cos_t = sbt("cos_t", [128, NTS * 512], BF16)
        sin_t = sbt("sin_t", [128, NTS * 512], BF16)
        posr = sbt("posr", [1, N], F32)
        jfr = sbt("jfr", [1, C // 2], F32)
        jl = sbt("jl", [1, C // 2], F32)
        invf_r = sbt("invf_r", [1, C // 2], F32)
        lnv = sbt("lnv", [1, 1], F32)
        tc1 = sbt("tc1", [128, 512], F32)
        tc2 = sbt("tc2", [128, 512], F32)

        mu1 = sbt("mu1", [128, NT], F32)
        rstd1 = sbt("rstd1", [128, NT], F32)
        sumc = sbt("sumc", [128, NT], F32)
        sqc = sbt("sqc", [128, NT], F32)
        sum2c = sbt("sum2c", [128, NT], F32)
        sq2c = sbt("sq2c", [128, NT], F32)
        mu2ln = sbt("mu2ln", [128, NT], F32)
        rstd2ln = sbt("rstd2ln", [128, NT], F32)

        xt = [sbt(f"xt{i}", [128, C]) for i in range(2)]
        xn = sbt("xn", [128, C])
        xn_b = sbt("xn_b", [128, C], BF16)
        xnT = sbt("xnT", [128, KCH * 128], BF16)
        kg = sbt("kg", [128, C])
        kg_b = sbt("kg_b", [128, C], BF16)
        kr_b = sbt("kr_b", [128, C], BF16)
        vg_b = sbt("vg_b", [128, C], BF16)
        qg = sbt("qg", [128, C])
        qg_b = sbt("qg_b", [128, C], BF16)
        qr_b = sbt("qr_b", [128, C], BF16)
        qrT = sbt("qrT", [64, H * 128], BF16)
        rp1 = sbt("rp1", [128, 512], BF16)
        rp2 = sbt("rp2", [128, 512], BF16)
        kv_s = sbt("kv_s", [64, H * 64], BF16)
        ksum_row = sbt("ksum_row", [1, C], F32)
        kmb_row = sbt("kmb_row", [1, C], BF16)
        kmeanB = sbt("kmeanB", [128, C], BF16)
        zd = sbt("zd", [128, C], BF16)
        zcol = sbt("zcol", [128, H], F32)
        zinv = sbt("zinv", [128, H], F32)
        res2 = sbt("res2", [128, C], F32)
        scr = sbt("scr", [128, C], F32)
        scr2 = sbt("scr2", [128, C], F32)
        rs_cols = sbt("rs_cols", [128, H], F32)
        colA = sbt("colA", [128, 1], F32)
        colB = sbt("colB", [128, 1], F32)
        colC = sbt("colC", [128, 1], F32)
        colD = sbt("colD", [128, 1], F32)
        x2t = [sbt(f"x2t{i}", [128, C]) for i in range(2)]
        probs_t = sbt("probs_t", [128, 8], F32)
        eps5_col = sbt("eps5_col", [128, 1], F32)
        eps12_col = sbt("eps12_col", [128, 1], F32)
        negmax = sbt("negmax", [128, 1], F32)
        smsum = sbt("smsum", [128, 1], F32)

        ps_t = [pst(f"ps_t{i}", [128, 128], BF16) for i in range(3)]
        ps_a = pst("ps_a", [128, 512])
        ps_b = pst("ps_b", [128, 512])
        ps_e = pst("ps_e", [128, 512])
        ps_f = pst("ps_f", [128, 512])
        ps_g = pst("ps_g", [128, 512])

        S = {nm: Sem(ctx.enter_context(nc.semaphore(nm)))
             for nm in ("LD", "LDB", "W", "V", "A", "T", "G", "ST", "STB", "STP")}
        (LD, LDB, W, V, A, T, G, ST, STB, STP) = (
            S[n] for n in ("LD", "LDB", "W", "V", "A", "T", "G", "ST", "STB", "STP"))
        LDp = [LD, LDB]
        STp = [ST, STB]
        CN = Sem(ctx.enter_context(nc.semaphore("CN")))
        block = ctx.enter_context(nc.Block())
        sch = Sched({"vector": V, "scalar": A, "gpsimd": G})

        for cch in range(KCH):
            for dst, src in ((wq_b, wqT), (wk_b, wkT), (wv_b, wvT)):
                sch.step("gdma", [], lambda g, dst=dst, src=src, cch=cch: g.dma_start(
                    out=dst[:, cch * C:(cch + 1) * C],
                    in_=src[cch * 128:(cch + 1) * 128, :]), inc=(W, 16))
            sch.step("gdma", [], lambda g, cch=cch: g.dma_start(
                out=wg_b[:, cch * 8:(cch + 1) * 8],
                in_=wg[cch * 128:(cch + 1) * 128, :]), inc=(W, 16))
        for i, row in enumerate(bias_rows):
            sch.step("gdma", [], lambda g, i=i, row=row: g.dma_start(
                out=row[:], in_=bqkv[i:i + 1, :]), inc=(W, 16))
        sch.step("gdma", [], lambda g: g.dma_start(out=gam_b[:], in_=gb[0:1, :]), inc=(W, 16))
        sch.step("gdma", [], lambda g: g.dma_start(out=bet_b[:], in_=gb[1:2, :]), inc=(W, 16))
        sch.step("sync", [], lambda s: s.dma_start(out=posr[:], in_=posT), inc=(CN, 16))
        sch.step("sync", [], lambda s: s.dma_start(out=jfr[:], in_=jfrac), inc=(CN, 16))
        w_all = W.n
        w_all_ld = CN.n

        sch.step("vector", [], lambda v: v.memset(ones_row[:], 1.0))
        sch.step("vector", [], lambda v: v.memset(ones_col[:], 1.0))
        sch.step("vector", [], lambda v: v.memset(eps5_col[:], 1e-5))
        sch.step("vector", [], lambda v: v.memset(eps12_col[:], 1e-12))
        consts_v = sch.step("vector", [], lambda v: v.memset(lnv[:], 10000.0))
        id_ms = sch.step("gpsimd", [], lambda g: g.memset(ident[:], 1.0))
        id_done = sch.step("gpsimd", [(G, id_ms)], lambda g: g.affine_select(
            ident[:], ident[:], pattern=[[-1, 128]], compare_op=ALU.is_equal,
            fill=0.0, base=0, channel_multiplier=1))

        for half in range(2):
            for row, dstps in ((gam_b, ps_a), (bet_b, ps_b)):
                tmm = sch.step("tensor", [(W, w_all), (CN, w_all_ld), (V, consts_v)],
                               lambda t, row=row, half=half, dstps=dstps:
                               t.matmul(dstps[:, 0:512], ones_row[:],
                                        row[0:1, half * 512:(half + 1) * 512],
                                        start=True, stop=True), inc=(T, 1))
            sch.step("scalar", [(T, T.n)], lambda s, half=half: s.copy(
                gamB[:, half * 512:(half + 1) * 512], ps_a[:, 0:512]))
            sch.step("scalar", [], lambda s, half=half: s.copy(
                betB[:, half * 512:(half + 1) * 512], ps_b[:, 0:512]))
            if half == 0:
                ev = A.n
                sch.step("tensor", [(A, ev)], lambda t: None)

        gb_bcast_done = A.n

        lnv_done = sch.step("scalar", [(V, 1)], lambda s: s.activation(
            lnv[:], lnv[:], AF.Ln))
        jl_done = sch.step("vector", [(W, w_all), (CN, w_all_ld), (A, lnv_done)], lambda v: v.tensor_scalar(
            jl[:], jfr[:], lnv[0:1, 0:1], None, ALU.mult))
        invf_done = sch.step("scalar", [(V, jl_done)], lambda s: s.activation(
            invf_r[:], jl[:], AF.Exp, bias=0.0, scale=-1.0))
        pse_free = 0
        prev_s_done = 0
        prev_c_done = 0
        tc3 = sbt("tc3", [128, 512], F32)
        for chunk in range(NTS):
            mm = sch.step("tensor", [(A, invf_done), (V, pse_free)],
                          lambda t, chunk=chunk: t.matmul(
                              ps_e[:, 0:512],
                              posr[0:1, chunk * 128:(chunk + 1) * 128],
                              invf_r[0:1, :], start=True, stop=True), inc=(T, 1))
            sch.step("vector", [(T, mm)], lambda v: v.tensor_scalar(
                tc1[:], ps_e[:, 0:512], INV_2PI, None, ALU.mult))
            sch.step("vector", [], lambda v: v.tensor_scalar(
                tc1[:], tc1[:], MAGIC, MAGIC, ALU.add, ALU.subtract))
            sch.step("vector", [(A, prev_s_done)], lambda v: v.scalar_tensor_tensor(
                tc2[:], tc1[:], -TWO_PI_HI, ps_e[:, 0:512], ALU.mult, ALU.add))
            sch.step("vector", [], lambda v: v.scalar_tensor_tensor(
                tc2[:], tc1[:], -TWO_PI_LO, tc2[:], ALU.mult, ALU.add))
            sch.step("vector", [], lambda v: v.tensor_single_scalar(
                tc1[:], tc2[:], PI_F, ALU.is_gt))
            sch.step("vector", [], lambda v: v.scalar_tensor_tensor(
                tc2[:], tc1[:], -TWO_PI_HI, tc2[:], ALU.mult, ALU.add))
            sch.step("vector", [], lambda v: v.tensor_single_scalar(
                tc1[:], tc2[:], -PI_F, ALU.is_lt))
            sin_in = sch.step("vector", [], lambda v: v.scalar_tensor_tensor(
                tc2[:], tc1[:], TWO_PI_HI, tc2[:], ALU.mult, ALU.add))
            s_done = sch.step("scalar", [(V, sin_in)], lambda s, chunk=chunk: s.activation(
                sin_t[:, chunk * 512:(chunk + 1) * 512], tc2[:], AF.Sin))
            prev_s_done = s_done
            pse_free = sch.step("vector", [(A, prev_c_done)], lambda v: v.tensor_scalar(
                tc3[:], ps_e[:, 0:512], PIO2_HI, None, ALU.add))
            sch.step("vector", [], lambda v: v.tensor_scalar(
                tc1[:], tc3[:], INV_2PI, None, ALU.mult))
            sch.step("vector", [], lambda v: v.tensor_scalar(
                tc1[:], tc1[:], MAGIC, MAGIC, ALU.add, ALU.subtract))
            sch.step("vector", [], lambda v: v.scalar_tensor_tensor(
                tc3[:], tc1[:], -TWO_PI_HI, tc3[:], ALU.mult, ALU.add))
            sch.step("vector", [], lambda v: v.scalar_tensor_tensor(
                tc3[:], tc1[:], -TWO_PI_LO, tc3[:], ALU.mult, ALU.add))
            sch.step("vector", [], lambda v: v.tensor_scalar(
                tc3[:], tc3[:], PIO2_LO, None, ALU.add))
            sch.step("vector", [], lambda v: v.tensor_single_scalar(
                tc1[:], tc3[:], PI_F, ALU.is_gt))
            sch.step("vector", [], lambda v: v.scalar_tensor_tensor(
                tc3[:], tc1[:], -TWO_PI_HI, tc3[:], ALU.mult, ALU.add))
            sch.step("vector", [], lambda v: v.tensor_single_scalar(
                tc1[:], tc3[:], -PI_F, ALU.is_lt))
            cos_in = sch.step("vector", [(A, s_done)], lambda v: v.scalar_tensor_tensor(
                tc3[:], tc1[:], TWO_PI_HI, tc3[:], ALU.mult, ALU.add))
            c_done = sch.step("scalar", [(V, cos_in)], lambda s, chunk=chunk: s.activation(
                cos_t[:, chunk * 512:(chunk + 1) * 512], tc3[:], AF.Sin))
            prev_c_done = c_done
        tables_done_a = A.n
        tables_done_v = V.n

        x_rel = []
        for ti in range(NT):
            w = [(V, x_rel[ti - 2])] if ti >= 2 else []
            ld = sch.step("sync", w, lambda s, ti=ti: s.dma_start(
                out=xt[ti % 2][:], in_=xA[ti * 128:(ti + 1) * 128, :]), inc=(LDp[ti % 2], 16))
            sch.step("vector", [(LDp[ti % 2], ld)], lambda v, ti=ti: v.tensor_reduce(
                sumc[:, ti:ti + 1], xt[ti % 2][:], AX.X, ALU.add))
            rel = sch.step("vector", [], lambda v, ti=ti: v.scalar_tensor_tensor(
                scr[:], xt[ti % 2][:], 0.0, xt[ti % 2][:], ALU.add, ALU.mult,
                accum_out=sqc[:, ti:ti + 1]))
            x_rel.append(rel)
        sch.step("vector", [], lambda v: v.tensor_scalar(
            mu1[:], sumc[:], 1.0 / C, None, ALU.mult))
        sch.step("vector", [], lambda v: v.tensor_scalar(
            sqc[:], sqc[:], 1.0 / C, None, ALU.mult))
        sch.step("vector", [], lambda v: v.tensor_tensor(
            scr[:, 0:NT], mu1[:], mu1[:], ALU.mult))
        var_done = sch.step("vector", [], lambda v: v.tensor_tensor(
            sqc[:], sqc[:], scr[:, 0:NT], ALU.subtract))
        std_done = sch.step("scalar", [(V, var_done)], lambda s: s.activation(
            rstd1[:], sqc[:], AF.Sqrt, bias=eps5_col[:, 0:1], scale=1.0))
        stats1_done = sch.step("vector", [(A, std_done)], lambda v: v.reciprocal(
            rstd1[:], rstd1[:]))

        psbf_last = [0]

        def transposes(src_b, dst, nchunk, width, wait_v, base_tag):
            evA = []
            for c2 in range(nchunk):
                slot = c2 % 3
                waits = [(V, wait_v)] if c2 == 0 else []
                if c2 >= 3:
                    waits.append((A, evA[c2 - 3]))
                if c2 == 0:
                    waits.append((G, id_done))
                    waits.append((A, psbf_last[0]))
                sch.step("tensor", waits, lambda t, c2=c2, slot=slot: t.transpose(
                    ps_t[slot][0:width, 0:128],
                    src_b[:, c2 * width:(c2 + 1) * width], ident[:]), inc=(T, 1))
                ev = sch.step("scalar", [(T, T.n)], lambda s, c2=c2, slot=slot: s.copy(
                    dst[0:width, c2 * 128:(c2 + 1) * 128],
                    ps_t[slot][0:width, 0:128]))
                evA.append(ev)
            psbf_last[0] = evA[-1]
            return evA[-1]

        def proj(dst_ps, wmat, bias_row, xnT_done):
            last = None
            for half in range(2):
                ps = dst_ps[half]
                for c2 in range(KCH):
                    sch.step("tensor", [(A, xnT_done)] if c2 == 0 and half == 0 else [],
                             lambda t, ps=ps, c2=c2, half=half: t.matmul(
                                 ps[:, 0:512],
                                 xnT[:, c2 * 128:(c2 + 1) * 128],
                                 wmat[:, c2 * C + half * 512: c2 * C + half * 512 + 512],
                                 start=(c2 == 0), stop=False))
                brow = bias_rows[bias_row]
                last = sch.step("tensor", [], lambda t, ps=ps, half=half, brow=brow: t.matmul(
                    ps[:, 0:512], ones_row[:],
                    brow[0:1, half * 512:(half + 1) * 512],
                    start=False, stop=True), inc=(T, 1))
            return last

        def rope(src_b, dst_b, tile_in_seq, wait_v):
            co = cos_t[:, tile_in_seq * 512:(tile_in_seq + 1) * 512]
            si = sin_t[:, tile_in_seq * 512:(tile_in_seq + 1) * 512]
            x1 = src_b[:, 0:C:2]
            x2 = src_b[:, 1:C:2]
            o1 = dst_b[:, 0:C:2]
            o2 = dst_b[:, 1:C:2]
            sch.step("vector", [(V, wait_v)], lambda v: v.tensor_tensor(rp1[:], x1, co, ALU.mult))
            sch.step("vector", [], lambda v: v.tensor_tensor(rp2[:], x2, si, ALU.mult))
            sch.step("vector", [], lambda v: v.tensor_tensor(o1, rp1[:], rp2[:], ALU.subtract))
            sch.step("vector", [], lambda v: v.tensor_tensor(rp1[:], x2, co, ALU.mult))
            sch.step("vector", [], lambda v: v.tensor_tensor(rp2[:], x1, si, ALU.mult))
            return sch.step("vector", [], lambda v: v.tensor_tensor(
                o2, rp1[:], rp2[:], ALU.add))

        x2_store = {}
        for s_ix in range(BPC if phase >= 3 else (1 if phase >= 1 else 0)):
            ksum_reset = sch.step("vector", [], lambda v: v.memset(ksum_row[:], 0.0))
            for t_ix in range(min(NTS, ntiles)):
                ti = s_ix * NTS + t_ix
                w = [(V, x_rel[-2])] if len(x_rel) >= 2 else []
                ld = sch.step("sync", w, lambda s, ti=ti: s.dma_start(
                    out=xt[ti % 2][:], in_=xA[ti * 128:(ti + 1) * 128, :]), inc=(LDp[ti % 2], 16))
                nrm = sch.step("vector", [(LDp[ti % 2], ld), (V, stats1_done)],
                               lambda v, ti=ti: v.tensor_scalar(
                                   xn_b[:], xt[ti % 2][:], mu1[:, ti:ti + 1],
                                   rstd1[:, ti:ti + 1], ALU.subtract, ALU.mult))
                x_rel.append(nrm)
                if sub < 1:
                    continue
                xnT_done = transposes(xn_b, xnT, KCH, 128, nrm, f"p1t{ti}")
                if sub < 2:
                    continue
                kproj = proj((ps_a, ps_b), wk_b, 1, xnT_done)
                kg_done = sch.step("scalar", [(T, kproj)], lambda s: s.activation(
                    kg[:, 0:512], ps_a[:, 0:512], AF.Gelu))
                kg_done = sch.step("scalar", [], lambda s: s.activation(
                    kg[:, 512:1024], ps_b[:, 0:512], AF.Gelu))
                vproj = proj((ps_a, ps_b), wv_b, 2, kg_done)
                vg_done = sch.step("scalar", [(T, vproj)], lambda s: s.copy(
                    vg_b[:, 0:512], ps_a[:, 0:512]))
                vg_done = sch.step("scalar", [], lambda s: s.copy(
                    vg_b[:, 512:1024], ps_b[:, 0:512]))
                kgb_done = sch.step("vector", [(A, kg_done)], lambda v: v.tensor_scalar(
                    kg_b[:], kg[:], 0.21, None, ALU.add))
                if sub < 3:
                    continue
                kr_done = rope(kg_b, kr_b, t_ix, kgb_done)
                if sub < 4:
                    continue
                def kv_mm(t, t_ix=t_ix):
                    r = None
                    for h in range(H):
                        psd = ps_e if h < 8 else ps_f
                        r = t.matmul(psd[0:64, (h % 8) * 64:(h % 8) * 64 + 64],
                                     kr_b[:, h * D:(h + 1) * D],
                                     vg_b[:, h * D:(h + 1) * D],
                                     start=(t_ix == 0 and h % 8 == 0),
                                     stop=(t_ix == NTS - 1 and h % 8 == 7))
                    return r
                kvw = [(V, kr_done), (A, vg_done)]
                if t_ix == 0 and s_ix > 0:
                    kvw.append((A, kv_evac_prev))
                    kvw.append((V, ksum_reset))
                kv_done = sch.step("tensor", kvw, kv_mm, inc=(T, 1))
                if sub < 5:
                    continue
                ks1 = sch.step("tensor", [(V, kgb_done), (T, kv_done)], lambda t: t.matmul(
                    ps_g[0:1, 0:512], ones_col[:], kg_b[:, 0:512],
                    start=True, stop=True), inc=(T, 1))
                ksa = sch.step("vector", [(T, ks1)], lambda v: v.scalar_tensor_tensor(
                    ksum_row[0:1, 0:512], ps_g[0:1, 0:512], 0.0,
                    ksum_row[0:1, 0:512], ALU.add, ALU.add))
                ks2 = sch.step("tensor", [(V, ksa)], lambda t: t.matmul(
                    ps_g[0:1, 0:512], ones_col[:], kg_b[:, 512:1024],
                    start=True, stop=True), inc=(T, 1))
                ksb = sch.step("vector", [(T, ks2)], lambda v: v.scalar_tensor_tensor(
                    ksum_row[0:1, 512:1024], ps_g[0:1, 0:512], 0.0,
                    ksum_row[0:1, 512:1024], ALU.add, ALU.add))
            if sub < 5 or ntiles < NTS:
                break
            kv_ev = sch.step("scalar", [(T, kv_done)], lambda s: s.activation(
                kv_s[0:64, 0:512], ps_e[0:64, 0:512],
                AF.Copy, bias=0.0, scale=1.0 / N))
            kv_ev = sch.step("scalar", [], lambda s: s.activation(
                kv_s[0:64, 512:1024], ps_f[0:64, 0:512],
                AF.Copy, bias=0.0, scale=1.0 / N))
            km_ev = kv_ev
            kmb_done = sch.step("vector", [(A, km_ev), (V, ksb)], lambda v: v.tensor_scalar(
                kmb_row[:], ksum_row[:], 1.0 / N, None, ALU.mult))
            for half in range(2):
                psd = (ps_a, ps_b)[half]
                sch.step("tensor", [(V, kmb_done)], lambda t, half=half, psd=psd: t.matmul(
                    psd[:, 0:512], ones_row[:],
                    kmb_row[0:1, half * 512:(half + 1) * 512],
                    start=True, stop=True), inc=(T, 1))
            kmB_done = sch.step("scalar", [(T, T.n)], lambda s: (
                s.copy(kmeanB[:, 0:512], ps_a[:, 0:512]),
                s.copy(kmeanB[:, 512:1024], ps_b[:, 0:512]))[-1])
            kv_evac_prev = kmB_done

            for t_ix in range(NTS if phase >= 2 else 0):
                ti = s_ix * NTS + t_ix
                w = [(V, x_rel[-2])] if len(x_rel) >= 2 else []
                ld = sch.step("sync", w, lambda s, ti=ti: s.dma_start(
                    out=xt[ti % 2][:], in_=xA[ti * 128:(ti + 1) * 128, :]), inc=(LDp[ti % 2], 16))
                nrm = sch.step("vector", [(LDp[ti % 2], ld)], lambda v, ti=ti: v.tensor_scalar(
                    xn[:], xt[ti % 2][:], mu1[:, ti:ti + 1], rstd1[:, ti:ti + 1],
                    ALU.subtract, ALU.mult))
                nrm = sch.step("vector", [], lambda v: v.tensor_copy(xn_b[:], xn[:]),
                               inc=(V, 1))
                xnT_done = transposes(xn_b, xnT, KCH, 128, nrm, f"p2t{ti}")
                qproj = proj((ps_a, ps_b), wq_b, 0, xnT_done)
                qg_done = sch.step("scalar", [(T, qproj)], lambda s: s.activation(
                    qg[:, 0:512], ps_a[:, 0:512], AF.Gelu))
                qg_done = sch.step("scalar", [], lambda s: s.activation(
                    qg[:, 512:1024], ps_b[:, 0:512], AF.Gelu))
                qgb_done = sch.step("vector", [(A, qg_done)], lambda v: v.tensor_scalar(
                    qg_b[:], qg[:], 0.21, None, ALU.add))
                sch.step("vector", [(A, kmB_done)], lambda v: v.tensor_tensor(
                    zd[:], qg_b[:], kmeanB[:], ALU.mult))
                sch.step("vector", [], lambda v: v.tensor_reduce(
                    zcol[:], zd[:].rearrange("p (h d) -> p h d", d=D), AX.X, ALU.add))
                sch.step("vector", [], lambda v: v.tensor_scalar(
                    zinv[:], zcol[:], 1e-6, None, ALU.add))
                z_done = sch.step("vector", [], lambda v: v.reciprocal(
                    zinv[:], zinv[:]))
                qr_done = rope(qg_b, qr_b, t_ix, qgb_done)
                evA = []
                for h in range(H):
                    slot = h % 3
                    waits = [(V, qr_done)] if h == 0 else []
                    if h >= 3:
                        waits.append((A, evA[h - 3]))
                    if h == 0:
                        waits.append((A, psbf_last[0]))
                    sch.step("tensor", waits, lambda t, h=h, slot=slot: t.transpose(
                        ps_t[slot][0:64, 0:128],
                        qr_b[:, h * D:(h + 1) * D], ident[:]), inc=(T, 1))
                    ev = sch.step("scalar", [(T, T.n)], lambda s, h=h, slot=slot: s.copy(
                        qrT[0:64, h * 128:(h + 1) * 128],
                        ps_t[slot][0:64, 0:128]))
                    evA.append(ev)
                qrT_done = evA[-1]
                psbf_last[0] = qrT_done
                def out_mm(t):
                    r = None
                    for h in range(H):
                        psd = ps_e if h < 8 else ps_f
                        off = (h % 8) * 64
                        r = t.matmul(psd[:, off:off + 64],
                                     qrT[0:64, h * 128:(h + 1) * 128],
                                     kv_s[0:64, h * 64:(h + 1) * 64],
                                     start=(h % 8 == 0), stop=(h % 8 == 7))
                    return r
                omm = sch.step("tensor", [(A, qrT_done)], out_mm, inc=(T, 1))
                def zres(v):
                    r = None
                    for h in range(H):
                        psd = ps_e if h < 8 else ps_f
                        off = (h % 8) * 64
                        r = v.scalar_tensor_tensor(
                            res2[:, h * D:(h + 1) * D], psd[:, off:off + 64],
                            zinv[:, h:h + 1], xn[:, h * D:(h + 1) * D],
                            ALU.mult, ALU.add, accum_out=rs_cols[:, h:h + 1])
                    return r
                res_done = sch.step("vector", [(T, omm), (V, z_done)], zres)
                sch.step("vector", [], lambda v: v.tensor_reduce(
                    colA[:], rs_cols[:], AX.X, ALU.add))
                sq_done = sch.step("vector", [], lambda v: v.scalar_tensor_tensor(
                    scr[:], res2[:], 0.0, res2[:], ALU.add, ALU.mult,
                    accum_out=colB[:]))
                sch.step("vector", [], lambda v: v.tensor_scalar(
                    colA[:], colA[:], 1.0 / C, None, ALU.mult))
                sch.step("vector", [], lambda v: v.tensor_scalar(
                    colB[:], colB[:], 1.0 / C, None, ALU.mult))
                sch.step("vector", [], lambda v: v.tensor_tensor(
                    colC[:], colA[:], colA[:], ALU.mult))
                var2_done = sch.step("vector", [], lambda v: v.tensor_tensor(
                    colB[:], colB[:], colC[:], ALU.subtract))
                sd2 = sch.step("scalar", [(V, var2_done)], lambda s: s.activation(
                    colD[:], colB[:], AF.Sqrt, bias=eps12_col[:, 0:1], scale=1.0))
                rs2 = sch.step("vector", [(A, sd2)], lambda v: v.reciprocal(
                    colD[:], colD[:]))
                sch.step("vector", [], lambda v: v.tensor_scalar(
                    scr2[:], res2[:], colA[:, 0:1], colD[:, 0:1], ALU.subtract, ALU.mult))
                sch.step("vector", [(A, gb_bcast_done)], lambda v: v.tensor_tensor(
                    scr2[:], scr2[:], gamB[:], ALU.mult))
                sch.step("vector", [], lambda v: v.tensor_tensor(
                    scr2[:], scr2[:], betB[:], ALU.add))
                x2w = [(STp[ti % 2], x2_store[ti - 2])] if (ti - 2) in x2_store else []
                x2_done = sch.step("vector", x2w, lambda v, ti=ti: v.tensor_tensor(
                    x2t[ti % 2][:], scr2[:], xt[ti % 2][:], ALU.add))
                x_rel.append(x2_done)
                sch.step("vector", [], lambda v, ti=ti: v.tensor_reduce(
                    sum2c[:, ti:ti + 1], x2t[ti % 2][:], AX.X, ALU.add))
                st2 = sch.step("vector", [], lambda v, ti=ti: v.scalar_tensor_tensor(
                    scr[:], x2t[ti % 2][:], 0.0, x2t[ti % 2][:], ALU.add, ALU.mult,
                    accum_out=sq2c[:, ti:ti + 1]))
                stv = sch.step("sync", [(V, x2_done)], lambda s, ti=ti: s.dma_start(
                    out=x2_out[ti * 128:(ti + 1) * 128, :], in_=x2t[ti % 2][:]),
                    inc=(STp[ti % 2], 16))
                x2_store[ti] = stv

        sch.step("vector", [], lambda v: v.tensor_scalar(
            mu2ln[:], sum2c[:], 1.0 / C, None, ALU.mult))
        sch.step("vector", [], lambda v: v.tensor_scalar(
            sq2c[:], sq2c[:], 1.0 / C, None, ALU.mult))
        sch.step("vector", [], lambda v: v.tensor_tensor(
            scr[:, 0:NT], mu2ln[:], mu2ln[:], ALU.mult))
        v2_done = sch.step("vector", [], lambda v: v.tensor_tensor(
            sq2c[:], sq2c[:], scr[:, 0:NT], ALU.subtract))
        sd_ln2 = sch.step("scalar", [(V, v2_done)], lambda s: s.activation(
            rstd2ln[:], sq2c[:], AF.Sqrt, bias=eps5_col[:, 0:1], scale=1.0))
        stats2_done = sch.step("vector", [(A, sd_ln2)], lambda v: v.reciprocal(
            rstd2ln[:], rstd2ln[:]))

        for ti in range(NT if phase >= 4 else 0):
            w = [(V, x_rel[-2])] if len(x_rel) >= 2 else []
            w = w + [(ST, ST.n), (STB, STB.n)]
            ld = sch.step("sync", w, lambda s, ti=ti: s.dma_start(
                out=xt[ti % 2][:], in_=x2_out[ti * 128:(ti + 1) * 128, :]), inc=(LDp[ti % 2], 16))
            nrm = sch.step("vector", [(LDp[ti % 2], ld), (V, stats2_done)],
                           lambda v, ti=ti: v.tensor_scalar(
                               xn_b[:], xt[ti % 2][:], mu2ln[:, ti:ti + 1],
                               rstd2ln[:, ti:ti + 1], ALU.subtract, ALU.mult))
            x_rel.append(nrm)
            xnT_done = transposes(xn_b, xnT, KCH, 128, nrm, f"rt{ti}")
            def lg_mm(t):
                r = None
                for c2 in range(KCH):
                    r = t.matmul(ps_a[:, 0:8], xnT[:, c2 * 128:(c2 + 1) * 128],
                                 wg_b[:, c2 * 8:(c2 + 1) * 8],
                                 start=(c2 == 0), stop=(c2 == KCH - 1))
                return r
            lg = sch.step("tensor", [(A, xnT_done)], lg_mm, inc=(T, 1))
            sch.step("vector", [(T, lg)], lambda v: v.tensor_reduce(
                negmax[:], ps_a[:, 0:8], AX.X, ALU.max))
            ngm = sch.step("vector", [], lambda v: v.tensor_scalar(
                negmax[:], negmax[:], -1.0, None, ALU.mult))
            exw = [(V, ngm)]
            if ti > 0:
                exw.append((STP, pr_store))
            ex = sch.step("scalar", exw, lambda s: s.activation(
                probs_t[:], ps_a[:, 0:8], AF.Exp, bias=negmax[:, 0:1], scale=1.0,
                accum_out=smsum[:, 0:1]))
            sch.step("vector", [(A, ex)], lambda v: v.reciprocal(smsum[:], smsum[:]))
            pr = sch.step("vector", [], lambda v: v.tensor_scalar(
                probs_t[:], probs_t[:], smsum[:, 0:1], None, ALU.mult))
            pr_store = sch.step("sync", [(V, pr)], lambda s, ti=ti: s.dma_start(
                out=probs_out[ti * 128:(ti + 1) * 128, :], in_=probs_t[:]), inc=(STP, 16))

        sch.step("gpsimd", [(ST, ST.n), (STB, STB.n), (STP, STP.n)],
                 lambda g: None)

        sch.emit(block)
    return nc



def build_stage_b1():
    nc = bass.Bass("TRN2", target_bir_lowering=False)
    x2 = nc.dram_tensor("x2", [TPC, C], F32, kind="ExternalInput").ap()
    probs_all = nc.dram_tensor("probs_all", [B * N, 8], F32, kind="ExternalInput").ap()
    probs_loc = nc.dram_tensor("probs_loc", [TPC, 8], F32, kind="ExternalInput").ap()
    w1_in = nc.dram_tensor("w1_in", [8, C, 2048], F32, kind="ExternalInput").ap()
    w2_in = nc.dram_tensor("w2_in", [8, 2048, C], F32, kind="ExternalInput").ap()
    x3_out = nc.dram_tensor("x3_out", [TPC, C], F32, kind="ExternalOutput").ap()

    NHALF = 2
    TPH = NT // NHALF
    NFC = 16
    CAP = float(B * N // 8)

    with ExitStack() as ctx:
        sbt = lambda n, s, d=F32: ctx.enter_context(nc.sbuf_tensor(n, s, d))
        pst = lambda n, s, d=F32: ctx.enter_context(nc.psum_tensor(n, s, d))

        probs_il = sbt("probs_il", [128, 2048])
        ones_rowf = sbt("ones_rowf", [1, 128], F32)
        ones_colf = sbt("ones_colf", [128, 1], F32)
        ident = sbt("ident", [128, 128], BF16)
        lo_r = sbt("lo_r", [1, 8])
        hi_r = sbt("hi_r", [1, 8])
        mid_r = sbt("mid_r", [1, 8])
        cnt_r = sbt("cnt_r", [1, 8])
        cge_r = sbt("cge_r", [1, 8])
        dd_r = sbt("dd_r", [1, 8])
        ge_t = sbt("ge_t", [128, 2048], BF16)
        colr8 = sbt("colr8", [128, 8])
        thrB = sbt("thrB", [128, 8])
        gates_all = sbt("gates_all", [128, NT * 8])
        p_t = sbt("p_t", [128, 8])
        g_m1 = sbt("g_m1", [128, 8])
        g_msk = sbt("g_msk", [128, 8])
        g_t1 = sbt("g_t1", [128, 8])
        g_u = sbt("g_u", [128, 8])
        g_eq = sbt("g_eq", [128, 8])
        g_ind = sbt("g_ind", [128, 8])
        colA = sbt("colA", [128, 1])
        colP = sbt("colP", [128, 1])
        colW = sbt("colW", [128, 1])
        cola = sbt("cola", [128, 1])
        colna = sbt("colna", [128, 1])
        colpref = sbt("colpref", [128, 1])
        colt = sbt("colt", [128, 1])
        mu2 = sbt("mu2", [128, NT])
        rstd2 = sbt("rstd2", [128, NT])
        sumc = sbt("sumc", [128, NT])
        sqc = sbt("sqc", [128, NT])
        scr_nt = sbt("scr_nt", [128, NT])
        eps5_col = sbt("eps5_col", [128, 1])
        xt = [sbt(f"xt{i}", [128, C]) for i in range(2)]
        x3t = [sbt(f"x3t{i}", [128, C]) for i in range(2)]
        xn_b = sbt("xn_b", [128, C], BF16)
        scr = sbt("scr", [128, C])
        xnT_half = sbt("xnT_half", [128, KCH * (TPH * 128)], BF16)
        out_acc = sbt("out_acc", [128, TPH * C])
        w1_b = sbt("w1_b", [128, KCH * 2048], BF16)
        w2_b = sbt("w2_b", [128, NFC * C], BF16)
        a_fm = sbt("a_fm", [128, NFC * 256], BF16)

        ps_t = [pst(f"ps_t{i}", [128, 128], BF16) for i in range(3)]
        ps_q = pst("ps_q", [128, 512])
        ps_s = pst("ps_s", [128, 512])
        ps_h1 = pst("ps_h1", [128, 512])
        ps_h2 = pst("ps_h2", [128, 512])
        ps_g = pst("ps_g", [128, 512])

        S = {nm: Sem(ctx.enter_context(nc.semaphore(nm)))
             for nm in ("LD", "LDB", "W", "WA", "WB", "WP", "V", "A", "T", "G", "ST", "STB")}
        (LD, LDB, W, WA, WB, WP, V, A, T, G, ST, STB) = (
            S[n] for n in ("LD", "LDB", "W", "WA", "WB", "WP", "V", "A", "T", "G", "ST", "STB"))
        LDp = [LD, LDB]
        STp = [ST, STB]
        block = ctx.enter_context(nc.Block())
        sch = Sched({"vector": V, "scalar": A, "gpsimd": G})

        sch.step("sync", [], lambda s: s.dma_start(
            out=probs_il[:], in_=probs_all.rearrange("(p r) e -> p r e", p=128)),
            inc=(W, 16))
        w_all = W.n
        consts_v = sch.step("vector", [], lambda v: (
            v.memset(ones_rowf[:], 1.0), v.memset(ones_colf[:], 1.0),
            v.memset(eps5_col[:], 1e-5), v.memset(colpref[:], 0.0),
            v.memset(lo_r[:], 0.0), v.memset(hi_r[:], 2.0))[-1])
        id_ms = sch.step("gpsimd", [], lambda g: g.memset(ident[:], 1.0))
        id_done = sch.step("gpsimd", [(G, id_ms)], lambda g: g.affine_select(
            ident[:], ident[:], pattern=[[-1, 128]], compare_op=ALU.is_equal,
            fill=0.0, base=0, channel_multiplier=1))

        x_rel = []
        for ti in range(NT):
            w = [(V, x_rel[-2])] if len(x_rel) >= 2 else []
            ld = sch.step("sync", w, lambda s, ti=ti: s.dma_start(
                out=xt[ti % 2][:], in_=x2[ti * 128:(ti + 1) * 128, :]),
                inc=(LDp[ti % 2], 16))
            sch.step("vector", [(LDp[ti % 2], ld)], lambda v, ti=ti: v.tensor_reduce(
                sumc[:, ti:ti + 1], xt[ti % 2][:], AX.X, ALU.add))
            rel = sch.step("vector", [], lambda v, ti=ti: v.scalar_tensor_tensor(
                scr[:], xt[ti % 2][:], 0.0, xt[ti % 2][:], ALU.add, ALU.mult,
                accum_out=sqc[:, ti:ti + 1]))
            x_rel.append(rel)
        sch.step("vector", [], lambda v: v.tensor_scalar(
            mu2[:], sumc[:], 1.0 / C, None, ALU.mult))
        sch.step("vector", [], lambda v: v.tensor_scalar(
            sqc[:], sqc[:], 1.0 / C, None, ALU.mult))
        sch.step("vector", [], lambda v: v.tensor_tensor(
            scr_nt[:], mu2[:], mu2[:], ALU.mult))
        var_done = sch.step("vector", [], lambda v: v.tensor_tensor(
            sqc[:], sqc[:], scr_nt[:], ALU.subtract))
        std_done = sch.step("scalar", [(V, var_done)], lambda s: s.activation(
            rstd2[:], sqc[:], AF.Sqrt, bias=eps5_col[:, 0:1], scale=1.0))
        stats_done = sch.step("vector", [(A, std_done)], lambda v: v.reciprocal(
            rstd2[:], rstd2[:]))

        prev_ge_v = 0
        for it in range(40):
            w0 = [(W, w_all)] if it == 0 else []
            sch.step("vector", w0, lambda v: v.tensor_tensor(
                mid_r[:], lo_r[:], hi_r[:], ALU.add))
            mid_done = sch.step("vector", [], lambda v: v.tensor_scalar(
                mid_r[:], mid_r[:], 0.5, None, ALU.mult))
            mid_rep = bass.AP(mid_r[:].tensor, 0, [[8, 1], [0, 64], [1, 8]])
            for blk in range(4):
                mm = sch.step("tensor", [(V, mid_done)], lambda t: t.matmul(
                    ps_g[:, 0:512], ones_rowf[:], mid_rep,
                    start=True, stop=True), inc=(T, 1))
                geb = sch.step("vector", [(T, mm)], lambda v, blk=blk: v.tensor_tensor(
                    ge_t[:, blk * 512:(blk + 1) * 512],
                    probs_il[:, blk * 512:(blk + 1) * 512],
                    ps_g[:, 0:512], ALU.is_ge))
                if blk < 3:
                    sch.step("tensor", [(V, geb)], lambda t: None)
            red = sch.step("vector", [], lambda v: v.tensor_reduce(
                colr8[:], ge_t[:].rearrange("p (r e) -> p e r", e=8), AX.X, ALU.add))
            cm = sch.step("tensor", [(V, red)], lambda t: t.matmul(
                ps_g[0:1, 0:8], ones_colf[:], colr8[:], start=True, stop=True),
                inc=(T, 1))
            sch.step("vector", [(T, cm)], lambda v: v.tensor_copy(
                cnt_r[:], ps_g[0:1, 0:8]))
            sch.step("vector", [], lambda v: v.tensor_single_scalar(
                cge_r[:], cnt_r[:], CAP, ALU.is_ge))
            sch.step("vector", [], lambda v: v.tensor_tensor(
                dd_r[:], mid_r[:], lo_r[:], ALU.subtract))
            sch.step("vector", [], lambda v: v.tensor_tensor(
                dd_r[:], dd_r[:], cge_r[:], ALU.mult))
            sch.step("vector", [], lambda v: v.tensor_tensor(
                lo_r[:], lo_r[:], dd_r[:], ALU.add))
            sch.step("vector", [], lambda v: v.tensor_tensor(
                dd_r[:], hi_r[:], mid_r[:], ALU.subtract))
            sch.step("vector", [], lambda v: v.tensor_tensor(
                dd_r[:], dd_r[:], cge_r[:], ALU.mult))
            prev_ge_v = sch.step("vector", [], lambda v: v.tensor_tensor(
                hi_r[:], mid_r[:], dd_r[:], ALU.add))
        tb = sch.step("tensor", [(V, prev_ge_v)], lambda t: t.matmul(
            ps_g[:, 0:8], ones_rowf[:], lo_r[:], start=True, stop=True), inc=(T, 1))
        thr_done = sch.step("vector", [(T, tb)], lambda v: v.tensor_copy(
            thrB[:], ps_g[:, 0:8]))

        gv = 0
        for ti in range(NT):
            ldg = sch.step("sync", [(V, gv)] if gv else [], lambda s, ti=ti: s.dma_start(
                out=p_t[:], in_=probs_loc[ti * 128:(ti + 1) * 128, :]), inc=(WP, 16))
            gw = [(WP, WP.n), (V, thr_done)] if ti == 0 else [(WP, WP.n)]
            sch.step("vector", gw, lambda v: v.tensor_tensor(
                g_eq[:], p_t[:], thrB[:], ALU.is_ge))
            sch.step("vector", [], lambda v: v.scalar_tensor_tensor(
                g_m1[:], g_eq[:], 30.0, p_t[:], ALU.mult, ALU.add))
            sch.step("vector", [], lambda v: v.tensor_scalar(
                g_msk[:], g_m1[:], 30.0, None, ALU.subtract))
            sch.step("vector", [], lambda v: v.tensor_reduce(
                colA[:], g_msk[:], AX.X, ALU.max))
            sch.step("vector", [], lambda v: v.tensor_reduce(
                colP[:], p_t[:], AX.X, ALU.max))
            sch.step("vector", [], lambda v: v.tensor_single_scalar(
                cola[:], colA[:], -1.0, ALU.is_ge))
            sch.step("vector", [], lambda v: v.tensor_tensor(
                colt[:], colA[:], colP[:], ALU.subtract))
            sch.step("vector", [], lambda v: v.scalar_tensor_tensor(
                colW[:], colt[:], cola[:, 0:1], colP[:], ALU.mult, ALU.add))
            sch.step("vector", [], lambda v: v.tensor_scalar(
                colna[:], cola[:], -1.0, 1.0, ALU.mult, ALU.add))
            sch.step("vector", [], lambda v: v.tensor_scalar(
                g_t1[:], g_eq[:], -30.0, 30.0, ALU.mult, ALU.add))
            sch.step("vector", [], lambda v: v.scalar_tensor_tensor(
                g_u[:], g_t1[:], colna[:, 0:1], g_msk[:], ALU.mult, ALU.add))
            sch.step("vector", [], lambda v: v.tensor_reduce(
                colt[:], g_u[:], AX.X, ALU.max))
            sch.step("vector", [], lambda v: v.tensor_scalar(
                g_eq[:], g_u[:], colt[:, 0:1], None, ALU.is_equal))
            sch.step("vector", [], lambda v: v.memset(colpref[:], 0.0))
            for e in range(8):
                sch.step("vector", [], lambda v: v.tensor_scalar(
                    colt[:], colpref[:], -1.0, 1.0, ALU.mult, ALU.add))
                sch.step("vector", [], lambda v, e=e: v.tensor_tensor(
                    g_ind[:, e:e + 1], g_eq[:, e:e + 1], colt[:], ALU.mult))
                sch.step("vector", [], lambda v, e=e: v.tensor_tensor(
                    colpref[:], colpref[:], g_eq[:, e:e + 1], ALU.max))
            gv = sch.step("vector", [], lambda v, ti=ti: v.tensor_scalar(
                gates_all[:, ti * 8:(ti + 1) * 8], g_ind[:], colW[:, 0:1],
                None, ALU.mult))
        gates_done = gv

        psbf_last = [0]
        xnb_T = [0]
        wa_last = [0]
        wb_last = [0]
        x3_store = {}
        for hf in range(NHALF):
            xnT_done = 0
            for tt in range(TPH):
                ti = hf * TPH + tt
                w = [(V, x_rel[-2])] if len(x_rel) >= 2 else []
                ld = sch.step("sync", w, lambda s, ti=ti: s.dma_start(
                    out=xt[ti % 2][:], in_=x2[ti * 128:(ti + 1) * 128, :]),
                    inc=(LDp[ti % 2], 16))
                nw = [(LDp[ti % 2], ld)]
                if xnb_T[0]:
                    nw.append((T, xnb_T[0]))
                nrm = sch.step("vector", nw,
                               lambda v, ti=ti: v.tensor_scalar(
                                   xn_b[:], xt[ti % 2][:], mu2[:, ti:ti + 1],
                                   rstd2[:, ti:ti + 1], ALU.subtract, ALU.mult))
                x_rel.append(nrm)
                evA = []
                for c2 in range(KCH):
                    slot = c2 % 3
                    waits = [(V, nrm)] if c2 == 0 else []
                    if c2 >= 3:
                        waits.append((A, evA[c2 - 3]))
                    if c2 == 0:
                        waits.append((G, id_done))
                        waits.append((A, psbf_last[0]))
                    sch.step("tensor", waits, lambda t, c2=c2, slot=slot, tt=tt:
                             t.transpose(ps_t[slot][0:128, 0:128],
                                         xn_b[:, c2 * 128:(c2 + 1) * 128],
                                         ident[:]), inc=(T, 1))
                    ev = sch.step("scalar", [(T, T.n)],
                                  lambda s, c2=c2, slot=slot, tt=tt: s.copy(
                                      xnT_half[0:128,
                                               c2 * (TPH * 128) + tt * 128:
                                               c2 * (TPH * 128) + tt * 128 + 128],
                                      ps_t[slot][0:128, 0:128]))
                    evA.append(ev)
                psbf_last[0] = evA[-1]
                xnT_done = evA[-1]
                xnb_T[0] = T.n

            if hf == 0:
                prev_hev = 0
                lastA_T = [0]
                lastH_T = [0]
            NG = 8
            for e in range(8):
                for c2 in range(KCH):
                    wv = [(T, lastA_T[0])] if c2 == 0 else []
                    sch.step("gdma", wv, lambda g, e=e, c2=c2: g.dma_start(
                        out=w1_b[:, c2 * 2048:(c2 + 1) * 2048],
                        in_=w1_in[e, c2 * 128:(c2 + 1) * 128, :]), inc=(WA, 16))
                for fc in range(NFC):
                    wv = [(T, lastH_T[0])] if fc == 0 else []
                    sch.step("gdma", wv, lambda g, e=e, fc=fc: g.dma_start(
                        out=w2_b[:, fc * C:(fc + 1) * C],
                        in_=w2_in[e, fc * 128:(fc + 1) * 128, :]), inc=(WB, 16))
                wa_tot, wb_tot = WA.n, WB.n

                for tg in range(NG):
                    aev = {}
                    for fc in range(NFC):
                        psA = ps_q if fc % 2 == 0 else ps_s
                        def a_mm(t, fc=fc, tg=tg, psA=psA, e=e):
                            r = None
                            for c2 in range(KCH):
                                r = t.matmul(
                                    psA[:, 0:256],
                                    w1_b[:, c2 * 2048 + fc * 128:
                                         c2 * 2048 + fc * 128 + 128],
                                    xnT_half[:, c2 * (TPH * 128) + tg * 256:
                                             c2 * (TPH * 128) + tg * 256 + 256],
                                    start=(c2 == 0), stop=(c2 == KCH - 1))
                            return r
                        waits = []
                        if fc == 0 and tg == 0:
                            waits = [(WA, wa_tot), (A, xnT_done)]
                        if fc >= 2:
                            waits.append((A, aev[fc - 2]))
                        elif not (fc == 0 and tg == 0 and e == 0 and hf == 0):
                            waits.append((A, A.n))
                        amm = sch.step("tensor", waits, a_mm, inc=(T, 1))
                        aev[fc] = sch.step("scalar", [(T, amm)],
                                           lambda s, fc=fc, psA=psA:
                                           s.activation(a_fm[:, fc * 256:(fc + 1) * 256],
                                                        psA[:, 0:256], AF.Gelu))
                    lastA_T[0] = T.n
                    for ts_ in range(2):
                        tt = tg * 2 + ts_
                        ti = hf * TPH + tt
                        def h_mm(t, ts_=ts_):
                            r = None
                            for fc in range(NFC):
                                t.matmul(ps_h1[:, 0:512],
                                         a_fm[:, fc * 256 + ts_ * 128:
                                              fc * 256 + ts_ * 128 + 128],
                                         w2_b[:, fc * C:fc * C + 512],
                                         start=(fc == 0), stop=(fc == NFC - 1))
                                r = t.matmul(ps_h2[:, 0:512],
                                             a_fm[:, fc * 256 + ts_ * 128:
                                                  fc * 256 + ts_ * 128 + 128],
                                             w2_b[:, fc * C + 512:fc * C + 1024],
                                             start=(fc == 0), stop=(fc == NFC - 1))
                            return r
                        hw = [(A, aev[NFC - 1])]
                        if ts_ == 0:
                            hw.append((WB, wb_tot))
                        if prev_hev:
                            hw.append((V, prev_hev))
                        hmm = sch.step("tensor", hw, h_mm, inc=(T, 1))
                        gcol = gates_all[:, ti * 8 + e:ti * 8 + e + 1]
                        hv = [(T, hmm), (V, gates_done)]
                        if e == 0:
                            ev1 = sch.step("vector", hv, lambda v, tt=tt, gcol=gcol:
                                           v.tensor_scalar(
                                               out_acc[:, tt * C:tt * C + 512],
                                               ps_h1[:, 0:512], gcol, None, ALU.mult))
                            ev2 = sch.step("vector", [], lambda v, tt=tt, gcol=gcol:
                                           v.tensor_scalar(
                                               out_acc[:, tt * C + 512:tt * C + 1024],
                                               ps_h2[:, 0:512], gcol, None, ALU.mult))
                        else:
                            ev1 = sch.step("vector", hv, lambda v, tt=tt, gcol=gcol:
                                           v.scalar_tensor_tensor(
                                               out_acc[:, tt * C:tt * C + 512],
                                               ps_h1[:, 0:512], gcol,
                                               out_acc[:, tt * C:tt * C + 512],
                                               ALU.mult, ALU.add))
                            ev2 = sch.step("vector", [], lambda v, tt=tt, gcol=gcol:
                                           v.scalar_tensor_tensor(
                                               out_acc[:, tt * C + 512:tt * C + 1024],
                                               ps_h2[:, 0:512], gcol,
                                               out_acc[:, tt * C + 512:tt * C + 1024],
                                               ALU.mult, ALU.add))
                        prev_hev = ev2
                    lastH_T[0] = T.n


            for tt in range(TPH):
                ti = hf * TPH + tt
                w = [(V, x_rel[-2])] if len(x_rel) >= 2 else []
                ld = sch.step("sync", w, lambda s, ti=ti: s.dma_start(
                    out=xt[ti % 2][:], in_=x2[ti * 128:(ti + 1) * 128, :]),
                    inc=(LDp[ti % 2], 16))
                x3w = [(LDp[ti % 2], ld), (V, prev_hev)]
                if (ti - 2) in x3_store:
                    x3w.append((STp[ti % 2], x3_store[ti - 2]))
                x3v = sch.step("vector", x3w, lambda v, ti=ti, tt=tt: v.tensor_tensor(
                    x3t[ti % 2][:], xt[ti % 2][:], out_acc[:, tt * C:(tt + 1) * C],
                    ALU.add))
                x_rel.append(x3v)
                stv = sch.step("sync", [(V, x3v)], lambda s, ti=ti: s.dma_start(
                    out=x3_out[ti * 128:(ti + 1) * 128, :], in_=x3t[ti % 2][:]),
                    inc=(STp[ti % 2], 16))
                x3_store[ti] = stv

        sch.step("gpsimd", [(ST, ST.n), (STB, STB.n)], lambda g: None)
        sch.emit(block)
    return nc


_STAGE_B1 = None


def run_stage_b1(x2_full, probs_full, inputs):
    global _STAGE_B1
    if _STAGE_B1 is None:
        _STAGE_B1 = build_stage_b1()
    w1 = np.ascontiguousarray(np.asarray(inputs["L0_ffn_W1"], np.float32))
    w2 = np.ascontiguousarray(np.asarray(inputs["L0_ffn_W2"], np.float32))
    in_maps = []
    for c in range(NCORES):
        x2c = np.ascontiguousarray(x2_full[c * BPC:(c + 1) * BPC].reshape(TPC, C))
        in_maps.append(dict(x2=x2c, probs_all=probs_full,
                            probs_loc=np.ascontiguousarray(
                                probs_full[c * TPC:(c + 1) * TPC]),
                            w1_in=w1, w2_in=w2))
    res = run_bass_kernel_spmd(_STAGE_B1, in_maps, core_ids=list(range(NCORES)))
    x3 = np.stack([res.results[c]["x3_out"].reshape(BPC, N, C)
                   for c in range(NCORES)]).reshape(B, N, C)
    return x3



def build_stage_b2():
    nc = bass.Bass("TRN2", target_bir_lowering=False)
    x3 = nc.dram_tensor("x3", [TPC, C], F32, kind="ExternalInput").ap()
    k123T = nc.dram_tensor("k123T", [C, 48], F32, kind="ExternalInput").ap()
    abR = nc.dram_tensor("abR", [1, 2 * H * 64], F32, kind="ExternalInput").ap()
    bT = nc.dram_tensor("bT", [64, 2 * H], F32, kind="ExternalInput").ap()
    ccT = nc.dram_tensor("ccT", [64, H], F32, kind="ExternalInput").ap()
    vwF = nc.dram_tensor("vwF", [C, H * 64], F32, kind="ExternalInput").ap()
    owS = nc.dram_tensor("owS", [H * 64, C], F32, kind="ExternalInput").ap()
    posT = nc.dram_tensor("posT", [1, N], F32, kind="ExternalInput").ap()
    utri = nc.dram_tensor("utri", [128, 128], F32, kind="ExternalInput").ap()
    x4_out = nc.dram_tensor("x4_out", [TPC, C], F32, kind="ExternalOutput").ap()
    m0_out = nc.dram_tensor("m0_out", [1, 1], F32, kind="ExternalOutput").ap()
    mu4_out = nc.dram_tensor("mu4_out", [128, NT], F32, kind="ExternalOutput").ap()
    rstd4_out = nc.dram_tensor("rstd4_out", [128, NT], F32, kind="ExternalOutput").ap()

    with ExitStack() as ctx:
        sbt = lambda n, s, d=F32: ctx.enter_context(nc.sbuf_tensor(n, s, d))
        pst = lambda n, s, d=F32: ctx.enter_context(nc.psum_tensor(n, s, d))

        ones_row = sbt("ones_row", [1, 128], BF16)
        ones_col = sbt("ones_col", [128, 1], BF16)
        ident = sbt("ident", [128, 128], BF16)
        identf = sbt("identf", [128, 128], F32)
        uptri = sbt("uptri", [128, 128], BF16)
        utri_f = sbt("utri_f", [128, 128], F32)
        k123_b = sbt("k123_b", [128, KCH * 48], BF16)
        vw_b = sbt("vw_b", [128, KCH * (H * 64)], BF16)
        ow_b = sbt("ow_b", [128, 8 * C], BF16)
        abR_s = sbt("abR_s", [1, 2 * H * 64], F32)
        bT_s = sbt("bT_s", [64, 2 * H], F32)
        ccT_s = sbt("ccT_s", [64, H], F32)
        posr = sbt("posr", [1, N], F32)
        nhat = sbt("nhat", [1, N], F32)
        sinm = sbt("sinm", [64, 512], F32)
        p1T = sbt("p1T", [128, NTS * H], F32)
        p2eT = sbt("p2eT", [128, NTS * H], F32)
        mu3 = sbt("mu3", [128, NT], F32)
        rstd3 = sbt("rstd3", [128, NT], F32)
        mu4 = sbt("mu4", [128, NT], F32)
        rstd4 = sbt("rstd4", [128, NT], F32)
        sumc = sbt("sumc", [128, NT], F32)
        sqc = sbt("sqc", [128, NT], F32)
        scr_nt = sbt("scr_nt", [128, NT], F32)
        eps5_col = sbt("eps5_col", [128, 1], F32)
        xt = [sbt(f"xt{i}", [128, C]) for i in range(2)]
        x4t = [sbt(f"x4t{i}", [128, C]) for i in range(2)]
        xn_b = sbt("xn_b", [128, C], BF16)
        xnT = sbt("xnT", [128, C], BF16)
        scr = sbt("scr", [128, C])
        v_b = sbt("v_b", [128, C], BF16)
        cde_s = sbt("cde_s", [128, 64])
        cdeE = sbt("cdeE", [128, 48])
        crE_b = sbt("crE_b", [128, H], BF16)
        c1t = sbt("c1t", [128, H])
        denp = sbt("denp", [128, H])
        dent = sbt("dent", [128, H])
        s_b = sbt("s_b", [128, C], BF16)
        vdiag = sbt("vdiag", [128, C])
        numt = sbt("numt", [128, C])
        o_b = sbt("o_b", [128, C], BF16)
        oTp = [sbt(f"oTp{j}", [128, 128], BF16) for j in range(8)]
        carry_sb = sbt("carry_sb", [1, C], F32)
        carry_b = sbt("carry_b", [1, C], BF16)
        cden_sb = sbt("cden_sb", [1, H], F32)
        cden_b = sbt("cden_b", [1, H], BF16)
        m0c = sbt("m0c", [128, 1], F32)
        m0row = sbt("m0row", [1, 128], F32)
        m0s = sbt("m0s", [1, 1], F32)
        colA = sbt("colA", [128, 1])

        ps_t = [pst(f"ps_t{i}", [128, 128], BF16) for i in range(3)]
        ps_v1 = pst("ps_v1", [128, 512])
        ps_v2 = pst("ps_v2", [128, 512])
        ps_p1 = pst("ps_p1", [128, 512])
        ps_p2 = pst("ps_p2", [128, 512])
        ps_c = pst("ps_c", [128, 512])

        S = {nm: Sem(ctx.enter_context(nc.semaphore(nm)))
             for nm in ("LD", "LDB", "W", "CN", "V", "A", "T", "G", "ST", "STB")}
        (LD, LDB, W, CN, V, A, T, G, ST, STB) = (
            S[n] for n in ("LD", "LDB", "W", "CN", "V", "A", "T", "G", "ST", "STB"))
        LDp = [LD, LDB]
        STp = [ST, STB]
        block = ctx.enter_context(nc.Block())
        sch = Sched({"vector": V, "scalar": A, "gpsimd": G})

        for c2 in range(KCH):
            sch.step("gdma", [], lambda g, c2=c2: g.dma_start(
                out=k123_b[:, c2 * 48:(c2 + 1) * 48],
                in_=k123T[c2 * 128:(c2 + 1) * 128, :]), inc=(W, 16))
            sch.step("gdma", [], lambda g, c2=c2: g.dma_start(
                out=vw_b[:, c2 * (H * 64):(c2 + 1) * (H * 64)],
                in_=vwF[c2 * 128:(c2 + 1) * 128, :]), inc=(W, 16))
            sch.step("gdma", [], lambda g, c2=c2: g.dma_start(
                out=ow_b[:, c2 * C:(c2 + 1) * C],
                in_=owS[c2 * 128:(c2 + 1) * 128, :]), inc=(W, 16))
        sch.step("sync", [], lambda s: s.dma_start(out=abR_s[:], in_=abR), inc=(CN, 16))
        sch.step("sync", [], lambda s: s.dma_start(out=bT_s[:], in_=bT), inc=(CN, 16))
        sch.step("sync", [], lambda s: s.dma_start(out=ccT_s[:], in_=ccT), inc=(CN, 16))
        sch.step("sync", [], lambda s: s.dma_start(out=posr[:], in_=posT), inc=(CN, 16))
        w_all, cn_all = W.n, CN.n

        consts_v = sch.step("vector", [], lambda v: (
            v.memset(ones_row[:], 1.0), v.memset(ones_col[:], 1.0),
            v.memset(eps5_col[:], 1e-5), v.memset(m0c[:], 0.0),
            v.memset(identf[:], 1.0))[-1])
        id_ms = sch.step("gpsimd", [], lambda g: g.memset(ident[:], 1.0))
        id_done = sch.step("gpsimd", [(G, id_ms)], lambda g: g.affine_select(
            ident[:], ident[:], pattern=[[-1, 128]], compare_op=ALU.is_equal,
            fill=0.0, base=0, channel_multiplier=1))
        ut_ld = sch.step("sync", [], lambda s: s.dma_start(
            out=utri_f[:], in_=utri), inc=(CN, 16))
        ut_done = sch.step("vector", [(CN, CN.n)], lambda v: v.tensor_copy(
            uptri[:], utri_f[:]))
        idf_done = sch.step("gpsimd", [(G, id_done), (V, consts_v)], lambda g: g.affine_select(
            identf[:], identf[:], pattern=[[-1, 128]], compare_op=ALU.is_equal,
            fill=0.0, base=0, channel_multiplier=1))

        nh_done = sch.step("vector", [(CN, cn_all), (V, consts_v)],
                           lambda v: v.tensor_scalar(
                               nhat[:], posr[:], 1.0 / float(N - 1), None, ALU.mult))

        prev_sin_a = 0
        for pp in range(2):
            dstT = p1T if pp == 0 else p2eT
            for h in range(H):
                for chunk in range(4):
                    mm = sch.step("tensor",
                                  [(V, nh_done), (A, prev_sin_a)],
                                  lambda t, pp=pp, h=h, chunk=chunk: t.matmul(
                                      ps_c[0:64, 0:512],
                                      abR_s[0:1, pp * 1024 + h * 64:
                                            pp * 1024 + (h + 1) * 64],
                                      nhat[0:1, chunk * 512:(chunk + 1) * 512],
                                      start=True, stop=True), inc=(T, 1))
                    prev_sin_a = sch.step("scalar", [(T, mm)],
                                          lambda s, pp=pp, h=h: s.activation(
                                              sinm[:], ps_c[0:64, 0:512], AF.Sin,
                                              bias=bT_s[0:64, pp * H + h:pp * H + h + 1],
                                              scale=1.0))
                    for sub in range(4):
                        mm2 = sch.step("tensor", [(A, prev_sin_a)],
                                       lambda t, sub=sub, h=h: t.matmul(
                                           ps_v1[0:128, sub:sub + 1],
                                           sinm[0:64, sub * 128:(sub + 1) * 128],
                                           ccT_s[0:64, h:h + 1],
                                           start=True, stop=True), inc=(T, 1))
                    out3 = (dstT[:].rearrange("p (t hh) -> p t hh", hh=H)
                            [:, chunk * 4:chunk * 4 + 4, h:h + 1])
                    in3 = ps_v1[0:128, 0:4].rearrange("p (a b) -> p a b", b=1)
                    ev = sch.step("scalar", [(T, T.n)],
                                  lambda s, out3=out3, in3=in3: s.copy(out3, in3))
                    prev_sin_a = ev
        sch.step("vector", [(A, prev_sin_a)], lambda v: v.tensor_scalar(
            p2eT[:], p2eT[:], 20.0, None, ALU.min))
        p2c = sch.step("vector", [], lambda v: v.tensor_scalar(
            p2eT[:], p2eT[:], -20.0, None, ALU.max))
        p2e_done = sch.step("scalar", [(V, p2c)], lambda s: s.activation(
            p2eT[:], p2eT[:], AF.Exp))

        x_rel = []
        for ti in range(NT):
            w = [(V, x_rel[-2])] if len(x_rel) >= 2 else []
            ld = sch.step("sync", w, lambda s, ti=ti: s.dma_start(
                out=xt[ti % 2][:], in_=x3[ti * 128:(ti + 1) * 128, :]),
                inc=(LDp[ti % 2], 16))
            sch.step("vector", [(LDp[ti % 2], ld)], lambda v, ti=ti: v.tensor_reduce(
                sumc[:, ti:ti + 1], xt[ti % 2][:], AX.X, ALU.add))
            rel = sch.step("vector", [], lambda v, ti=ti: v.scalar_tensor_tensor(
                scr[:], xt[ti % 2][:], 0.0, xt[ti % 2][:], ALU.add, ALU.mult,
                accum_out=sqc[:, ti:ti + 1]))
            x_rel.append(rel)
        sch.step("vector", [], lambda v: v.tensor_scalar(
            mu3[:], sumc[:], 1.0 / C, None, ALU.mult))
        sch.step("vector", [], lambda v: v.tensor_scalar(
            sqc[:], sqc[:], 1.0 / C, None, ALU.mult))
        sch.step("vector", [], lambda v: v.tensor_tensor(
            scr_nt[:], mu3[:], mu3[:], ALU.mult))
        var_done = sch.step("vector", [], lambda v: v.tensor_tensor(
            sqc[:], sqc[:], scr_nt[:], ALU.subtract))
        std_done = sch.step("scalar", [(V, var_done)], lambda s: s.activation(
            rstd3[:], sqc[:], AF.Sqrt, bias=eps5_col[:, 0:1], scale=1.0))
        stats_done = sch.step("vector", [(A, std_done)], lambda v: v.reciprocal(
            rstd3[:], rstd3[:]))

        psbf_last = [0]
        xnb_T = [0]
        x4_store = {}
        prev_tile_v = 0
        prev_tile_a = 0
        prev_oT_T = 0
        for s_ix in range(BPC):
            cr_reset = sch.step("vector", [], lambda v: (
                v.memset(carry_sb[:], 0.0), v.memset(cden_sb[:], 0.0))[-1])
            for t_ix in range(NTS):
                ti = s_ix * NTS + t_ix
                w = [(V, x_rel[-2])] if len(x_rel) >= 2 else []
                ld = sch.step("sync", w, lambda s, ti=ti: s.dma_start(
                    out=xt[ti % 2][:], in_=x3[ti * 128:(ti + 1) * 128, :]),
                    inc=(LDp[ti % 2], 16))
                nw = [(LDp[ti % 2], ld), (V, stats_done)]
                if xnb_T[0]:
                    nw.append((T, xnb_T[0]))
                nrm = sch.step("vector", nw, lambda v, ti=ti: v.tensor_scalar(
                    xn_b[:], xt[ti % 2][:], mu3[:, ti:ti + 1], rstd3[:, ti:ti + 1],
                    ALU.subtract, ALU.mult))
                evA = []
                for c2 in range(KCH):
                    slot = c2 % 3
                    waits = [(V, nrm)] if c2 == 0 else []
                    if c2 >= 3:
                        waits.append((A, evA[c2 - 3]))
                    if c2 == 0:
                        waits.append((G, idf_done))
                        waits.append((A, psbf_last[0]))
                    sch.step("tensor", waits, lambda t, c2=c2, slot=slot: t.transpose(
                        ps_t[slot][0:128, 0:128],
                        xn_b[:, c2 * 128:(c2 + 1) * 128], ident[:]), inc=(T, 1))
                    ev = sch.step("scalar", [(T, T.n)], lambda s, c2=c2, slot=slot:
                                  s.copy(xnT[0:128, c2 * 128:(c2 + 1) * 128],
                                         ps_t[slot][0:128, 0:128]))
                    evA.append(ev)
                psbf_last[0] = evA[-1]
                xnT_done = evA[-1]
                xnb_T[0] = T.n

                def v_mm(t):
                    r = None
                    for c2 in range(KCH):
                        t.matmul(ps_v1[:, 0:512], xnT[:, c2 * 128:(c2 + 1) * 128],
                                 vw_b[:, c2 * 1024:c2 * 1024 + 512],
                                 start=(c2 == 0), stop=(c2 == KCH - 1))
                        r = t.matmul(ps_v2[:, 0:512], xnT[:, c2 * 128:(c2 + 1) * 128],
                                     vw_b[:, c2 * 1024 + 512:c2 * 1024 + 1024],
                                     start=(c2 == 0), stop=(c2 == KCH - 1))
                    return r
                vw_w = [(A, xnT_done), (W, w_all)]
                if prev_tile_v:
                    vw_w.append((V, prev_tile_v))
                vmm = sch.step("tensor", vw_w, v_mm, inc=(T, 1))
                vev = sch.step("scalar", [(T, vmm)], lambda s: s.copy(
                    v_b[:, 0:512], ps_v1[:, 0:512]))
                vev = sch.step("scalar", [], lambda s: s.copy(
                    v_b[:, 512:1024], ps_v2[:, 0:512]))

                def cde_mm(t):
                    r = None
                    for j in range(3):
                        for c2 in range(KCH):
                            r = t.matmul(ps_c[:, j * 16:(j + 1) * 16],
                                         xnT[:, c2 * 128:(c2 + 1) * 128],
                                         k123_b[:, c2 * 48 + j * 16:c2 * 48 + (j + 1) * 16],
                                         start=(j == 0 and c2 == 0),
                                         stop=(j == 2 and c2 == KCH - 1))
                    return r
                cw = [(A, xnT_done)]
                if prev_tile_v:
                    cw.append((V, prev_tile_v))
                cmm = sch.step("tensor", cw, cde_mm, inc=(T, 1))
                cev = sch.step("scalar", [(T, cmm)], lambda s: s.copy(
                    cde_s[:, 0:48], ps_c[:, 0:48]))

                sch.step("vector", [(A, cev)], lambda v, t_ix=t_ix: v.tensor_tensor(
                    cdeE[:, 0:16], cde_s[:, 0:16],
                    p1T[:, t_ix * H:(t_ix + 1) * H], ALU.add))
                sch.step("vector", [], lambda v: v.tensor_copy(
                    cdeE[:, 16:48], cde_s[:, 16:48]))
                sch.step("vector", [], lambda v: v.tensor_scalar(
                    cdeE[:], cdeE[:], 20.0, None, ALU.min))
                ce1 = sch.step("vector", [], lambda v: v.tensor_scalar(
                    cdeE[:], cdeE[:], -20.0, None, ALU.max))
                ce2 = sch.step("scalar", [(V, ce1), (A, p2e_done)], lambda s: s.activation(
                    cdeE[:], cdeE[:], AF.Exp))
                sv = None
                for h in range(H):
                    sv = sch.step("vector", [(A, ce2)] if h == 0 else [],
                                  lambda v, h=h: v.tensor_scalar(
                                      s_b[:, h * 64:(h + 1) * 64],
                                      v_b[:, h * 64:(h + 1) * 64],
                                      cdeE[:, h:h + 1], None, ALU.mult))
                crb = sch.step("vector", [], lambda v: v.tensor_copy(
                    crE_b[:], cdeE[:, 0:16]))
                cab = sch.step("vector", [], lambda v: v.tensor_copy(
                    carry_b[:], carry_sb[:]))
                cdb = sch.step("vector", [], lambda v: v.tensor_copy(
                    cden_b[:], cden_sb[:]))

                def pre_mm(t):
                    r = None
                    for h in range(H):
                        psd = ps_p1 if h < 8 else ps_p2
                        off = (h % 8) * 64
                        t.matmul(psd[:, off:off + 64], uptri[:],
                                 s_b[:, h * 64:(h + 1) * 64],
                                 start=(h % 8 == 0), stop=False)
                        r = t.matmul(psd[:, off:off + 64], ones_row[:],
                                     carry_b[0:1, h * 64:(h + 1) * 64],
                                     start=False, stop=(h % 8 == 7))
                    return r
                pw = [(V, cdb), (V, ut_done)]
                pmm = sch.step("tensor", pw, pre_mm, inc=(T, 1))
                def den_mm(t):
                    t.matmul(ps_c[:, 48:64], uptri[:], crE_b[:],
                             start=True, stop=False)
                    return t.matmul(ps_c[:, 48:64], ones_row[:], cden_b[0:1, :],
                                    start=False, stop=True)
                dmm = sch.step("tensor", [(V, cdb)], den_mm, inc=(T, 1))
                def csum_mm(t):
                    t.matmul(ps_v1[0:1, 0:512], ones_col[:], s_b[:, 0:512],
                             start=True, stop=True)
                    t.matmul(ps_v2[0:1, 0:512], ones_col[:], s_b[:, 512:1024],
                             start=True, stop=True)
                    return t.matmul(ps_c[0:1, 64:80], ones_col[:], crE_b[:],
                                    start=True, stop=True)
                smm = sch.step("tensor", [(T, pmm), (A, A.n)], csum_mm, inc=(T, 1))
                cup = sch.step("vector", [(T, smm)], lambda v: v.scalar_tensor_tensor(
                    carry_sb[0:1, 0:512], ps_v1[0:1, 0:512], 0.0,
                    carry_sb[0:1, 0:512], ALU.add, ALU.add))
                cup = sch.step("vector", [], lambda v: v.scalar_tensor_tensor(
                    carry_sb[0:1, 512:1024], ps_v2[0:1, 0:512], 0.0,
                    carry_sb[0:1, 512:1024], ALU.add, ALU.add))
                cup = sch.step("vector", [], lambda v: v.scalar_tensor_tensor(
                    cden_sb[0:1, :], ps_c[0:1, 64:80], 0.0,
                    cden_sb[0:1, :], ALU.add, ALU.add))

                sch.step("vector", [], lambda v, t_ix=t_ix: v.tensor_tensor(
                    c1t[:], p2eT[:, t_ix * H:(t_ix + 1) * H], cdeE[:, 32:48], ALU.mult))
                sch.step("vector", [(T, smm)], lambda v: v.tensor_copy(
                    denp[:], ps_c[:, 48:64]))
                sch.step("vector", [], lambda v: v.tensor_tensor(
                    dent[:], denp[:], c1t[:], ALU.mult))
                sch.step("vector", [], lambda v: v.tensor_tensor(
                    dent[:], dent[:], cdeE[:, 16:32], ALU.add))
                dr = sch.step("vector", [], lambda v: v.reciprocal(dent[:], dent[:]))
                ov = None
                for h in range(H):
                    psd = ps_p1 if h < 8 else ps_p2
                    off = (h % 8) * 64
                    sch.step("vector", [(T, pmm)] if h == 0 else [],
                             lambda v, h=h: v.tensor_scalar(
                                 vdiag[:, h * 64:(h + 1) * 64],
                                 v_b[:, h * 64:(h + 1) * 64],
                                 cdeE[:, 16 + h:17 + h], None, ALU.mult))
                    sch.step("vector", [], lambda v, h=h, psd=psd, off=off:
                             v.scalar_tensor_tensor(
                                 numt[:, h * 64:(h + 1) * 64],
                                 psd[:, off:off + 64], c1t[:, h:h + 1],
                                 vdiag[:, h * 64:(h + 1) * 64], ALU.mult, ALU.add))
                    ov = sch.step("vector", [], lambda v, h=h: v.tensor_scalar(
                        o_b[:, h * 64:(h + 1) * 64],
                        numt[:, h * 64:(h + 1) * 64],
                        dent[:, h:h + 1], None, ALU.mult))
                evO = []
                for h in range(H):
                    slot = h % 3
                    waits = [(V, ov)] if h == 0 else []
                    if h >= 3:
                        waits.append((A, evO[h - 3]))
                    if h == 0:
                        waits.append((A, psbf_last[0]))
                        if prev_oT_T:
                            waits.append((T, prev_oT_T))
                    sch.step("tensor", waits, lambda t, h=h, slot=slot: t.transpose(
                        ps_t[slot][0:64, 0:128],
                        o_b[:, h * 64:(h + 1) * 64], ident[:]), inc=(T, 1))
                    ev = sch.step("scalar", [(T, T.n)], lambda s, h=h, slot=slot:
                                  s.copy(oTp[h // 2][(h % 2) * 64:(h % 2) * 64 + 64, :],
                                         ps_t[slot][0:64, 0:128]))
                    evO.append(ev)
                psbf_last[0] = evO[-1]
                def o_mm(t):
                    r = None
                    for j in range(8):
                        t.matmul(ps_v1[:, 0:512], oTp[j][:],
                                 ow_b[:, j * C:j * C + 512],
                                 start=(j == 0), stop=(j == 7))
                        r = t.matmul(ps_v2[:, 0:512], oTp[j][:],
                                     ow_b[:, j * C + 512:j * C + 1024],
                                     start=(j == 0), stop=(j == 7))
                    return r
                omm = sch.step("tensor", [(A, evO[-1]), (V, cup)], o_mm, inc=(T, 1))
                prev_oT_T = omm
                x4w = [(T, omm)]
                if (ti - 2) in x4_store:
                    x4w.append((STp[ti % 2], x4_store[ti - 2]))
                sch.step("vector", x4w, lambda v, ti=ti: v.tensor_tensor(
                    x4t[ti % 2][:, 0:512], xt[ti % 2][:, 0:512],
                    ps_v1[:, 0:512], ALU.add))
                x4v = sch.step("vector", [], lambda v, ti=ti: v.tensor_tensor(
                    x4t[ti % 2][:, 512:1024], xt[ti % 2][:, 512:1024],
                    ps_v2[:, 0:512], ALU.add))
                x_rel.append(x4v)
                sch.step("vector", [], lambda v, ti=ti: v.tensor_reduce(
                    sumc[:, ti:ti + 1], x4t[ti % 2][:], AX.X, ALU.add))
                st2 = sch.step("vector", [], lambda v, ti=ti: v.scalar_tensor_tensor(
                    scr[:], x4t[ti % 2][:], 0.0, x4t[ti % 2][:], ALU.add, ALU.mult,
                    accum_out=sqc[:, ti:ti + 1]))
                prev_tile_v = st2
                stv = sch.step("sync", [(V, x4v)], lambda s, ti=ti: s.dma_start(
                    out=x4_out[ti * 128:(ti + 1) * 128, :], in_=x4t[ti % 2][:]),
                    inc=(STp[ti % 2], 16))
                x4_store[ti] = stv

        sch.step("vector", [], lambda v: v.tensor_scalar(
            mu4[:], sumc[:], 1.0 / C, None, ALU.mult))
        sch.step("vector", [], lambda v: v.tensor_scalar(
            sqc[:], sqc[:], 1.0 / C, None, ALU.mult))
        sch.step("vector", [], lambda v: v.tensor_tensor(
            scr_nt[:], mu4[:], mu4[:], ALU.mult))
        var4 = sch.step("vector", [], lambda v: v.tensor_tensor(
            sqc[:], sqc[:], scr_nt[:], ALU.subtract))
        sd4 = sch.step("scalar", [(V, var4)], lambda s: s.activation(
            rstd4[:], sqc[:], AF.Sqrt, bias=eps5_col[:, 0:1], scale=1.0))
        st4 = sch.step("vector", [(A, sd4)], lambda v: v.reciprocal(
            rstd4[:], rstd4[:]))
        for ti in range(NT):
            w = [(V, x_rel[-2])] if len(x_rel) >= 2 else []
            w = w + [(ST, ST.n), (STB, STB.n)]
            ld = sch.step("sync", w, lambda s, ti=ti: s.dma_start(
                out=xt[ti % 2][:], in_=x4_out[ti * 128:(ti + 1) * 128, :]),
                inc=(LDp[ti % 2], 16))
            nrm = sch.step("vector", [(LDp[ti % 2], ld), (V, st4)],
                           lambda v, ti=ti: v.tensor_scalar(
                               scr[:], xt[ti % 2][:], mu4[:, ti:ti + 1],
                               rstd4[:, ti:ti + 1], ALU.subtract, ALU.mult))
            x_rel.append(nrm)
            sch.step("vector", [], lambda v: v.tensor_reduce(
                colA[:], scr[:], AX.X, ALU.max, apply_absolute_value=True))
            m0u = sch.step("vector", [], lambda v: v.tensor_tensor(
                m0c[:], m0c[:], colA[:], ALU.max))
        m0t = sch.step("tensor", [(V, m0u), (G, idf_done)], lambda t: t.transpose(
            ps_c[0:1, 0:128], m0c[:], identf[:]), inc=(T, 1))
        m0e = sch.step("vector", [(T, m0t)], lambda v: v.tensor_copy(
            m0row[:], ps_c[0:1, 0:128]))
        m0r = sch.step("vector", [], lambda v: v.tensor_reduce(
            m0s[:], m0row[:], AX.X, ALU.max))
        sch.step("sync", [(V, m0r)], lambda s: s.dma_start(
            out=m0_out, in_=m0s[:]), inc=(ST, 16))
        sch.step("sync", [], lambda s: s.dma_start(
            out=mu4_out, in_=mu4[:]), inc=(ST, 16))
        sch.step("sync", [], lambda s: s.dma_start(
            out=rstd4_out, in_=rstd4[:]), inc=(ST, 16))
        sch.step("gpsimd", [(ST, ST.n), (STB, STB.n)], lambda g: None)
        sch.emit(block)
    return nc


_STAGE_B2 = None


def run_stage_b2(x3_full, inputs):
    global _STAGE_B2
    if _STAGE_B2 is None:
        _STAGE_B2 = build_stage_b2()
    k1 = np.asarray(inputs["L1_attn_k1"], np.float32)
    k2 = np.asarray(inputs["L1_attn_k2"], np.float32)
    k3 = np.asarray(inputs["L1_attn_k3"], np.float32)
    k123T = np.ascontiguousarray(np.concatenate([k1.T, k2.T, k3.T], axis=1))
    a1 = np.asarray(inputs["L1_attn_a1"], np.float32).reshape(H, 64)
    a2 = np.asarray(inputs["L1_attn_a2"], np.float32).reshape(H, 64)
    b1 = np.asarray(inputs["L1_attn_b1"], np.float32).reshape(H, 64)
    b2 = np.asarray(inputs["L1_attn_b2"], np.float32).reshape(H, 64)
    cc = np.asarray(inputs["L1_attn_c"], np.float32).reshape(H, 64)
    abR = np.zeros((1, 2 * H * 64), np.float32)
    abR[0, :H * 64] = a1.reshape(-1)
    abR[0, H * 64:] = a2.reshape(-1)
    bT = np.ascontiguousarray(np.concatenate([b1.T, b2.T], axis=1))
    ccT = np.ascontiguousarray(cc.T)
    vw = np.asarray(inputs["L1_attn_value_weight"], np.float32)
    vwF = np.ascontiguousarray(vw.transpose(1, 0, 2).reshape(C, H * 64))
    ow = np.asarray(inputs["L1_attn_output_weight"], np.float32)
    owS = np.ascontiguousarray(ow.transpose(0, 2, 1).reshape(H * 64, C))
    posT = np.arange(N, dtype=np.float32).reshape(1, N)
    in_maps = []
    for c in range(NCORES):
        x3c = np.ascontiguousarray(x3_full[c * BPC:(c + 1) * BPC].reshape(TPC, C))
        in_maps.append(dict(x3=x3c, k123T=k123T, abR=abR, bT=bT, ccT=ccT,
                            vwF=vwF, owS=owS, posT=posT,
                            utri=np.triu(np.ones((128, 128), np.float32))))
    res = run_bass_kernel_spmd(_STAGE_B2, in_maps, core_ids=list(range(NCORES)))
    x4 = np.stack([res.results[c]["x4_out"].reshape(BPC, N, C)
                   for c in range(NCORES)]).reshape(B, N, C)
    m0 = np.stack([res.results[c]["m0_out"].reshape(1) for c in range(NCORES)])
    mu4 = np.stack([res.results[c]["mu4_out"] for c in range(NCORES)])
    rstd4 = np.stack([res.results[c]["rstd4_out"] for c in range(NCORES)])
    return x4, m0, mu4, rstd4



def build_stage_l1ffn(mode):
    nc = bass.Bass("TRN2", target_bir_lowering=False)
    x4 = nc.dram_tensor("x4", [TPC, C], F32, kind="ExternalInput").ap()
    mu4c = nc.dram_tensor("mu4c", [128, NT], F32, kind="ExternalInput").ap()
    rstd4c = nc.dram_tensor("rstd4c", [128, NT], F32, kind="ExternalInput").ap()
    m_all = nc.dram_tensor("m_all", [1, 64], F32, kind="ExternalInput").ap()
    if mode == "c":
        wu_nat = nc.dram_tensor("wu_nat", [3072, C], F32, kind="ExternalInput").ap()
        wg_nat = nc.dram_tensor("wg_nat", [3072, C], F32, kind="ExternalInput").ap()
        wuT_out = nc.dram_tensor("wuT_out", [128, KCH * 3072], BF16,
                                 kind="ExternalOutput").ap()
        wgT_out = nc.dram_tensor("wgT_out", [128, KCH * 3072], BF16,
                                 kind="ExternalOutput").ap()
        m_out = nc.dram_tensor("m_out", [1, 2], F32, kind="ExternalOutput").ap()
    elif mode in ("d", "e"):
        wuT_in = nc.dram_tensor("wuT_in", [128, KCH * 3072], BF16,
                                kind="ExternalInput").ap()
        wgT_in = nc.dram_tensor("wgT_in", [128, KCH * 3072], BF16,
                                kind="ExternalInput").ap()
        m_out = nc.dram_tensor("m_out", [1, 1], F32, kind="ExternalOutput").ap()
    else:
        wuT_in = nc.dram_tensor("wuT_in", [128, KCH * 3072], BF16,
                                kind="ExternalInput").ap()
        wgT_in = nc.dram_tensor("wgT_in", [128, KCH * 3072], BF16,
                                kind="ExternalInput").ap()
        o_out = nc.dram_tensor("o_out", [TPC, 3072], BF16, kind="ExternalOutput").ap()

    QMAX = 32767.0
    NRC = 3072 // 128
    NDC = 6

    with ExitStack() as ctx:
        sbt = lambda n, s, d=F32: ctx.enter_context(nc.sbuf_tensor(n, s, d))
        pst = lambda n, s, d=F32: ctx.enter_context(nc.psum_tensor(n, s, d))

        ones_rowf = sbt("ones_rowf", [1, 128], F32)
        ident = sbt("ident", [128, 128], BF16)
        identf = sbt("identf", [128, 128], F32)
        m_s = sbt("m_s", [1, 64], F32)
        mrow = sbt("mrow", [1, 1], F32)
        sc_sc = sbt("sc_sc", [1, 8])
        rrow = sbt("rrow", [1, 8])
        scB = sbt("scB", [128, 8])
        scB2 = sbt("scB2", [128, 8])
        mu4s = sbt("mu4s", [128, NT], F32)
        rstd4s = sbt("rstd4s", [128, NT], F32)
        xt = [sbt(f"xt{i}", [128, C]) for i in range(2)]
        xq_b = sbt("xq_b", [128, C], BF16)
        scr = sbt("scr", [128, C])
        xqT = sbt("xqT", [128, KCH * (NT * 128)], BF16)
        wt_f = [sbt(f"wt_f{i}", [128, C]) for i in range(2)]
        wsc = sbt("wsc", [128, 32])
        wsc_r = sbt("wsc_r", [128, 32])
        wdq_b = sbt("wdq_b", [128, C], BF16)
        wuT = sbt("wuT", [128, KCH * 3072], BF16)
        wgT = sbt("wgT", [128, KCH * 3072], BF16)
        upf = sbt("upf", [128, 512])
        gtf = sbt("gtf", [128, 512])
        sigf = sbt("sigf", [128, 512])
        t3f = sbt("t3f", [128, 512])
        t4f = sbt("t4f", [128, 512])
        colm = [sbt(f"colm{i}", [128, 1]) for i in range(2)]
        m_run = sbt("m_run", [128, 2])
        mrow128 = sbt("mrow128", [1, 128])
        mres = sbt("mres", [1, 2])
        if mode == "f1":
            of_t = [sbt("of_t0", [128, 3072], BF16)] * 2

        ps_t = [pst(f"ps_t{i}", [128, 128], BF16) for i in range(3)]
        ps_u1 = pst("ps_u1", [128, 512])
        ps_u2 = pst("ps_u2", [128, 512])
        ps_g1 = pst("ps_g1", [128, 512])
        ps_g2 = pst("ps_g2", [128, 512])
        ps_m = pst("ps_m", [128, 512])

        S = {nm: Sem(ctx.enter_context(nc.semaphore(nm)))
             for nm in ("LD", "LDB", "WL", "WLB", "CN", "V", "A", "T", "G", "ST", "STB")}
        (LD, LDB, WL, WLB, CN, V, A, T, G, ST, STB) = (
            S[n] for n in ("LD", "LDB", "WL", "WLB", "CN", "V", "A", "T", "G", "ST", "STB"))
        LDp = [LD, LDB]
        WLp = [WL, WLB]
        STp = [ST, STB]
        block = ctx.enter_context(nc.Block())
        sch = Sched({"vector": V, "scalar": A, "gpsimd": G})

        sch.step("sync", [], lambda s: s.dma_start(out=m_s[:], in_=m_all), inc=(CN, 16))
        sch.step("sync", [], lambda s: s.dma_start(out=mu4s[:], in_=mu4c), inc=(CN, 16))
        sch.step("sync", [], lambda s: s.dma_start(out=rstd4s[:], in_=rstd4c), inc=(CN, 16))
        cn_all = CN.n
        consts_v = sch.step("vector", [], lambda v: (
            v.memset(ones_rowf[:], 1.0), v.memset(identf[:], 1.0),
            v.memset(sc_sc[:], 1.0), v.memset(rrow[:], 1.0),
            v.memset(m_run[:], 0.0))[-1])
        id_ms = sch.step("gpsimd", [], lambda g: g.memset(ident[:], 1.0))
        id_done = sch.step("gpsimd", [(G, id_ms)], lambda g: g.affine_select(
            ident[:], ident[:], pattern=[[-1, 128]], compare_op=ALU.is_equal,
            fill=0.0, base=0, channel_multiplier=1))
        idf_done = sch.step("gpsimd", [(G, id_done), (V, consts_v)],
                            lambda g: g.affine_select(
                                identf[:], identf[:], pattern=[[-1, 128]],
                                compare_op=ALU.is_equal, fill=0.0, base=0,
                                channel_multiplier=1))

        nsc = {"c": 1, "d": 3, "e": 4, "f1": 5}[mode]
        sc_done = None
        for j in range(nsc):
            sch.step("vector", [(CN, cn_all), (V, consts_v)] if j == 0 else [],
                     lambda v, j=j: v.tensor_reduce(
                         mrow[:], m_s[0:1, j * 8:(j + 1) * 8], AX.X, ALU.max))
            sch.step("vector", [], lambda v, j=j: v.tensor_scalar(
                sc_sc[0:1, j:j + 1], mrow[:], 1.0 / QMAX, None, ALU.mult))
            sc_done = sch.step("vector", [], lambda v, j=j: v.tensor_scalar(
                sc_sc[0:1, j:j + 1], sc_sc[0:1, j:j + 1], 1e-12, None, ALU.max))
        rc = sch.step("vector", [(V, sc_done)], lambda v: v.reciprocal(
            rrow[0:1, 0:nsc], sc_sc[0:1, 0:nsc]))
        bc1 = sch.step("tensor", [(V, rc)], lambda t: t.matmul(
            ps_m[:, 0:8], ones_rowf[:], sc_sc[:], start=True, stop=True), inc=(T, 1))
        bv1 = sch.step("vector", [(T, bc1)], lambda v: v.tensor_copy(
            scB[:], ps_m[:, 0:8]))
        bc2 = sch.step("tensor", [(V, bv1)], lambda t: t.matmul(
            ps_m[:, 0:8], ones_rowf[:], rrow[:], start=True, stop=True), inc=(T, 1))
        bv2 = sch.step("vector", [(T, bc2)], lambda v: v.tensor_copy(
            scB2[:], ps_m[:, 0:8]))
        s_col = lambda j: scB[:, j:j + 1]
        r_col = lambda j: scB2[:, j:j + 1]

        x_rel = []
        psbf_last = [0]
        xnb_T = [0]
        for ti in range(NT):
            w = [(V, x_rel[-2])] if len(x_rel) >= 2 else []
            ld = sch.step("sync", w, lambda s, ti=ti: s.dma_start(
                out=xt[ti % 2][:], in_=x4[ti * 128:(ti + 1) * 128, :]),
                inc=(LDp[ti % 2], 16))
            nw = [(LDp[ti % 2], ld), (V, bv2)]
            if xnb_T[0]:
                nw.append((T, xnb_T[0]))
            sch.step("vector", nw, lambda v, ti=ti: v.tensor_scalar(
                scr[:], xt[ti % 2][:], mu4s[:, ti:ti + 1], rstd4s[:, ti:ti + 1],
                ALU.subtract, ALU.mult))
            sch.step("vector", [], lambda v: v.tensor_scalar(
                scr[:], scr[:], r_col(0), MAGIC, ALU.mult, ALU.add))
            sch.step("vector", [], lambda v: v.tensor_scalar(
                scr[:], scr[:], MAGIC, QMAX, ALU.subtract, ALU.min))
            nrm = sch.step("vector", [], lambda v: v.tensor_scalar(
                xq_b[:], scr[:], -QMAX - 1.0, s_col(0), ALU.max, ALU.mult))
            x_rel.append(nrm)
            evA = []
            for c2 in range(KCH):
                slot = c2 % 3
                waits = [(V, nrm)] if c2 == 0 else []
                if c2 >= 3:
                    waits.append((A, evA[c2 - 3]))
                if c2 == 0:
                    waits.append((G, idf_done))
                    waits.append((A, psbf_last[0]))
                sch.step("tensor", waits, lambda t, c2=c2, slot=slot: t.transpose(
                    ps_t[slot][0:128, 0:128],
                    xq_b[:, c2 * 128:(c2 + 1) * 128], ident[:]), inc=(T, 1))
                ev = sch.step("scalar", [(T, T.n)], lambda s, c2=c2, slot=slot, ti=ti:
                              s.copy(xqT[0:128, c2 * (NT * 128) + ti * 128:
                                          c2 * (NT * 128) + ti * 128 + 128],
                                     ps_t[slot][0:128, 0:128]))
                evA.append(ev)
            psbf_last[0] = evA[-1]
            xnb_T[0] = T.n
        xqT_done = evA[-1]

        wdq_T = [0]
        def dequant_weight(w_nat, dstT, first_v):
            last_v = first_v
            last_ev = psbf_last[0]
            for rc_ in range(NRC):
                buf = wt_f[rc_ % 2]
                lw = [(V, last_v)] if last_v else []
                ldw = sch.step("sync", lw, lambda s, rc_=rc_, w_nat=w_nat, buf=buf:
                               s.dma_start(out=buf[:],
                                           in_=w_nat[rc_ * 128:(rc_ + 1) * 128, :]),
                               inc=(WLp[rc_ % 2], 16))
                dqw = [(WLp[rc_ % 2], ldw)]
                if wdq_T[0]:
                    dqw.append((T, wdq_T[0]))
                sch.step("vector", dqw, lambda v, buf=buf: v.tensor_reduce(
                    wsc[:], buf[:].rearrange("p (b k) -> p b k", k=32),
                    AX.X, ALU.max, apply_absolute_value=True))
                sch.step("vector", [], lambda v: v.tensor_scalar(
                    wsc[:], wsc[:], 1.0 / 7.0, 1e-12, ALU.mult, ALU.max))
                sch.step("vector", [], lambda v: v.reciprocal(wsc_r[:], wsc[:]))
                sc3 = bass.AP(wsc_r[:].tensor, 0, [[32, 128], [1, 32], [0, 32]])
                sc3b = bass.AP(wsc[:].tensor, 0, [[32, 128], [1, 32], [0, 32]])
                b3 = buf[:].rearrange("p (b k) -> p b k", k=32)
                sch.step("vector", [], lambda v, b3=b3, sc3=sc3: v.tensor_tensor(
                    b3, b3, sc3, ALU.mult))
                sch.step("vector", [], lambda v, buf=buf: v.tensor_scalar(
                    buf[:], buf[:], MAGIC, MAGIC, ALU.add, ALU.subtract))
                sch.step("vector", [], lambda v, buf=buf: v.tensor_scalar(
                    buf[:], buf[:], 7.0, -8.0, ALU.min, ALU.max))
                lv = sch.step("vector", [], lambda v, b3=b3, sc3b=sc3b: v.tensor_tensor(
                    wdq_b[:].rearrange("p (b k) -> p b k", k=32), b3, sc3b, ALU.mult))
                last_v = lv
                evB = []
                for c2 in range(KCH):
                    slot = c2 % 3
                    waits = [(V, lv)] if c2 == 0 else []
                    if c2 >= 3:
                        waits.append((A, evB[c2 - 3]))
                    if c2 == 0:
                        waits.append((A, psbf_last[0]))
                    sch.step("tensor", waits, lambda t, c2=c2, slot=slot: t.transpose(
                        ps_t[slot][0:128, 0:128],
                        wdq_b[:, c2 * 128:(c2 + 1) * 128], ident[:]), inc=(T, 1))
                    ev = sch.step("scalar", [(T, T.n)],
                                  lambda s, c2=c2, slot=slot, rc_=rc_, dstT=dstT:
                                  s.copy(dstT[0:128, c2 * 3072 + rc_ * 128:
                                              c2 * 3072 + rc_ * 128 + 128],
                                         ps_t[slot][0:128, 0:128]))
                    evB.append(ev)
                psbf_last[0] = evB[-1]
                wdq_T[0] = T.n
                last_ev = evB[-1]
            return last_v, last_ev

        if mode == "c":
            vU, aU = dequant_weight(wu_nat, wuT, 0)
            vG, aG = dequant_weight(wg_nat, wgT, vU)
            wT_done = aG
            sch.step("sync", [(A, wT_done)], lambda s: s.dma_start(
                out=wuT_out, in_=wuT[:]), inc=(ST, 16))
            sch.step("sync", [], lambda s: s.dma_start(
                out=wgT_out, in_=wgT[:]), inc=(ST, 16))
        else:
            for half in range(2):
                sch.step("sync", [], lambda s, half=half: s.dma_start(
                    out=wuT[:, half * (KCH * 1536):(half + 1) * (KCH * 1536)],
                    in_=wuT_in[:, half * (KCH * 1536):(half + 1) * (KCH * 1536)]),
                    inc=(WL, 16))
                sch.step("sync", [], lambda s, half=half: s.dma_start(
                    out=wgT[:, half * (KCH * 1536):(half + 1) * (KCH * 1536)],
                    in_=wgT_in[:, half * (KCH * 1536):(half + 1) * (KCH * 1536)]),
                    inc=(WL, 16))
            wld = WL.n
            wT_done = sch.step("scalar", [(WL, wld)], lambda s: s.memzero(
                mrow128[:]))

        frag_hist = []
        o_store = {}
        fv = 0
        for ti in range(NT):
            for dc in range(NDC):
                pu = ps_u1 if dc % 2 == 0 else ps_u2
                pg = ps_g1 if dc % 2 == 0 else ps_g2
                def up_mm(t, ti=ti, dc=dc, pu=pu, pg=pg):
                    r = None
                    for c2 in range(KCH):
                        t.matmul(pu[:, 0:512],
                                 xqT[:, c2 * (NT * 128) + ti * 128:
                                     c2 * (NT * 128) + ti * 128 + 128],
                                 wuT[:, c2 * 3072 + dc * 512:c2 * 3072 + dc * 512 + 512],
                                 start=(c2 == 0), stop=(c2 == KCH - 1))
                        r = t.matmul(pg[:, 0:512],
                                     xqT[:, c2 * (NT * 128) + ti * 128:
                                         c2 * (NT * 128) + ti * 128 + 128],
                                     wgT[:, c2 * 3072 + dc * 512:c2 * 3072 + dc * 512 + 512],
                                     start=(c2 == 0), stop=(c2 == KCH - 1))
                    return r
                mw = []
                if ti == 0 and dc == 0:
                    mw = [(A, wT_done), (A, xqT_done)]
                if len(frag_hist) >= 2:
                    mw.append((V, frag_hist[-2]))
                umm = sch.step("tensor", mw, up_mm, inc=(T, 1))
                if mode == "c":
                    sch.step("vector", [(T, umm)], lambda v, pu=pu: v.tensor_reduce(
                        colm[0][:], pu[:, 0:512], AX.X, ALU.max,
                        apply_absolute_value=True))
                    sch.step("vector", [], lambda v: v.tensor_tensor(
                        m_run[:, 0:1], m_run[:, 0:1], colm[0][:], ALU.max))
                    sch.step("vector", [], lambda v, pg=pg: v.tensor_reduce(
                        colm[1][:], pg[:, 0:512], AX.X, ALU.max,
                        apply_absolute_value=True))
                    fv = sch.step("vector", [], lambda v: v.tensor_tensor(
                        m_run[:, 1:2], m_run[:, 1:2], colm[1][:], ALU.max))
                else:
                    sch.step("vector", [(T, umm)], lambda v, pu=pu: v.tensor_scalar(
                        upf[:], pu[:, 0:512], r_col(1), MAGIC, ALU.mult, ALU.add))
                    sch.step("vector", [], lambda v: v.tensor_scalar(
                        upf[:], upf[:], MAGIC, QMAX, ALU.subtract, ALU.min))
                    sch.step("vector", [], lambda v: v.tensor_scalar(
                        upf[:], upf[:], -QMAX - 1.0, s_col(1), ALU.max, ALU.mult))
                    sch.step("vector", [], lambda v, pg=pg: v.tensor_scalar(
                        gtf[:], pg[:, 0:512], r_col(2), MAGIC, ALU.mult, ALU.add))
                    sch.step("vector", [], lambda v: v.tensor_scalar(
                        gtf[:], gtf[:], MAGIC, QMAX, ALU.subtract, ALU.min))
                    gq = sch.step("vector", [], lambda v: v.tensor_scalar(
                        gtf[:], gtf[:], -QMAX - 1.0, s_col(2), ALU.max, ALU.mult))
                    sg = sch.step("scalar", [(V, gq)], lambda s: s.activation(
                        sigf[:], gtf[:], AF.Sigmoid))
                    sch.step("vector", [(A, sg)], lambda v: v.tensor_scalar(
                        sigf[:], sigf[:], 65536.0, MAGIC, ALU.mult, ALU.add))
                    sch.step("vector", [], lambda v: v.tensor_scalar(
                        sigf[:], sigf[:], MAGIC, 65535.0, ALU.subtract, ALU.min))
                    sch.step("vector", [], lambda v: v.tensor_scalar(
                        sigf[:], sigf[:], 0.0, 1.0 / 65536.0, ALU.max, ALU.mult))
                    t3v = sch.step("vector", [], lambda v: v.tensor_tensor(
                        t3f[:], gtf[:], sigf[:], ALU.mult))
                    if mode == "d":
                        sch.step("vector", [], lambda v: v.tensor_reduce(
                            colm[0][:], t3f[:], AX.X, ALU.max,
                            apply_absolute_value=True))
                        fv = sch.step("vector", [], lambda v: v.tensor_tensor(
                            m_run[:, 0:1], m_run[:, 0:1], colm[0][:], ALU.max))
                    else:
                        sch.step("vector", [], lambda v: v.tensor_scalar(
                            t3f[:], t3f[:], r_col(3), MAGIC, ALU.mult, ALU.add))
                        sch.step("vector", [], lambda v: v.tensor_scalar(
                            t3f[:], t3f[:], MAGIC, QMAX, ALU.subtract, ALU.min))
                        sch.step("vector", [], lambda v: v.tensor_scalar(
                            t3f[:], t3f[:], -QMAX - 1.0, s_col(3), ALU.max, ALU.mult))
                        t4v = sch.step("vector", [], lambda v: v.tensor_tensor(
                            t4f[:], t3f[:], upf[:], ALU.mult))
                        if mode == "e":
                            sch.step("vector", [], lambda v: v.tensor_reduce(
                                colm[0][:], t4f[:], AX.X, ALU.max,
                                apply_absolute_value=True))
                            fv = sch.step("vector", [], lambda v: v.tensor_tensor(
                                m_run[:, 0:1], m_run[:, 0:1], colm[0][:], ALU.max))
                        else:
                            sch.step("vector", [], lambda v: v.tensor_scalar(
                                t4f[:], t4f[:], r_col(4), MAGIC, ALU.mult, ALU.add))
                            sch.step("vector", [], lambda v: v.tensor_scalar(
                                t4f[:], t4f[:], MAGIC, QMAX, ALU.subtract, ALU.min))
                            ow_ = [(STp[(ti - 1) % 2], o_store[ti - 1])] \
                                if (dc == 0 and (ti - 1) in o_store) else []
                            fv = sch.step("vector", ow_, lambda v, dc=dc, ti=ti:
                                          v.tensor_scalar(
                                              of_t[ti % 2][:, dc * 512:(dc + 1) * 512],
                                              t4f[:], -QMAX - 1.0, s_col(4),
                                              ALU.max, ALU.mult))
                frag_hist.append(fv)
            if mode == "f1":
                stv = sch.step("sync", [(V, fv)], lambda s, ti=ti: s.dma_start(
                    out=o_out[ti * 128:(ti + 1) * 128, :], in_=of_t[ti % 2][:]),
                    inc=(STp[ti % 2], 16))
                o_store[ti] = stv

        if mode != "f1":
            nout = 2 if mode == "c" else 1
            last = fv
            for j in range(nout):
                tj = sch.step("tensor", [(V, last), (G, idf_done)],
                              lambda t, j=j: t.transpose(
                                  ps_m[0:1, 0:128], m_run[:, j:j + 1], identf[:]),
                              inc=(T, 1))
                sch.step("vector", [(T, tj)], lambda v: v.tensor_copy(
                    mrow128[:], ps_m[0:1, 0:128]))
                last = sch.step("vector", [], lambda v, j=j: v.tensor_reduce(
                    mres[0:1, j:j + 1], mrow128[:], AX.X, ALU.max))
            sch.step("sync", [(V, last)], lambda s: s.dma_start(
                out=m_out, in_=mres[0:1, 0:nout]), inc=(ST, 16))
        sch.step("gpsimd", [(ST, ST.n), (STB, STB.n)], lambda g: None)
        sch.emit(block)
    return nc



_STAGE_A = None


def _stage_a_nc():
    global _STAGE_A
    if _STAGE_A is None:
        _STAGE_A = build_stage_a()
    return _STAGE_A


def run_stage_a(inputs):
    x = np.asarray(inputs["L0_attn_x"], np.float32)
    wqT = np.ascontiguousarray(np.asarray(inputs["L0_attn_Wq"], np.float32).T)
    wkT = np.ascontiguousarray(np.asarray(inputs["L0_attn_Wk"], np.float32).T)
    wvT = np.ascontiguousarray(np.asarray(inputs["L0_attn_Wv"], np.float32).T)
    bqkv = np.stack([np.asarray(inputs["L0_attn_bq"], np.float32),
                     np.asarray(inputs["L0_attn_bk"], np.float32),
                     np.asarray(inputs["L0_attn_bv"], np.float32)])
    gb = np.stack([np.asarray(inputs["L0_attn_gamma"], np.float32),
                   np.asarray(inputs["L0_attn_beta"], np.float32)])
    wg = np.asarray(inputs["L0_ffn_Wg"], np.float32)
    posT = np.arange(N, dtype=np.float32).reshape(1, N)
    jfrac = (np.arange(0, C, 2, dtype=np.float32) / np.float32(C)).reshape(1, C // 2)

    nc = _stage_a_nc()
    in_maps = []
    for c in range(NCORES):
        xa = np.ascontiguousarray(x[c * BPC:(c + 1) * BPC].reshape(TPC, C))
        in_maps.append(dict(xA=xa, wqT=wqT, wkT=wkT, wvT=wvT, bqkv=bqkv, gb=gb,
                            wg=wg, posT=posT, jfrac=jfrac))
    res = run_bass_kernel_spmd(nc, in_maps, core_ids=list(range(NCORES)))
    x2 = np.stack([res.results[c]["x2_out"].reshape(BPC, N, C)
                   for c in range(NCORES)]).reshape(B, N, C)
    probs = np.stack([res.results[c]["probs_out"].reshape(BPC * N, 8)
                      for c in range(NCORES)]).reshape(B * N, 8)
    return x2, probs


def build_stage_f2():
    nc = bass.Bass("TRN2", target_bir_lowering=False)
    o_in = nc.dram_tensor("o_in", [TPC, 3072], BF16, kind="ExternalInput").ap()
    x4 = nc.dram_tensor("x4", [TPC, C], F32, kind="ExternalInput").ap()
    wd_nat = nc.dram_tensor("wd_nat", [C, 3072], F32, kind="ExternalInput").ap()
    y_out = nc.dram_tensor("y_out", [TPC, C], F32, kind="ExternalOutput").ap()

    NI3 = 24

    with ExitStack() as ctx:
        sbt = lambda n, s, d=F32: ctx.enter_context(nc.sbuf_tensor(n, s, d))
        pst = lambda n, s, d=F32: ctx.enter_context(nc.psum_tensor(n, s, d))
        ident = sbt("ident", [128, 128], BF16)
        wt_f = [sbt(f"wt_f{i}", [128, 3072]) for i in range(2)]
        wsc = sbt("wsc", [128, 96])
        wsc_r = sbt("wsc_r", [128, 96])
        wdq_b = sbt("wdq_b", [128, 3072], BF16)
        wdT = sbt("wdT", [128, NI3 * C], BF16)
        ot_b = [sbt(f"ot_b{i}", [128, 3072], BF16) for i in range(2)]
        oT = sbt("oT", [128, NI3 * 128], BF16)
        xt = [sbt(f"xt{i}", [128, C]) for i in range(2)]
        yt = [sbt(f"yt{i}", [128, C]) for i in range(2)]
        ps_t = [pst(f"ps_t{i}", [128, 128], BF16) for i in range(3)]
        ps_y1 = pst("ps_y1", [128, 512])
        ps_y2 = pst("ps_y2", [128, 512])

        S = {nm: Sem(ctx.enter_context(nc.semaphore(nm)))
             for nm in ("LD", "LDB", "LO", "LOB", "WL", "WLB", "V", "A", "T", "G", "ST", "STB")}
        (LD, LDB, LO, LOB, WL, WLB, V, A, T, G, ST, STB) = (
            S[n] for n in ("LD", "LDB", "LO", "LOB", "WL", "WLB", "V", "A", "T", "G",
                           "ST", "STB"))
        WLp = [WL, WLB]
        LDp = [LD, LDB]
        LOp = [LO, LOB]
        STp = [ST, STB]
        block = ctx.enter_context(nc.Block())
        sch = Sched({"vector": V, "scalar": A, "gpsimd": G})

        id_ms = sch.step("gpsimd", [], lambda g: g.memset(ident[:], 1.0))
        id_done = sch.step("gpsimd", [(G, id_ms)], lambda g: g.affine_select(
            ident[:], ident[:], pattern=[[-1, 128]], compare_op=ALU.is_equal,
            fill=0.0, base=0, channel_multiplier=1))

        psbf_last = [0]
        wdq_T = [0]
        last_v = 0
        for rc_ in range(KCH):
            buf = wt_f[rc_ % 2]
            lw = [(V, last_v)] if rc_ >= 2 else []
            ldw = sch.step("sync", lw, lambda s, rc_=rc_, buf=buf: s.dma_start(
                out=buf[:], in_=wd_nat[rc_ * 128:(rc_ + 1) * 128, :]),
                inc=(WLp[rc_ % 2], 16))
            dqw = [(WLp[rc_ % 2], ldw)]
            if wdq_T[0]:
                dqw.append((T, wdq_T[0]))
            sch.step("vector", dqw, lambda v, buf=buf: v.tensor_reduce(
                wsc[:], buf[:].rearrange("p (b k) -> p b k", k=32),
                AX.X, ALU.max, apply_absolute_value=True))
            sch.step("vector", [], lambda v: v.tensor_scalar(
                wsc[:], wsc[:], 1.0 / 7.0, 1e-12, ALU.mult, ALU.max))
            sch.step("vector", [], lambda v: v.reciprocal(wsc_r[:], wsc[:]))
            sc3 = bass.AP(wsc_r[:].tensor, 0, [[96, 128], [1, 96], [0, 32]])
            sc3b = bass.AP(wsc[:].tensor, 0, [[96, 128], [1, 96], [0, 32]])
            b3 = buf[:].rearrange("p (b k) -> p b k", k=32)
            sch.step("vector", [], lambda v, b3=b3, sc3=sc3: v.tensor_tensor(
                b3, b3, sc3, ALU.mult))
            sch.step("vector", [], lambda v, buf=buf: v.tensor_scalar(
                buf[:], buf[:], MAGIC, MAGIC, ALU.add, ALU.subtract))
            sch.step("vector", [], lambda v, buf=buf: v.tensor_scalar(
                buf[:], buf[:], 7.0, -8.0, ALU.min, ALU.max))
            lv = sch.step("vector", [], lambda v, b3=b3, sc3b=sc3b: v.tensor_tensor(
                wdq_b[:].rearrange("p (b k) -> p b k", k=32), b3, sc3b, ALU.mult))
            last_v = lv
            evB = []
            for c2 in range(NI3):
                slot = c2 % 3
                waits = [(V, lv)] if c2 == 0 else []
                if c2 >= 3:
                    waits.append((A, evB[c2 - 3]))
                if c2 == 0:
                    waits.append((G, id_done))
                    waits.append((A, psbf_last[0]))
                sch.step("tensor", waits, lambda t, c2=c2, slot=slot: t.transpose(
                    ps_t[slot][0:128, 0:128],
                    wdq_b[:, c2 * 128:(c2 + 1) * 128], ident[:]), inc=(T, 1))
                ev = sch.step("scalar", [(T, T.n)],
                              lambda s, c2=c2, slot=slot, rc_=rc_: s.copy(
                                  wdT[0:128, c2 * C + rc_ * 128:
                                      c2 * C + rc_ * 128 + 128],
                                  ps_t[slot][0:128, 0:128]))
                evB.append(ev)
            psbf_last[0] = evB[-1]
            wdq_T[0] = T.n
        wT_done = psbf_last[0]

        o_rel = []
        x_rel = []
        oT_T = [0]
        y_store = {}
        for ti in range(NT):
            w = [(T, o_rel[-2])] if len(o_rel) >= 2 else []
            ldo = sch.step("sync", w, lambda s, ti=ti: s.dma_start(
                out=ot_b[ti % 2][:], in_=o_in[ti * 128:(ti + 1) * 128, :]),
                inc=(LOp[ti % 2], 16))
            wx = [(V, x_rel[-2])] if len(x_rel) >= 2 else []
            ldx = sch.step("sync", wx, lambda s, ti=ti: s.dma_start(
                out=xt[ti % 2][:], in_=x4[ti * 128:(ti + 1) * 128, :]),
                inc=(LDp[ti % 2], 16))
            evA = []
            for c2 in range(NI3):
                slot = c2 % 3
                waits = [(LOp[ti % 2], ldo)] if c2 == 0 else []
                if c2 >= 3:
                    waits.append((A, evA[c2 - 3]))
                if c2 == 0:
                    waits.append((A, psbf_last[0]))
                    if oT_T[0]:
                        waits.append((T, oT_T[0]))
                sch.step("tensor", waits, lambda t, c2=c2, slot=slot, ti=ti:
                         t.transpose(ps_t[slot][0:128, 0:128],
                                     ot_b[ti % 2][:, c2 * 128:(c2 + 1) * 128],
                                     ident[:]), inc=(T, 1))
                ev = sch.step("scalar", [(T, T.n)], lambda s, c2=c2, slot=slot:
                              s.copy(oT[0:128, c2 * 128:(c2 + 1) * 128],
                                     ps_t[slot][0:128, 0:128]))
                evA.append(ev)
            psbf_last[0] = evA[-1]
            o_rel.append(T.n)
            def y_mm(t, ti=ti):
                r = None
                for c2 in range(NI3):
                    t.matmul(ps_y1[:, 0:512], oT[:, c2 * 128:(c2 + 1) * 128],
                             wdT[:, c2 * C:c2 * C + 512],
                             start=(c2 == 0), stop=(c2 == NI3 - 1))
                    r = t.matmul(ps_y2[:, 0:512], oT[:, c2 * 128:(c2 + 1) * 128],
                                 wdT[:, c2 * C + 512:c2 * C + 1024],
                                 start=(c2 == 0), stop=(c2 == NI3 - 1))
                return r
            yw = [(A, evA[-1])]
            if ti > 0:
                yw.append((V, prev_yv))
            ymm = sch.step("tensor", yw, y_mm, inc=(T, 1))
            oT_T[0] = ymm
            y1w = [(T, ymm), (LDp[ti % 2], ldx)]
            if (ti - 2) in y_store:
                y1w.append((STp[ti % 2], y_store[ti - 2]))
            sch.step("vector", y1w, lambda v, ti=ti: v.tensor_tensor(
                yt[ti % 2][:, 0:512], xt[ti % 2][:, 0:512], ps_y1[:, 0:512], ALU.add))
            yv = sch.step("vector", [], lambda v, ti=ti: v.tensor_tensor(
                yt[ti % 2][:, 512:1024], xt[ti % 2][:, 512:1024],
                ps_y2[:, 0:512], ALU.add))
            prev_yv = yv
            x_rel.append(yv)
            stv = sch.step("sync", [(V, yv)], lambda s, ti=ti: s.dma_start(
                out=y_out[ti * 128:(ti + 1) * 128, :], in_=yt[ti % 2][:]),
                inc=(STp[ti % 2], 16))
            y_store[ti] = stv
        sch.step("gpsimd", [(ST, ST.n), (STB, STB.n)], lambda g: None)
        sch.emit(block)
    return nc



_L1FFN = {}
_STAGE_F2 = None


def run_stage_l1ffn(mode, x4_full, mu4, rstd4, m_parts, inputs, wrelay=None):
    if mode not in _L1FFN:
        _L1FFN[mode] = build_stage_l1ffn(mode)
    m_all = np.zeros((1, 64), np.float32)
    for j, mp in enumerate(m_parts):
        m_all[0, j * 8:j * 8 + len(mp)] = np.asarray(mp).reshape(-1)
    in_maps = []
    for c in range(NCORES):
        x4c = np.ascontiguousarray(x4_full[c * BPC:(c + 1) * BPC].reshape(TPC, C))
        im = dict(x4=x4c, mu4c=np.ascontiguousarray(mu4[c]),
                  rstd4c=np.ascontiguousarray(rstd4[c]), m_all=m_all)
        if mode == "c":
            im["wu_nat"] = np.ascontiguousarray(
                np.asarray(inputs["L1_ffn_w_up"], np.float32))
            im["wg_nat"] = np.ascontiguousarray(
                np.asarray(inputs["L1_ffn_w_gate"], np.float32))
        else:
            im["wuT_in"], im["wgT_in"] = wrelay
        in_maps.append(im)
    res = run_bass_kernel_spmd(_L1FFN[mode], in_maps, core_ids=list(range(NCORES)))
    if mode == "f1":
        o = np.stack([np.asarray(res.results[c]["o_out"]) for c in range(NCORES)])
        return o
    if mode == "c":
        wrelay_out = (np.asarray(res.results[0]["wuT_out"]),
                      np.asarray(res.results[0]["wgT_out"]))
        m = np.stack([np.asarray(res.results[c]["m_out"]).reshape(-1)
                      for c in range(NCORES)])
        return m, wrelay_out
    return np.stack([np.asarray(res.results[c]["m_out"]).reshape(-1)
                     for c in range(NCORES)])


def run_stage_f2(o_shards, x4_full, inputs):
    global _STAGE_F2
    if _STAGE_F2 is None:
        _STAGE_F2 = build_stage_f2()
    wd = np.ascontiguousarray(np.asarray(inputs["L1_ffn_w_down"], np.float32))
    in_maps = []
    for c in range(NCORES):
        x4c = np.ascontiguousarray(x4_full[c * BPC:(c + 1) * BPC].reshape(TPC, C))
        in_maps.append(dict(o_in=np.ascontiguousarray(o_shards[c]), x4=x4c, wd_nat=wd))
    res = run_bass_kernel_spmd(_STAGE_F2, in_maps, core_ids=list(range(NCORES)))
    y = np.stack([res.results[c]["y_out"].reshape(BPC, N, C)
                  for c in range(NCORES)]).reshape(B, N, C)
    return y


def kernel(**inputs):
    x2, probs = run_stage_a(inputs)
    x3 = run_stage_b1(x2, probs, inputs)
    x4, m0, mu4, rstd4 = run_stage_b2(x3, inputs)
    m12, wrelay = run_stage_l1ffn("c", x4, mu4, rstd4, [m0], inputs)
    m1 = m12[:, 0]
    m2 = m12[:, 1]
    m3 = run_stage_l1ffn("d", x4, mu4, rstd4, [m0, m1, m2], inputs,
                         wrelay)[:, 0]
    m4 = run_stage_l1ffn("e", x4, mu4, rstd4, [m0, m1, m2, m3], inputs,
                         wrelay)[:, 0]
    o = run_stage_l1ffn("f1", x4, mu4, rstd4, [m0, m1, m2, m3, m4], inputs,
                        wrelay)
    y = run_stage_f2(o, x4, inputs)
    return y


# version 05 (final)
# speedup vs baseline: 1.3707x; 1.1135x over previous
import numpy as np
import ml_dtypes
from contextlib import ExitStack

import concourse.bass as bass
import concourse.mybir as mybir
from concourse.bass_utils import run_bass_kernel_spmd

F32 = mybir.dt.float32
BF16 = mybir.dt.bfloat16
I32 = mybir.dt.int32
I16 = mybir.dt.int16
AF = mybir.ActivationFunctionType
ALU = mybir.AluOpType
AX = mybir.AxisListType

MAGIC = float(np.float32(1.5 * 2 ** 23))
TWO_PI_HI = float(np.float32(2.0 * np.pi))
TWO_PI_LO = float(np.float32(2.0 * np.pi - np.float64(np.float32(2.0 * np.pi))))
INV_2PI = float(np.float32(1.0 / (2.0 * np.pi)))
PIO2_HI = float(np.float32(0.5 * np.pi))
PIO2_LO = float(np.float32(0.5 * np.pi - np.float64(np.float32(0.5 * np.pi))))
PI_F = float(np.float32(np.pi))

NCORES = 8
B, N, C = 16, 2048, 1024
BPC = B // NCORES
TPC = BPC * N
NT = TPC // 128
NTS = N // 128
H, D = 16, 64
KCH = C // 128

BF = ml_dtypes.bfloat16


class Sem:
    def __init__(self, h):
        self.h = h
        self.n = 0


class Sched:

    ENGS = ("sync", "vector", "scalar", "tensor", "gpsimd", "gdma")
    AUTO = ("vector", "scalar", "gpsimd")

    def __init__(self, esems):
        self.ev = {e: [] for e in self.ENGS}
        self.esems = esems

    def step(self, eng, waits, fn, inc=None):
        waits = [(s, v) for (s, v) in waits if v > 0]
        ret = None
        if eng in self.AUTO:
            assert inc is None or inc[0] is self.esems[eng]
            es = self.esems[eng]
            if es.n > 0:
                waits.append((es, es.n))
            es.n += 1
            inc = (es, 1)
            ret = es.n
        elif inc is not None:
            sem, amt = inc
            sem.n += amt
            ret = sem.n

        def run(h, waits=tuple(waits), fn=fn, inc=inc):
            for s, v in waits:
                h.wait_ge(s.h, v)
            r = fn(h)
            if inc is not None:
                sem, amt = inc
                if r is None:
                    r = h.nop()
                r.then_inc(sem.h, amt)
        self.ev["gpsimd" if eng == "gdma" else eng].append(run)
        return ret

    def emit(self, block):
        merged = dict(self.ev)
        for eng in self.ENGS:
            if eng == "gdma":
                continue
            fns = self.ev[eng]
            if not fns:
                continue
            def make(fns):
                def body(h):
                    for f in fns:
                        f(h)
                return body
            getattr(block, eng)(make(fns))



def build_stage_a(phase=9, sub=9, ntiles=NTS):
    nc = bass.Bass("TRN2", target_bir_lowering=False)
    xA = nc.dram_tensor("xA", [TPC, C], F32, kind="ExternalInput").ap()
    wqT = nc.dram_tensor("wqT", [C, C], F32, kind="ExternalInput").ap()
    wkT = nc.dram_tensor("wkT", [C, C], F32, kind="ExternalInput").ap()
    wvT = nc.dram_tensor("wvT", [C, C], F32, kind="ExternalInput").ap()
    bqkv = nc.dram_tensor("bqkv", [3, C], F32, kind="ExternalInput").ap()
    gb = nc.dram_tensor("gb", [2, C], F32, kind="ExternalInput").ap()
    wg = nc.dram_tensor("wg", [C, 8], F32, kind="ExternalInput").ap()
    posT = nc.dram_tensor("posT", [1, N], F32, kind="ExternalInput").ap()
    jfrac = nc.dram_tensor("jfrac", [1, C // 2], F32, kind="ExternalInput").ap()
    x2_out = nc.dram_tensor("x2_out", [TPC, C], F32, kind="ExternalOutput").ap()
    probs_out = nc.dram_tensor("probs_out", [TPC, 8], F32, kind="ExternalOutput").ap()

    with ExitStack() as ctx:
        sbt = lambda n, s, d=F32: ctx.enter_context(nc.sbuf_tensor(n, s, d))
        pst = lambda n, s, d=F32: ctx.enter_context(nc.psum_tensor(n, s, d))

        wq_b = sbt("wq_b", [128, KCH * C], BF16)
        wk_b = sbt("wk_b", [128, KCH * C], BF16)
        wv_b = sbt("wv_b", [128, KCH * C], BF16)
        wg_b = sbt("wg_b", [128, KCH * 8], BF16)
        bq_b = sbt("bq_b", [1, C], BF16)
        bk_b = sbt("bk_b", [1, C], BF16)
        bv_b = sbt("bv_b", [1, C], BF16)
        bias_rows = [bq_b, bk_b, bv_b]
        gam_b = sbt("gam_b", [1, C], BF16)
        bet_b = sbt("bet_b", [1, C], BF16)
        gamB = sbt("gamB", [128, C], F32)
        betB = sbt("betB", [128, C], F32)
        ones_row = sbt("ones_row", [1, 128], BF16)
        ones_col = sbt("ones_col", [128, 1], BF16)
        ident = sbt("ident", [128, 128], BF16)
        cos_t = sbt("cos_t", [128, NTS * 512], BF16)
        sin_t = sbt("sin_t", [128, NTS * 512], BF16)
        posr = sbt("posr", [1, N], F32)
        jfr = sbt("jfr", [1, C // 2], F32)
        jl = sbt("jl", [1, C // 2], F32)
        invf_r = sbt("invf_r", [1, C // 2], F32)
        lnv = sbt("lnv", [1, 1], F32)
        tc1 = sbt("tc1", [128, 512], F32)
        tc2 = sbt("tc2", [128, 512], F32)

        mu1 = sbt("mu1", [128, NT], F32)
        rstd1 = sbt("rstd1", [128, NT], F32)
        sumc = sbt("sumc", [128, NT], F32)
        sqc = sbt("sqc", [128, NT], F32)
        sum2c = sbt("sum2c", [128, NT], F32)
        sq2c = sbt("sq2c", [128, NT], F32)
        mu2ln = sbt("mu2ln", [128, NT], F32)
        rstd2ln = sbt("rstd2ln", [128, NT], F32)

        xt = [sbt(f"xt{i}", [128, C]) for i in range(2)]
        xn = sbt("xn", [128, C])
        xn_b = sbt("xn_b", [128, C], BF16)
        xnT = sbt("xnT", [128, KCH * 128], BF16)
        kg = sbt("kg", [128, C])
        kg_b = sbt("kg_b", [128, C], BF16)
        kr_b = sbt("kr_b", [128, C], BF16)
        vg_b = sbt("vg_b", [128, C], BF16)
        qg = sbt("qg", [128, C])
        qg_b = sbt("qg_b", [128, C], BF16)
        qr_b = sbt("qr_b", [128, C], BF16)
        qrT = sbt("qrT", [64, H * 128], BF16)
        rp1 = sbt("rp1", [128, 512], BF16)
        rp2 = sbt("rp2", [128, 512], BF16)
        kv_s = sbt("kv_s", [64, H * 64], BF16)
        ksum_row = sbt("ksum_row", [1, C], F32)
        kmb_row = sbt("kmb_row", [1, C], BF16)
        kmeanB = sbt("kmeanB", [128, C], BF16)
        zd = sbt("zd", [128, C], BF16)
        zcol = sbt("zcol", [128, H], F32)
        zinv = sbt("zinv", [128, H], F32)
        res2 = sbt("res2", [128, C], F32)
        scr = sbt("scr", [128, C], F32)
        scr2 = sbt("scr2", [128, C], F32)
        rs_cols = sbt("rs_cols", [128, H], F32)
        colA = sbt("colA", [128, 1], F32)
        colB = sbt("colB", [128, 1], F32)
        colC = sbt("colC", [128, 1], F32)
        colD = sbt("colD", [128, 1], F32)
        x2t = [sbt(f"x2t{i}", [128, C]) for i in range(2)]
        probs_t = sbt("probs_t", [128, 8], F32)
        eps5_col = sbt("eps5_col", [128, 1], F32)
        eps12_col = sbt("eps12_col", [128, 1], F32)
        negmax = sbt("negmax", [128, 1], F32)
        smsum = sbt("smsum", [128, 1], F32)

        ps_t = [pst(f"ps_t{i}", [128, 128], BF16) for i in range(3)]
        ps_a = pst("ps_a", [128, 512])
        ps_b = pst("ps_b", [128, 512])
        ps_e = pst("ps_e", [128, 512])
        ps_f = pst("ps_f", [128, 512])
        ps_g = pst("ps_g", [128, 512])

        S = {nm: Sem(ctx.enter_context(nc.semaphore(nm)))
             for nm in ("LD", "LDB", "W", "V", "A", "T", "G", "ST", "STB", "STP")}
        (LD, LDB, W, V, A, T, G, ST, STB, STP) = (
            S[n] for n in ("LD", "LDB", "W", "V", "A", "T", "G", "ST", "STB", "STP"))
        LDp = [LD, LDB]
        STp = [ST, STB]
        CN = Sem(ctx.enter_context(nc.semaphore("CN")))
        block = ctx.enter_context(nc.Block())
        sch = Sched({"vector": V, "scalar": A, "gpsimd": G})

        for cch in range(KCH):
            for dst, src in ((wq_b, wqT), (wk_b, wkT), (wv_b, wvT)):
                sch.step("gdma", [], lambda g, dst=dst, src=src, cch=cch: g.dma_start(
                    out=dst[:, cch * C:(cch + 1) * C],
                    in_=src[cch * 128:(cch + 1) * 128, :]), inc=(W, 16))
            sch.step("gdma", [], lambda g, cch=cch: g.dma_start(
                out=wg_b[:, cch * 8:(cch + 1) * 8],
                in_=wg[cch * 128:(cch + 1) * 128, :]), inc=(W, 16))
        for i, row in enumerate(bias_rows):
            sch.step("gdma", [], lambda g, i=i, row=row: g.dma_start(
                out=row[:], in_=bqkv[i:i + 1, :]), inc=(W, 16))
        sch.step("gdma", [], lambda g: g.dma_start(out=gam_b[:], in_=gb[0:1, :]), inc=(W, 16))
        sch.step("gdma", [], lambda g: g.dma_start(out=bet_b[:], in_=gb[1:2, :]), inc=(W, 16))
        sch.step("sync", [], lambda s: s.dma_start(out=posr[:], in_=posT), inc=(CN, 16))
        sch.step("sync", [], lambda s: s.dma_start(out=jfr[:], in_=jfrac), inc=(CN, 16))
        w_all = W.n
        w_all_ld = CN.n

        sch.step("vector", [], lambda v: v.memset(ones_row[:], 1.0))
        sch.step("vector", [], lambda v: v.memset(ones_col[:], 1.0))
        sch.step("vector", [], lambda v: v.memset(eps5_col[:], 1e-5))
        sch.step("vector", [], lambda v: v.memset(eps12_col[:], 1e-12))
        consts_v = sch.step("vector", [], lambda v: v.memset(lnv[:], 10000.0))
        id_ms = sch.step("gpsimd", [], lambda g: g.memset(ident[:], 1.0))
        id_done = sch.step("gpsimd", [(G, id_ms)], lambda g: g.affine_select(
            ident[:], ident[:], pattern=[[-1, 128]], compare_op=ALU.is_equal,
            fill=0.0, base=0, channel_multiplier=1))

        for half in range(2):
            for row, dstps in ((gam_b, ps_a), (bet_b, ps_b)):
                tmm = sch.step("tensor", [(W, w_all), (CN, w_all_ld), (V, consts_v)],
                               lambda t, row=row, half=half, dstps=dstps:
                               t.matmul(dstps[:, 0:512], ones_row[:],
                                        row[0:1, half * 512:(half + 1) * 512],
                                        start=True, stop=True), inc=(T, 1))
            sch.step("scalar", [(T, T.n)], lambda s, half=half: s.copy(
                gamB[:, half * 512:(half + 1) * 512], ps_a[:, 0:512]))
            sch.step("scalar", [], lambda s, half=half: s.copy(
                betB[:, half * 512:(half + 1) * 512], ps_b[:, 0:512]))
            if half == 0:
                ev = A.n
                sch.step("tensor", [(A, ev)], lambda t: None)

        gb_bcast_done = A.n

        lnv_done = sch.step("scalar", [(V, 1)], lambda s: s.activation(
            lnv[:], lnv[:], AF.Ln))
        jl_done = sch.step("vector", [(W, w_all), (CN, w_all_ld), (A, lnv_done)], lambda v: v.tensor_scalar(
            jl[:], jfr[:], lnv[0:1, 0:1], None, ALU.mult))
        invf_done = sch.step("scalar", [(V, jl_done)], lambda s: s.activation(
            invf_r[:], jl[:], AF.Exp, bias=0.0, scale=-1.0))
        pse_free = 0
        prev_s_done = 0
        prev_c_done = 0
        tc3 = sbt("tc3", [128, 512], F32)
        for chunk in range(NTS):
            mm = sch.step("tensor", [(A, invf_done), (V, pse_free)],
                          lambda t, chunk=chunk: t.matmul(
                              ps_e[:, 0:512],
                              posr[0:1, chunk * 128:(chunk + 1) * 128],
                              invf_r[0:1, :], start=True, stop=True), inc=(T, 1))
            sch.step("vector", [(T, mm)], lambda v: v.tensor_scalar(
                tc1[:], ps_e[:, 0:512], INV_2PI, None, ALU.mult))
            sch.step("vector", [], lambda v: v.tensor_scalar(
                tc1[:], tc1[:], MAGIC, MAGIC, ALU.add, ALU.subtract))
            sch.step("vector", [(A, prev_s_done)], lambda v: v.scalar_tensor_tensor(
                tc2[:], tc1[:], -TWO_PI_HI, ps_e[:, 0:512], ALU.mult, ALU.add))
            sch.step("vector", [], lambda v: v.scalar_tensor_tensor(
                tc2[:], tc1[:], -TWO_PI_LO, tc2[:], ALU.mult, ALU.add))
            sch.step("vector", [], lambda v: v.tensor_single_scalar(
                tc1[:], tc2[:], PI_F, ALU.is_gt))
            sch.step("vector", [], lambda v: v.scalar_tensor_tensor(
                tc2[:], tc1[:], -TWO_PI_HI, tc2[:], ALU.mult, ALU.add))
            sch.step("vector", [], lambda v: v.tensor_single_scalar(
                tc1[:], tc2[:], -PI_F, ALU.is_lt))
            sin_in = sch.step("vector", [], lambda v: v.scalar_tensor_tensor(
                tc2[:], tc1[:], TWO_PI_HI, tc2[:], ALU.mult, ALU.add))
            s_done = sch.step("scalar", [(V, sin_in)], lambda s, chunk=chunk: s.activation(
                sin_t[:, chunk * 512:(chunk + 1) * 512], tc2[:], AF.Sin))
            prev_s_done = s_done
            pse_free = sch.step("vector", [(A, prev_c_done)], lambda v: v.tensor_scalar(
                tc3[:], ps_e[:, 0:512], PIO2_HI, None, ALU.add))
            sch.step("vector", [], lambda v: v.tensor_scalar(
                tc1[:], tc3[:], INV_2PI, None, ALU.mult))
            sch.step("vector", [], lambda v: v.tensor_scalar(
                tc1[:], tc1[:], MAGIC, MAGIC, ALU.add, ALU.subtract))
            sch.step("vector", [], lambda v: v.scalar_tensor_tensor(
                tc3[:], tc1[:], -TWO_PI_HI, tc3[:], ALU.mult, ALU.add))
            sch.step("vector", [], lambda v: v.scalar_tensor_tensor(
                tc3[:], tc1[:], -TWO_PI_LO, tc3[:], ALU.mult, ALU.add))
            sch.step("vector", [], lambda v: v.tensor_scalar(
                tc3[:], tc3[:], PIO2_LO, None, ALU.add))
            sch.step("vector", [], lambda v: v.tensor_single_scalar(
                tc1[:], tc3[:], PI_F, ALU.is_gt))
            sch.step("vector", [], lambda v: v.scalar_tensor_tensor(
                tc3[:], tc1[:], -TWO_PI_HI, tc3[:], ALU.mult, ALU.add))
            sch.step("vector", [], lambda v: v.tensor_single_scalar(
                tc1[:], tc3[:], -PI_F, ALU.is_lt))
            cos_in = sch.step("vector", [(A, s_done)], lambda v: v.scalar_tensor_tensor(
                tc3[:], tc1[:], TWO_PI_HI, tc3[:], ALU.mult, ALU.add))
            c_done = sch.step("scalar", [(V, cos_in)], lambda s, chunk=chunk: s.activation(
                cos_t[:, chunk * 512:(chunk + 1) * 512], tc3[:], AF.Sin))
            prev_c_done = c_done
        tables_done_a = A.n
        tables_done_v = V.n

        x_rel = []
        for ti in range(NT):
            w = [(V, x_rel[ti - 2])] if ti >= 2 else []
            ld = sch.step("sync", w, lambda s, ti=ti: s.dma_start(
                out=xt[ti % 2][:], in_=xA[ti * 128:(ti + 1) * 128, :]), inc=(LDp[ti % 2], 16))
            sch.step("vector", [(LDp[ti % 2], ld)], lambda v, ti=ti: v.tensor_reduce(
                sumc[:, ti:ti + 1], xt[ti % 2][:], AX.X, ALU.add))
            rel = sch.step("vector", [], lambda v, ti=ti: v.scalar_tensor_tensor(
                scr[:], xt[ti % 2][:], 0.0, xt[ti % 2][:], ALU.add, ALU.mult,
                accum_out=sqc[:, ti:ti + 1]))
            x_rel.append(rel)
        sch.step("vector", [], lambda v: v.tensor_scalar(
            mu1[:], sumc[:], 1.0 / C, None, ALU.mult))
        sch.step("vector", [], lambda v: v.tensor_scalar(
            sqc[:], sqc[:], 1.0 / C, None, ALU.mult))
        sch.step("vector", [], lambda v: v.tensor_tensor(
            scr[:, 0:NT], mu1[:], mu1[:], ALU.mult))
        var_done = sch.step("vector", [], lambda v: v.tensor_tensor(
            sqc[:], sqc[:], scr[:, 0:NT], ALU.subtract))
        std_done = sch.step("scalar", [(V, var_done)], lambda s: s.activation(
            rstd1[:], sqc[:], AF.Sqrt, bias=eps5_col[:, 0:1], scale=1.0))
        stats1_done = sch.step("vector", [(A, std_done)], lambda v: v.reciprocal(
            rstd1[:], rstd1[:]))

        psbf_last = [0]

        def transposes(src_b, dst, nchunk, width, wait_v, base_tag):
            evA = []
            for c2 in range(nchunk):
                slot = c2 % 3
                waits = [(V, wait_v)] if c2 == 0 else []
                if c2 >= 3:
                    waits.append((A, evA[c2 - 3]))
                if c2 == 0:
                    waits.append((G, id_done))
                    waits.append((A, psbf_last[0]))
                sch.step("tensor", waits, lambda t, c2=c2, slot=slot: t.transpose(
                    ps_t[slot][0:width, 0:128],
                    src_b[:, c2 * width:(c2 + 1) * width], ident[:]), inc=(T, 1))
                ev = sch.step("scalar", [(T, T.n)], lambda s, c2=c2, slot=slot: s.copy(
                    dst[0:width, c2 * 128:(c2 + 1) * 128],
                    ps_t[slot][0:width, 0:128]))
                evA.append(ev)
            psbf_last[0] = evA[-1]
            return evA[-1]

        def proj(dst_ps, wmat, bias_row, xnT_done):
            last = None
            for half in range(2):
                ps = dst_ps[half]
                for c2 in range(KCH):
                    sch.step("tensor", [(A, xnT_done)] if c2 == 0 and half == 0 else [],
                             lambda t, ps=ps, c2=c2, half=half: t.matmul(
                                 ps[:, 0:512],
                                 xnT[:, c2 * 128:(c2 + 1) * 128],
                                 wmat[:, c2 * C + half * 512: c2 * C + half * 512 + 512],
                                 start=(c2 == 0), stop=False))
                brow = bias_rows[bias_row]
                last = sch.step("tensor", [], lambda t, ps=ps, half=half, brow=brow: t.matmul(
                    ps[:, 0:512], ones_row[:],
                    brow[0:1, half * 512:(half + 1) * 512],
                    start=False, stop=True), inc=(T, 1))
            return last

        def rope(src_b, dst_b, tile_in_seq, wait_v):
            co = cos_t[:, tile_in_seq * 512:(tile_in_seq + 1) * 512]
            si = sin_t[:, tile_in_seq * 512:(tile_in_seq + 1) * 512]
            x1 = src_b[:, 0:C:2]
            x2 = src_b[:, 1:C:2]
            o1 = dst_b[:, 0:C:2]
            o2 = dst_b[:, 1:C:2]
            sch.step("vector", [(V, wait_v)], lambda v: v.tensor_tensor(rp1[:], x1, co, ALU.mult))
            sch.step("vector", [], lambda v: v.tensor_tensor(rp2[:], x2, si, ALU.mult))
            sch.step("vector", [], lambda v: v.tensor_tensor(o1, rp1[:], rp2[:], ALU.subtract))
            sch.step("vector", [], lambda v: v.tensor_tensor(rp1[:], x2, co, ALU.mult))
            sch.step("vector", [], lambda v: v.tensor_tensor(rp2[:], x1, si, ALU.mult))
            return sch.step("vector", [], lambda v: v.tensor_tensor(
                o2, rp1[:], rp2[:], ALU.add))

        x2_store = {}
        for s_ix in range(BPC if phase >= 3 else (1 if phase >= 1 else 0)):
            ksum_reset = sch.step("vector", [], lambda v: v.memset(ksum_row[:], 0.0))
            for t_ix in range(min(NTS, ntiles)):
                ti = s_ix * NTS + t_ix
                w = [(V, x_rel[-2])] if len(x_rel) >= 2 else []
                ld = sch.step("sync", w, lambda s, ti=ti: s.dma_start(
                    out=xt[ti % 2][:], in_=xA[ti * 128:(ti + 1) * 128, :]), inc=(LDp[ti % 2], 16))
                nrm = sch.step("vector", [(LDp[ti % 2], ld), (V, stats1_done)],
                               lambda v, ti=ti: v.tensor_scalar(
                                   xn_b[:], xt[ti % 2][:], mu1[:, ti:ti + 1],
                                   rstd1[:, ti:ti + 1], ALU.subtract, ALU.mult))
                x_rel.append(nrm)
                if sub < 1:
                    continue
                xnT_done = transposes(xn_b, xnT, KCH, 128, nrm, f"p1t{ti}")
                if sub < 2:
                    continue
                kproj = proj((ps_a, ps_b), wk_b, 1, xnT_done)
                kg_done = sch.step("scalar", [(T, kproj)], lambda s: s.activation(
                    kg[:, 0:512], ps_a[:, 0:512], AF.Gelu))
                kg_done = sch.step("scalar", [], lambda s: s.activation(
                    kg[:, 512:1024], ps_b[:, 0:512], AF.Gelu))
                vproj = proj((ps_a, ps_b), wv_b, 2, kg_done)
                vg_done = sch.step("scalar", [(T, vproj)], lambda s: s.copy(
                    vg_b[:, 0:512], ps_a[:, 0:512]))
                vg_done = sch.step("scalar", [], lambda s: s.copy(
                    vg_b[:, 512:1024], ps_b[:, 0:512]))
                kgb_done = sch.step("vector", [(A, kg_done)], lambda v: v.tensor_scalar(
                    kg_b[:], kg[:], 0.21, None, ALU.add))
                if sub < 3:
                    continue
                kr_done = rope(kg_b, kr_b, t_ix, kgb_done)
                if sub < 4:
                    continue
                def kv_mm(t, t_ix=t_ix):
                    r = None
                    for h in range(H):
                        psd = ps_e if h < 8 else ps_f
                        r = t.matmul(psd[0:64, (h % 8) * 64:(h % 8) * 64 + 64],
                                     kr_b[:, h * D:(h + 1) * D],
                                     vg_b[:, h * D:(h + 1) * D],
                                     start=(t_ix == 0 and h % 8 == 0),
                                     stop=(t_ix == NTS - 1 and h % 8 == 7))
                    return r
                kvw = [(V, kr_done), (A, vg_done)]
                if t_ix == 0 and s_ix > 0:
                    kvw.append((A, kv_evac_prev))
                    kvw.append((V, ksum_reset))
                kv_done = sch.step("tensor", kvw, kv_mm, inc=(T, 1))
                if sub < 5:
                    continue
                ks1 = sch.step("tensor", [(V, kgb_done), (T, kv_done)], lambda t: t.matmul(
                    ps_g[0:1, 0:512], ones_col[:], kg_b[:, 0:512],
                    start=True, stop=True), inc=(T, 1))
                ksa = sch.step("vector", [(T, ks1)], lambda v: v.scalar_tensor_tensor(
                    ksum_row[0:1, 0:512], ps_g[0:1, 0:512], 0.0,
                    ksum_row[0:1, 0:512], ALU.add, ALU.add))
                ks2 = sch.step("tensor", [(V, ksa)], lambda t: t.matmul(
                    ps_g[0:1, 0:512], ones_col[:], kg_b[:, 512:1024],
                    start=True, stop=True), inc=(T, 1))
                ksb = sch.step("vector", [(T, ks2)], lambda v: v.scalar_tensor_tensor(
                    ksum_row[0:1, 512:1024], ps_g[0:1, 0:512], 0.0,
                    ksum_row[0:1, 512:1024], ALU.add, ALU.add))
            if sub < 5 or ntiles < NTS:
                break
            kv_ev = sch.step("scalar", [(T, kv_done)], lambda s: s.activation(
                kv_s[0:64, 0:512], ps_e[0:64, 0:512],
                AF.Copy, bias=0.0, scale=1.0 / N))
            kv_ev = sch.step("scalar", [], lambda s: s.activation(
                kv_s[0:64, 512:1024], ps_f[0:64, 0:512],
                AF.Copy, bias=0.0, scale=1.0 / N))
            km_ev = kv_ev
            kmb_done = sch.step("vector", [(A, km_ev), (V, ksb)], lambda v: v.tensor_scalar(
                kmb_row[:], ksum_row[:], 1.0 / N, None, ALU.mult))
            for half in range(2):
                psd = (ps_a, ps_b)[half]
                sch.step("tensor", [(V, kmb_done)], lambda t, half=half, psd=psd: t.matmul(
                    psd[:, 0:512], ones_row[:],
                    kmb_row[0:1, half * 512:(half + 1) * 512],
                    start=True, stop=True), inc=(T, 1))
            kmB_done = sch.step("scalar", [(T, T.n)], lambda s: (
                s.copy(kmeanB[:, 0:512], ps_a[:, 0:512]),
                s.copy(kmeanB[:, 512:1024], ps_b[:, 0:512]))[-1])
            kv_evac_prev = kmB_done

            for t_ix in range(NTS if phase >= 2 else 0):
                ti = s_ix * NTS + t_ix
                w = [(V, x_rel[-2])] if len(x_rel) >= 2 else []
                ld = sch.step("sync", w, lambda s, ti=ti: s.dma_start(
                    out=xt[ti % 2][:], in_=xA[ti * 128:(ti + 1) * 128, :]), inc=(LDp[ti % 2], 16))
                nrm = sch.step("vector", [(LDp[ti % 2], ld)], lambda v, ti=ti: v.tensor_scalar(
                    xn[:], xt[ti % 2][:], mu1[:, ti:ti + 1], rstd1[:, ti:ti + 1],
                    ALU.subtract, ALU.mult))
                nrm = sch.step("vector", [], lambda v: v.tensor_copy(xn_b[:], xn[:]),
                               inc=(V, 1))
                xnT_done = transposes(xn_b, xnT, KCH, 128, nrm, f"p2t{ti}")
                qproj = proj((ps_a, ps_b), wq_b, 0, xnT_done)
                qg_done = sch.step("scalar", [(T, qproj)], lambda s: s.activation(
                    qg[:, 0:512], ps_a[:, 0:512], AF.Gelu))
                qg_done = sch.step("scalar", [], lambda s: s.activation(
                    qg[:, 512:1024], ps_b[:, 0:512], AF.Gelu))
                qgb_done = sch.step("vector", [(A, qg_done)], lambda v: v.tensor_scalar(
                    qg_b[:], qg[:], 0.21, None, ALU.add))
                sch.step("vector", [(A, kmB_done)], lambda v: v.tensor_tensor(
                    zd[:], qg_b[:], kmeanB[:], ALU.mult))
                sch.step("vector", [], lambda v: v.tensor_reduce(
                    zcol[:], zd[:].rearrange("p (h d) -> p h d", d=D), AX.X, ALU.add))
                sch.step("vector", [], lambda v: v.tensor_scalar(
                    zinv[:], zcol[:], 1e-6, None, ALU.add))
                z_done = sch.step("vector", [], lambda v: v.reciprocal(
                    zinv[:], zinv[:]))
                qr_done = rope(qg_b, qr_b, t_ix, qgb_done)
                evA = []
                for h in range(H):
                    slot = h % 3
                    waits = [(V, qr_done)] if h == 0 else []
                    if h >= 3:
                        waits.append((A, evA[h - 3]))
                    if h == 0:
                        waits.append((A, psbf_last[0]))
                    sch.step("tensor", waits, lambda t, h=h, slot=slot: t.transpose(
                        ps_t[slot][0:64, 0:128],
                        qr_b[:, h * D:(h + 1) * D], ident[:]), inc=(T, 1))
                    ev = sch.step("scalar", [(T, T.n)], lambda s, h=h, slot=slot: s.copy(
                        qrT[0:64, h * 128:(h + 1) * 128],
                        ps_t[slot][0:64, 0:128]))
                    evA.append(ev)
                qrT_done = evA[-1]
                psbf_last[0] = qrT_done
                def out_mm(t):
                    r = None
                    for h in range(H):
                        psd = ps_e if h < 8 else ps_f
                        off = (h % 8) * 64
                        r = t.matmul(psd[:, off:off + 64],
                                     qrT[0:64, h * 128:(h + 1) * 128],
                                     kv_s[0:64, h * 64:(h + 1) * 64],
                                     start=(h % 8 == 0), stop=(h % 8 == 7))
                    return r
                omm = sch.step("tensor", [(A, qrT_done)], out_mm, inc=(T, 1))
                def zres(v):
                    r = None
                    for h in range(H):
                        psd = ps_e if h < 8 else ps_f
                        off = (h % 8) * 64
                        r = v.scalar_tensor_tensor(
                            res2[:, h * D:(h + 1) * D], psd[:, off:off + 64],
                            zinv[:, h:h + 1], xn[:, h * D:(h + 1) * D],
                            ALU.mult, ALU.add, accum_out=rs_cols[:, h:h + 1])
                    return r
                res_done = sch.step("vector", [(T, omm), (V, z_done)], zres)
                sch.step("vector", [], lambda v: v.tensor_reduce(
                    colA[:], rs_cols[:], AX.X, ALU.add))
                sq_done = sch.step("vector", [], lambda v: v.scalar_tensor_tensor(
                    scr[:], res2[:], 0.0, res2[:], ALU.add, ALU.mult,
                    accum_out=colB[:]))
                sch.step("vector", [], lambda v: v.tensor_scalar(
                    colA[:], colA[:], 1.0 / C, None, ALU.mult))
                sch.step("vector", [], lambda v: v.tensor_scalar(
                    colB[:], colB[:], 1.0 / C, None, ALU.mult))
                sch.step("vector", [], lambda v: v.tensor_tensor(
                    colC[:], colA[:], colA[:], ALU.mult))
                var2_done = sch.step("vector", [], lambda v: v.tensor_tensor(
                    colB[:], colB[:], colC[:], ALU.subtract))
                sd2 = sch.step("scalar", [(V, var2_done)], lambda s: s.activation(
                    colD[:], colB[:], AF.Sqrt, bias=eps12_col[:, 0:1], scale=1.0))
                rs2 = sch.step("vector", [(A, sd2)], lambda v: v.reciprocal(
                    colD[:], colD[:]))
                sch.step("vector", [], lambda v: v.tensor_scalar(
                    scr2[:], res2[:], colA[:, 0:1], colD[:, 0:1], ALU.subtract, ALU.mult))
                sch.step("vector", [(A, gb_bcast_done)], lambda v: v.tensor_tensor(
                    scr2[:], scr2[:], gamB[:], ALU.mult))
                sch.step("vector", [], lambda v: v.tensor_tensor(
                    scr2[:], scr2[:], betB[:], ALU.add))
                x2w = [(STp[ti % 2], x2_store[ti - 2])] if (ti - 2) in x2_store else []
                x2_done = sch.step("vector", x2w, lambda v, ti=ti: v.tensor_tensor(
                    x2t[ti % 2][:], scr2[:], xt[ti % 2][:], ALU.add))
                x_rel.append(x2_done)
                sch.step("vector", [], lambda v, ti=ti: v.tensor_reduce(
                    sum2c[:, ti:ti + 1], x2t[ti % 2][:], AX.X, ALU.add))
                st2 = sch.step("vector", [], lambda v, ti=ti: v.scalar_tensor_tensor(
                    scr[:], x2t[ti % 2][:], 0.0, x2t[ti % 2][:], ALU.add, ALU.mult,
                    accum_out=sq2c[:, ti:ti + 1]))
                stv = sch.step("sync", [(V, x2_done)], lambda s, ti=ti: s.dma_start(
                    out=x2_out[ti * 128:(ti + 1) * 128, :], in_=x2t[ti % 2][:]),
                    inc=(STp[ti % 2], 16))
                x2_store[ti] = stv

        sch.step("vector", [], lambda v: v.tensor_scalar(
            mu2ln[:], sum2c[:], 1.0 / C, None, ALU.mult))
        sch.step("vector", [], lambda v: v.tensor_scalar(
            sq2c[:], sq2c[:], 1.0 / C, None, ALU.mult))
        sch.step("vector", [], lambda v: v.tensor_tensor(
            scr[:, 0:NT], mu2ln[:], mu2ln[:], ALU.mult))
        v2_done = sch.step("vector", [], lambda v: v.tensor_tensor(
            sq2c[:], sq2c[:], scr[:, 0:NT], ALU.subtract))
        sd_ln2 = sch.step("scalar", [(V, v2_done)], lambda s: s.activation(
            rstd2ln[:], sq2c[:], AF.Sqrt, bias=eps5_col[:, 0:1], scale=1.0))
        stats2_done = sch.step("vector", [(A, sd_ln2)], lambda v: v.reciprocal(
            rstd2ln[:], rstd2ln[:]))

        for ti in range(NT if phase >= 4 else 0):
            w = [(V, x_rel[-2])] if len(x_rel) >= 2 else []
            w = w + [(ST, ST.n), (STB, STB.n)]
            ld = sch.step("sync", w, lambda s, ti=ti: s.dma_start(
                out=xt[ti % 2][:], in_=x2_out[ti * 128:(ti + 1) * 128, :]), inc=(LDp[ti % 2], 16))
            nrm = sch.step("vector", [(LDp[ti % 2], ld), (V, stats2_done)],
                           lambda v, ti=ti: v.tensor_scalar(
                               xn_b[:], xt[ti % 2][:], mu2ln[:, ti:ti + 1],
                               rstd2ln[:, ti:ti + 1], ALU.subtract, ALU.mult))
            x_rel.append(nrm)
            xnT_done = transposes(xn_b, xnT, KCH, 128, nrm, f"rt{ti}")
            def lg_mm(t):
                r = None
                for c2 in range(KCH):
                    r = t.matmul(ps_a[:, 0:8], xnT[:, c2 * 128:(c2 + 1) * 128],
                                 wg_b[:, c2 * 8:(c2 + 1) * 8],
                                 start=(c2 == 0), stop=(c2 == KCH - 1))
                return r
            lg = sch.step("tensor", [(A, xnT_done)], lg_mm, inc=(T, 1))
            sch.step("vector", [(T, lg)], lambda v: v.tensor_reduce(
                negmax[:], ps_a[:, 0:8], AX.X, ALU.max))
            ngm = sch.step("vector", [], lambda v: v.tensor_scalar(
                negmax[:], negmax[:], -1.0, None, ALU.mult))
            exw = [(V, ngm)]
            if ti > 0:
                exw.append((STP, pr_store))
            ex = sch.step("scalar", exw, lambda s: s.activation(
                probs_t[:], ps_a[:, 0:8], AF.Exp, bias=negmax[:, 0:1], scale=1.0,
                accum_out=smsum[:, 0:1]))
            sch.step("vector", [(A, ex)], lambda v: v.reciprocal(smsum[:], smsum[:]))
            pr = sch.step("vector", [], lambda v: v.tensor_scalar(
                probs_t[:], probs_t[:], smsum[:, 0:1], None, ALU.mult))
            pr_store = sch.step("sync", [(V, pr)], lambda s, ti=ti: s.dma_start(
                out=probs_out[ti * 128:(ti + 1) * 128, :], in_=probs_t[:]), inc=(STP, 16))

        sch.step("gpsimd", [(ST, ST.n), (STB, STB.n), (STP, STP.n)],
                 lambda g: None)

        sch.emit(block)
    return nc



def build_stage_b1():
    nc = bass.Bass("TRN2", target_bir_lowering=False)
    x2 = nc.dram_tensor("x2", [TPC, C], F32, kind="ExternalInput").ap()
    probs_all = nc.dram_tensor("probs_all", [B * N, 8], F32, kind="ExternalInput").ap()
    probs_loc = nc.dram_tensor("probs_loc", [TPC, 8], F32, kind="ExternalInput").ap()
    w1_in = nc.dram_tensor("w1_in", [8, C, 2048], F32, kind="ExternalInput").ap()
    w2_in = nc.dram_tensor("w2_in", [8, 2048, C], F32, kind="ExternalInput").ap()
    x3_out = nc.dram_tensor("x3_out", [TPC, C], F32, kind="ExternalOutput").ap()

    NHALF = 2
    TPH = NT // NHALF
    NFC = 16
    CAP = float(B * N // 8)

    with ExitStack() as ctx:
        sbt = lambda n, s, d=F32: ctx.enter_context(nc.sbuf_tensor(n, s, d))
        pst = lambda n, s, d=F32: ctx.enter_context(nc.psum_tensor(n, s, d))

        probs_il = sbt("probs_il", [128, 2048])
        ones_rowf = sbt("ones_rowf", [1, 128], F32)
        ones_colf = sbt("ones_colf", [128, 1], F32)
        ident = sbt("ident", [128, 128], BF16)
        lo_r = sbt("lo_r", [1, 8])
        hi_r = sbt("hi_r", [1, 8])
        mid_r = sbt("mid_r", [1, 8])
        cnt_r = sbt("cnt_r", [1, 8])
        cge_r = sbt("cge_r", [1, 8])
        dd_r = sbt("dd_r", [1, 8])
        ge_t = sbt("ge_t", [128, 2048], BF16)
        colr8 = sbt("colr8", [128, 8])
        thrB = sbt("thrB", [128, 8])
        gates_all = sbt("gates_all", [128, NT * 8])
        p_t = sbt("p_t", [128, 8])
        g_m1 = sbt("g_m1", [128, 8])
        g_msk = sbt("g_msk", [128, 8])
        g_t1 = sbt("g_t1", [128, 8])
        g_u = sbt("g_u", [128, 8])
        g_eq = sbt("g_eq", [128, 8])
        g_ind = sbt("g_ind", [128, 8])
        colA = sbt("colA", [128, 1])
        colP = sbt("colP", [128, 1])
        colW = sbt("colW", [128, 1])
        cola = sbt("cola", [128, 1])
        colna = sbt("colna", [128, 1])
        colpref = sbt("colpref", [128, 1])
        colt = sbt("colt", [128, 1])
        mu2 = sbt("mu2", [128, NT])
        rstd2 = sbt("rstd2", [128, NT])
        sumc = sbt("sumc", [128, NT])
        sqc = sbt("sqc", [128, NT])
        scr_nt = sbt("scr_nt", [128, NT])
        eps5_col = sbt("eps5_col", [128, 1])
        xt = [sbt(f"xt{i}", [128, C]) for i in range(2)]
        x3t = [sbt(f"x3t{i}", [128, C]) for i in range(2)]
        xn_b = sbt("xn_b", [128, C], BF16)
        scr = sbt("scr", [128, C])
        xnT_half = sbt("xnT_half", [128, KCH * (TPH * 128)], BF16)
        out_acc = sbt("out_acc", [128, TPH * C])
        w1_b = sbt("w1_b", [128, KCH * 2048], BF16)
        w2_b = sbt("w2_b", [128, NFC * C], BF16)
        a_fm = sbt("a_fm", [128, NFC * 256], BF16)

        ps_t = [pst(f"ps_t{i}", [128, 128], BF16) for i in range(3)]
        ps_q = pst("ps_q", [128, 512])
        ps_s = pst("ps_s", [128, 512])
        ps_h1 = pst("ps_h1", [128, 512])
        ps_h2 = pst("ps_h2", [128, 512])
        ps_g = pst("ps_g", [128, 512])

        S = {nm: Sem(ctx.enter_context(nc.semaphore(nm)))
             for nm in ("LD", "LDB", "W", "WA", "WB", "WP", "V", "A", "T", "G", "ST", "STB")}
        (LD, LDB, W, WA, WB, WP, V, A, T, G, ST, STB) = (
            S[n] for n in ("LD", "LDB", "W", "WA", "WB", "WP", "V", "A", "T", "G", "ST", "STB"))
        LDp = [LD, LDB]
        STp = [ST, STB]
        block = ctx.enter_context(nc.Block())
        sch = Sched({"vector": V, "scalar": A, "gpsimd": G})

        sch.step("sync", [], lambda s: s.dma_start(
            out=probs_il[:], in_=probs_all.rearrange("(p r) e -> p r e", p=128)),
            inc=(W, 16))
        w_all = W.n
        consts_v = sch.step("vector", [], lambda v: (
            v.memset(ones_rowf[:], 1.0), v.memset(ones_colf[:], 1.0),
            v.memset(eps5_col[:], 1e-5), v.memset(colpref[:], 0.0),
            v.memset(lo_r[:], 0.0), v.memset(hi_r[:], 2.0))[-1])
        id_ms = sch.step("gpsimd", [], lambda g: g.memset(ident[:], 1.0))
        id_done = sch.step("gpsimd", [(G, id_ms)], lambda g: g.affine_select(
            ident[:], ident[:], pattern=[[-1, 128]], compare_op=ALU.is_equal,
            fill=0.0, base=0, channel_multiplier=1))

        x_rel = []
        for ti in range(NT):
            w = [(V, x_rel[-2])] if len(x_rel) >= 2 else []
            ld = sch.step("sync", w, lambda s, ti=ti: s.dma_start(
                out=xt[ti % 2][:], in_=x2[ti * 128:(ti + 1) * 128, :]),
                inc=(LDp[ti % 2], 16))
            sch.step("vector", [(LDp[ti % 2], ld)], lambda v, ti=ti: v.tensor_reduce(
                sumc[:, ti:ti + 1], xt[ti % 2][:], AX.X, ALU.add))
            rel = sch.step("vector", [], lambda v, ti=ti: v.scalar_tensor_tensor(
                scr[:], xt[ti % 2][:], 0.0, xt[ti % 2][:], ALU.add, ALU.mult,
                accum_out=sqc[:, ti:ti + 1]))
            x_rel.append(rel)
        sch.step("vector", [], lambda v: v.tensor_scalar(
            mu2[:], sumc[:], 1.0 / C, None, ALU.mult))
        sch.step("vector", [], lambda v: v.tensor_scalar(
            sqc[:], sqc[:], 1.0 / C, None, ALU.mult))
        sch.step("vector", [], lambda v: v.tensor_tensor(
            scr_nt[:], mu2[:], mu2[:], ALU.mult))
        var_done = sch.step("vector", [], lambda v: v.tensor_tensor(
            sqc[:], sqc[:], scr_nt[:], ALU.subtract))
        std_done = sch.step("scalar", [(V, var_done)], lambda s: s.activation(
            rstd2[:], sqc[:], AF.Sqrt, bias=eps5_col[:, 0:1], scale=1.0))
        stats_done = sch.step("vector", [(A, std_done)], lambda v: v.reciprocal(
            rstd2[:], rstd2[:]))

        prev_ge_v = 0
        for it in range(40):
            w0 = [(W, w_all)] if it == 0 else []
            sch.step("vector", w0, lambda v: v.tensor_tensor(
                mid_r[:], lo_r[:], hi_r[:], ALU.add))
            mid_done = sch.step("vector", [], lambda v: v.tensor_scalar(
                mid_r[:], mid_r[:], 0.5, None, ALU.mult))
            mid_rep = bass.AP(mid_r[:].tensor, 0, [[8, 1], [0, 64], [1, 8]])
            for blk in range(4):
                mm = sch.step("tensor", [(V, mid_done)], lambda t: t.matmul(
                    ps_g[:, 0:512], ones_rowf[:], mid_rep,
                    start=True, stop=True), inc=(T, 1))
                geb = sch.step("vector", [(T, mm)], lambda v, blk=blk: v.tensor_tensor(
                    ge_t[:, blk * 512:(blk + 1) * 512],
                    probs_il[:, blk * 512:(blk + 1) * 512],
                    ps_g[:, 0:512], ALU.is_ge))
                if blk < 3:
                    sch.step("tensor", [(V, geb)], lambda t: None)
            red = sch.step("vector", [], lambda v: v.tensor_reduce(
                colr8[:], ge_t[:].rearrange("p (r e) -> p e r", e=8), AX.X, ALU.add))
            cm = sch.step("tensor", [(V, red)], lambda t: t.matmul(
                ps_g[0:1, 0:8], ones_colf[:], colr8[:], start=True, stop=True),
                inc=(T, 1))
            sch.step("vector", [(T, cm)], lambda v: v.tensor_copy(
                cnt_r[:], ps_g[0:1, 0:8]))
            sch.step("vector", [], lambda v: v.tensor_single_scalar(
                cge_r[:], cnt_r[:], CAP, ALU.is_ge))
            sch.step("vector", [], lambda v: v.tensor_tensor(
                dd_r[:], mid_r[:], lo_r[:], ALU.subtract))
            sch.step("vector", [], lambda v: v.tensor_tensor(
                dd_r[:], dd_r[:], cge_r[:], ALU.mult))
            sch.step("vector", [], lambda v: v.tensor_tensor(
                lo_r[:], lo_r[:], dd_r[:], ALU.add))
            sch.step("vector", [], lambda v: v.tensor_tensor(
                dd_r[:], hi_r[:], mid_r[:], ALU.subtract))
            sch.step("vector", [], lambda v: v.tensor_tensor(
                dd_r[:], dd_r[:], cge_r[:], ALU.mult))
            prev_ge_v = sch.step("vector", [], lambda v: v.tensor_tensor(
                hi_r[:], mid_r[:], dd_r[:], ALU.add))
        tb = sch.step("tensor", [(V, prev_ge_v)], lambda t: t.matmul(
            ps_g[:, 0:8], ones_rowf[:], lo_r[:], start=True, stop=True), inc=(T, 1))
        thr_done = sch.step("vector", [(T, tb)], lambda v: v.tensor_copy(
            thrB[:], ps_g[:, 0:8]))

        gv = 0
        for ti in range(NT):
            ldg = sch.step("sync", [(V, gv)] if gv else [], lambda s, ti=ti: s.dma_start(
                out=p_t[:], in_=probs_loc[ti * 128:(ti + 1) * 128, :]), inc=(WP, 16))
            gw = [(WP, WP.n), (V, thr_done)] if ti == 0 else [(WP, WP.n)]
            sch.step("vector", gw, lambda v: v.tensor_tensor(
                g_eq[:], p_t[:], thrB[:], ALU.is_ge))
            sch.step("vector", [], lambda v: v.scalar_tensor_tensor(
                g_m1[:], g_eq[:], 30.0, p_t[:], ALU.mult, ALU.add))
            sch.step("vector", [], lambda v: v.tensor_scalar(
                g_msk[:], g_m1[:], 30.0, None, ALU.subtract))
            sch.step("vector", [], lambda v: v.tensor_reduce(
                colA[:], g_msk[:], AX.X, ALU.max))
            sch.step("vector", [], lambda v: v.tensor_reduce(
                colP[:], p_t[:], AX.X, ALU.max))
            sch.step("vector", [], lambda v: v.tensor_single_scalar(
                cola[:], colA[:], -1.0, ALU.is_ge))
            sch.step("vector", [], lambda v: v.tensor_tensor(
                colt[:], colA[:], colP[:], ALU.subtract))
            sch.step("vector", [], lambda v: v.scalar_tensor_tensor(
                colW[:], colt[:], cola[:, 0:1], colP[:], ALU.mult, ALU.add))
            sch.step("vector", [], lambda v: v.tensor_scalar(
                colna[:], cola[:], -1.0, 1.0, ALU.mult, ALU.add))
            sch.step("vector", [], lambda v: v.tensor_scalar(
                g_t1[:], g_eq[:], -30.0, 30.0, ALU.mult, ALU.add))
            sch.step("vector", [], lambda v: v.scalar_tensor_tensor(
                g_u[:], g_t1[:], colna[:, 0:1], g_msk[:], ALU.mult, ALU.add))
            sch.step("vector", [], lambda v: v.tensor_reduce(
                colt[:], g_u[:], AX.X, ALU.max))
            sch.step("vector", [], lambda v: v.tensor_scalar(
                g_eq[:], g_u[:], colt[:, 0:1], None, ALU.is_equal))
            sch.step("vector", [], lambda v: v.memset(colpref[:], 0.0))
            for e in range(8):
                sch.step("vector", [], lambda v: v.tensor_scalar(
                    colt[:], colpref[:], -1.0, 1.0, ALU.mult, ALU.add))
                sch.step("vector", [], lambda v, e=e: v.tensor_tensor(
                    g_ind[:, e:e + 1], g_eq[:, e:e + 1], colt[:], ALU.mult))
                sch.step("vector", [], lambda v, e=e: v.tensor_tensor(
                    colpref[:], colpref[:], g_eq[:, e:e + 1], ALU.max))
            gv = sch.step("vector", [], lambda v, ti=ti: v.tensor_scalar(
                gates_all[:, ti * 8:(ti + 1) * 8], g_ind[:], colW[:, 0:1],
                None, ALU.mult))
        gates_done = gv

        psbf_last = [0]
        xnb_T = [0]
        wa_last = [0]
        wb_last = [0]
        x3_store = {}
        for hf in range(NHALF):
            xnT_done = 0
            for tt in range(TPH):
                ti = hf * TPH + tt
                w = [(V, x_rel[-2])] if len(x_rel) >= 2 else []
                ld = sch.step("sync", w, lambda s, ti=ti: s.dma_start(
                    out=xt[ti % 2][:], in_=x2[ti * 128:(ti + 1) * 128, :]),
                    inc=(LDp[ti % 2], 16))
                nw = [(LDp[ti % 2], ld)]
                if xnb_T[0]:
                    nw.append((T, xnb_T[0]))
                nrm = sch.step("vector", nw,
                               lambda v, ti=ti: v.tensor_scalar(
                                   xn_b[:], xt[ti % 2][:], mu2[:, ti:ti + 1],
                                   rstd2[:, ti:ti + 1], ALU.subtract, ALU.mult))
                x_rel.append(nrm)
                evA = []
                for c2 in range(KCH):
                    slot = c2 % 3
                    waits = [(V, nrm)] if c2 == 0 else []
                    if c2 >= 3:
                        waits.append((A, evA[c2 - 3]))
                    if c2 == 0:
                        waits.append((G, id_done))
                        waits.append((A, psbf_last[0]))
                    sch.step("tensor", waits, lambda t, c2=c2, slot=slot, tt=tt:
                             t.transpose(ps_t[slot][0:128, 0:128],
                                         xn_b[:, c2 * 128:(c2 + 1) * 128],
                                         ident[:]), inc=(T, 1))
                    ev = sch.step("scalar", [(T, T.n)],
                                  lambda s, c2=c2, slot=slot, tt=tt: s.copy(
                                      xnT_half[0:128,
                                               c2 * (TPH * 128) + tt * 128:
                                               c2 * (TPH * 128) + tt * 128 + 128],
                                      ps_t[slot][0:128, 0:128]))
                    evA.append(ev)
                psbf_last[0] = evA[-1]
                xnT_done = evA[-1]
                xnb_T[0] = T.n

            if hf == 0:
                prev_hev = 0
                lastA_T = [0]
                lastH_T = [0]
            NG = 8
            for e in range(8):
                for c2 in range(KCH):
                    wv = [(T, lastA_T[0])] if c2 == 0 else []
                    sch.step("gdma", wv, lambda g, e=e, c2=c2: g.dma_start(
                        out=w1_b[:, c2 * 2048:(c2 + 1) * 2048],
                        in_=w1_in[e, c2 * 128:(c2 + 1) * 128, :]), inc=(WA, 16))
                for fc in range(NFC):
                    wv = [(T, lastH_T[0])] if fc == 0 else []
                    sch.step("gdma", wv, lambda g, e=e, fc=fc: g.dma_start(
                        out=w2_b[:, fc * C:(fc + 1) * C],
                        in_=w2_in[e, fc * 128:(fc + 1) * 128, :]), inc=(WB, 16))
                wa_tot, wb_tot = WA.n, WB.n

                for tg in range(NG):
                    aev = {}
                    for fc in range(NFC):
                        psA = ps_q if fc % 2 == 0 else ps_s
                        def a_mm(t, fc=fc, tg=tg, psA=psA, e=e):
                            r = None
                            for c2 in range(KCH):
                                r = t.matmul(
                                    psA[:, 0:256],
                                    w1_b[:, c2 * 2048 + fc * 128:
                                         c2 * 2048 + fc * 128 + 128],
                                    xnT_half[:, c2 * (TPH * 128) + tg * 256:
                                             c2 * (TPH * 128) + tg * 256 + 256],
                                    start=(c2 == 0), stop=(c2 == KCH - 1))
                            return r
                        waits = []
                        if fc == 0 and tg == 0:
                            waits = [(WA, wa_tot), (A, xnT_done)]
                        if fc >= 2:
                            waits.append((A, aev[fc - 2]))
                        elif not (fc == 0 and tg == 0 and e == 0 and hf == 0):
                            waits.append((A, A.n))
                        amm = sch.step("tensor", waits, a_mm, inc=(T, 1))
                        aev[fc] = sch.step("scalar", [(T, amm)],
                                           lambda s, fc=fc, psA=psA:
                                           s.activation(a_fm[:, fc * 256:(fc + 1) * 256],
                                                        psA[:, 0:256], AF.Gelu))
                    lastA_T[0] = T.n
                    for ts_ in range(2):
                        tt = tg * 2 + ts_
                        ti = hf * TPH + tt
                        def h_mm(t, ts_=ts_):
                            r = None
                            for fc in range(NFC):
                                t.matmul(ps_h1[:, 0:512],
                                         a_fm[:, fc * 256 + ts_ * 128:
                                              fc * 256 + ts_ * 128 + 128],
                                         w2_b[:, fc * C:fc * C + 512],
                                         start=(fc == 0), stop=(fc == NFC - 1))
                                r = t.matmul(ps_h2[:, 0:512],
                                             a_fm[:, fc * 256 + ts_ * 128:
                                                  fc * 256 + ts_ * 128 + 128],
                                             w2_b[:, fc * C + 512:fc * C + 1024],
                                             start=(fc == 0), stop=(fc == NFC - 1))
                            return r
                        hw = [(A, aev[NFC - 1])]
                        if ts_ == 0:
                            hw.append((WB, wb_tot))
                        if prev_hev:
                            hw.append((V, prev_hev))
                        hmm = sch.step("tensor", hw, h_mm, inc=(T, 1))
                        gcol = gates_all[:, ti * 8 + e:ti * 8 + e + 1]
                        hv = [(T, hmm), (V, gates_done)]
                        if e == 0:
                            ev1 = sch.step("vector", hv, lambda v, tt=tt, gcol=gcol:
                                           v.tensor_scalar(
                                               out_acc[:, tt * C:tt * C + 512],
                                               ps_h1[:, 0:512], gcol, None, ALU.mult))
                            ev2 = sch.step("vector", [], lambda v, tt=tt, gcol=gcol:
                                           v.tensor_scalar(
                                               out_acc[:, tt * C + 512:tt * C + 1024],
                                               ps_h2[:, 0:512], gcol, None, ALU.mult))
                        else:
                            ev1 = sch.step("vector", hv, lambda v, tt=tt, gcol=gcol:
                                           v.scalar_tensor_tensor(
                                               out_acc[:, tt * C:tt * C + 512],
                                               ps_h1[:, 0:512], gcol,
                                               out_acc[:, tt * C:tt * C + 512],
                                               ALU.mult, ALU.add))
                            ev2 = sch.step("vector", [], lambda v, tt=tt, gcol=gcol:
                                           v.scalar_tensor_tensor(
                                               out_acc[:, tt * C + 512:tt * C + 1024],
                                               ps_h2[:, 0:512], gcol,
                                               out_acc[:, tt * C + 512:tt * C + 1024],
                                               ALU.mult, ALU.add))
                        prev_hev = ev2
                    lastH_T[0] = T.n


            for tt in range(TPH):
                ti = hf * TPH + tt
                w = [(V, x_rel[-2])] if len(x_rel) >= 2 else []
                ld = sch.step("sync", w, lambda s, ti=ti: s.dma_start(
                    out=xt[ti % 2][:], in_=x2[ti * 128:(ti + 1) * 128, :]),
                    inc=(LDp[ti % 2], 16))
                x3w = [(LDp[ti % 2], ld), (V, prev_hev)]
                if (ti - 2) in x3_store:
                    x3w.append((STp[ti % 2], x3_store[ti - 2]))
                x3v = sch.step("vector", x3w, lambda v, ti=ti, tt=tt: v.tensor_tensor(
                    x3t[ti % 2][:], xt[ti % 2][:], out_acc[:, tt * C:(tt + 1) * C],
                    ALU.add))
                x_rel.append(x3v)
                stv = sch.step("sync", [(V, x3v)], lambda s, ti=ti: s.dma_start(
                    out=x3_out[ti * 128:(ti + 1) * 128, :], in_=x3t[ti % 2][:]),
                    inc=(STp[ti % 2], 16))
                x3_store[ti] = stv

        sch.step("gpsimd", [(ST, ST.n), (STB, STB.n)], lambda g: None)
        sch.emit(block)
    return nc


_STAGE_B1 = None


def run_stage_b1(x2_full, probs_full, inputs):
    global _STAGE_B1
    if _STAGE_B1 is None:
        _STAGE_B1 = build_stage_b1()
    w1 = np.ascontiguousarray(np.asarray(inputs["L0_ffn_W1"], np.float32))
    w2 = np.ascontiguousarray(np.asarray(inputs["L0_ffn_W2"], np.float32))
    in_maps = []
    for c in range(NCORES):
        x2c = np.ascontiguousarray(x2_full[c * BPC:(c + 1) * BPC].reshape(TPC, C))
        in_maps.append(dict(x2=x2c, probs_all=probs_full,
                            probs_loc=np.ascontiguousarray(
                                probs_full[c * TPC:(c + 1) * TPC]),
                            w1_in=w1, w2_in=w2))
    res = run_bass_kernel_spmd(_STAGE_B1, in_maps, core_ids=list(range(NCORES)))
    x3 = np.stack([res.results[c]["x3_out"].reshape(BPC, N, C)
                   for c in range(NCORES)]).reshape(B, N, C)
    return x3



def build_stage_b2():
    nc = bass.Bass("TRN2", target_bir_lowering=False)
    x3 = nc.dram_tensor("x3", [TPC, C], F32, kind="ExternalInput").ap()
    k123T = nc.dram_tensor("k123T", [C, 48], F32, kind="ExternalInput").ap()
    abR = nc.dram_tensor("abR", [1, 2 * H * 64], F32, kind="ExternalInput").ap()
    bT = nc.dram_tensor("bT", [64, 2 * H], F32, kind="ExternalInput").ap()
    ccT = nc.dram_tensor("ccT", [64, H], F32, kind="ExternalInput").ap()
    vwF = nc.dram_tensor("vwF", [C, H * 64], F32, kind="ExternalInput").ap()
    owS = nc.dram_tensor("owS", [H * 64, C], F32, kind="ExternalInput").ap()
    posT = nc.dram_tensor("posT", [1, N], F32, kind="ExternalInput").ap()
    utri = nc.dram_tensor("utri", [128, 128], F32, kind="ExternalInput").ap()
    x4_out = nc.dram_tensor("x4_out", [TPC, C], F32, kind="ExternalOutput").ap()
    m0_out = nc.dram_tensor("m0_out", [1, 1], F32, kind="ExternalOutput").ap()
    mu4_out = nc.dram_tensor("mu4_out", [128, NT], F32, kind="ExternalOutput").ap()
    rstd4_out = nc.dram_tensor("rstd4_out", [128, NT], F32, kind="ExternalOutput").ap()

    with ExitStack() as ctx:
        sbt = lambda n, s, d=F32: ctx.enter_context(nc.sbuf_tensor(n, s, d))
        pst = lambda n, s, d=F32: ctx.enter_context(nc.psum_tensor(n, s, d))

        ones_row = sbt("ones_row", [1, 128], BF16)
        ones_col = sbt("ones_col", [128, 1], BF16)
        ident = sbt("ident", [128, 128], BF16)
        identf = sbt("identf", [128, 128], F32)
        uptri = sbt("uptri", [128, 128], BF16)
        utri_f = sbt("utri_f", [128, 128], F32)
        k123_b = sbt("k123_b", [128, KCH * 48], BF16)
        vw_b = sbt("vw_b", [128, KCH * (H * 64)], BF16)
        ow_b = sbt("ow_b", [128, 8 * C], BF16)
        abR_s = sbt("abR_s", [1, 2 * H * 64], F32)
        bT_s = sbt("bT_s", [64, 2 * H], F32)
        ccT_s = sbt("ccT_s", [64, H], F32)
        posr = sbt("posr", [1, N], F32)
        nhat = sbt("nhat", [1, N], F32)
        sinm = sbt("sinm", [64, 512], F32)
        p1T = sbt("p1T", [128, NTS * H], F32)
        p2eT = sbt("p2eT", [128, NTS * H], F32)
        mu3 = sbt("mu3", [128, NT], F32)
        rstd3 = sbt("rstd3", [128, NT], F32)
        mu4 = sbt("mu4", [128, NT], F32)
        rstd4 = sbt("rstd4", [128, NT], F32)
        sumc = sbt("sumc", [128, NT], F32)
        sqc = sbt("sqc", [128, NT], F32)
        scr_nt = sbt("scr_nt", [128, NT], F32)
        eps5_col = sbt("eps5_col", [128, 1], F32)
        xt = [sbt(f"xt{i}", [128, C]) for i in range(2)]
        x4t = [sbt(f"x4t{i}", [128, C]) for i in range(2)]
        xn_b = sbt("xn_b", [128, C], BF16)
        xnT = sbt("xnT", [128, C], BF16)
        scr = sbt("scr", [128, C])
        v_b = sbt("v_b", [128, C], BF16)
        cde_s = sbt("cde_s", [128, 64])
        cdeE = sbt("cdeE", [128, 48])
        crE_b = sbt("crE_b", [128, H], BF16)
        c1t = sbt("c1t", [128, H])
        denp = sbt("denp", [128, H])
        dent = sbt("dent", [128, H])
        s_b = sbt("s_b", [128, C], BF16)
        vdiag = sbt("vdiag", [128, C])
        numt = sbt("numt", [128, C])
        o_b = sbt("o_b", [128, C], BF16)
        oTp = [sbt(f"oTp{j}", [128, 128], BF16) for j in range(8)]
        carry_sb = sbt("carry_sb", [1, C], F32)
        carry_b = sbt("carry_b", [1, C], BF16)
        cden_sb = sbt("cden_sb", [1, H], F32)
        cden_b = sbt("cden_b", [1, H], BF16)
        m0c = sbt("m0c", [128, 1], F32)
        m0row = sbt("m0row", [1, 128], F32)
        m0s = sbt("m0s", [1, 1], F32)
        colA = sbt("colA", [128, 1])

        ps_t = [pst(f"ps_t{i}", [128, 128], BF16) for i in range(3)]
        ps_v1 = pst("ps_v1", [128, 512])
        ps_v2 = pst("ps_v2", [128, 512])
        ps_p1 = pst("ps_p1", [128, 512])
        ps_p2 = pst("ps_p2", [128, 512])
        ps_c = pst("ps_c", [128, 512])

        S = {nm: Sem(ctx.enter_context(nc.semaphore(nm)))
             for nm in ("LD", "LDB", "W", "CN", "V", "A", "T", "G", "ST", "STB")}
        (LD, LDB, W, CN, V, A, T, G, ST, STB) = (
            S[n] for n in ("LD", "LDB", "W", "CN", "V", "A", "T", "G", "ST", "STB"))
        LDp = [LD, LDB]
        STp = [ST, STB]
        block = ctx.enter_context(nc.Block())
        sch = Sched({"vector": V, "scalar": A, "gpsimd": G})

        for c2 in range(KCH):
            sch.step("gdma", [], lambda g, c2=c2: g.dma_start(
                out=k123_b[:, c2 * 48:(c2 + 1) * 48],
                in_=k123T[c2 * 128:(c2 + 1) * 128, :]), inc=(W, 16))
            sch.step("gdma", [], lambda g, c2=c2: g.dma_start(
                out=vw_b[:, c2 * (H * 64):(c2 + 1) * (H * 64)],
                in_=vwF[c2 * 128:(c2 + 1) * 128, :]), inc=(W, 16))
            sch.step("gdma", [], lambda g, c2=c2: g.dma_start(
                out=ow_b[:, c2 * C:(c2 + 1) * C],
                in_=owS[c2 * 128:(c2 + 1) * 128, :]), inc=(W, 16))
        sch.step("sync", [], lambda s: s.dma_start(out=abR_s[:], in_=abR), inc=(CN, 16))
        sch.step("sync", [], lambda s: s.dma_start(out=bT_s[:], in_=bT), inc=(CN, 16))
        sch.step("sync", [], lambda s: s.dma_start(out=ccT_s[:], in_=ccT), inc=(CN, 16))
        sch.step("sync", [], lambda s: s.dma_start(out=posr[:], in_=posT), inc=(CN, 16))
        w_all, cn_all = W.n, CN.n

        consts_v = sch.step("vector", [], lambda v: (
            v.memset(ones_row[:], 1.0), v.memset(ones_col[:], 1.0),
            v.memset(eps5_col[:], 1e-5), v.memset(m0c[:], 0.0),
            v.memset(identf[:], 1.0))[-1])
        id_ms = sch.step("gpsimd", [], lambda g: g.memset(ident[:], 1.0))
        id_done = sch.step("gpsimd", [(G, id_ms)], lambda g: g.affine_select(
            ident[:], ident[:], pattern=[[-1, 128]], compare_op=ALU.is_equal,
            fill=0.0, base=0, channel_multiplier=1))
        ut_ld = sch.step("sync", [], lambda s: s.dma_start(
            out=utri_f[:], in_=utri), inc=(CN, 16))
        ut_done = sch.step("vector", [(CN, CN.n)], lambda v: v.tensor_copy(
            uptri[:], utri_f[:]))
        idf_done = sch.step("gpsimd", [(G, id_done), (V, consts_v)], lambda g: g.affine_select(
            identf[:], identf[:], pattern=[[-1, 128]], compare_op=ALU.is_equal,
            fill=0.0, base=0, channel_multiplier=1))

        nh_done = sch.step("vector", [(CN, cn_all), (V, consts_v)],
                           lambda v: v.tensor_scalar(
                               nhat[:], posr[:], 1.0 / float(N - 1), None, ALU.mult))

        prev_sin_a = 0
        for pp in range(2):
            dstT = p1T if pp == 0 else p2eT
            for h in range(H):
                for chunk in range(4):
                    mm = sch.step("tensor",
                                  [(V, nh_done), (A, prev_sin_a)],
                                  lambda t, pp=pp, h=h, chunk=chunk: t.matmul(
                                      ps_c[0:64, 0:512],
                                      abR_s[0:1, pp * 1024 + h * 64:
                                            pp * 1024 + (h + 1) * 64],
                                      nhat[0:1, chunk * 512:(chunk + 1) * 512],
                                      start=True, stop=True), inc=(T, 1))
                    prev_sin_a = sch.step("scalar", [(T, mm)],
                                          lambda s, pp=pp, h=h: s.activation(
                                              sinm[:], ps_c[0:64, 0:512], AF.Sin,
                                              bias=bT_s[0:64, pp * H + h:pp * H + h + 1],
                                              scale=1.0))
                    for sub in range(4):
                        mm2 = sch.step("tensor", [(A, prev_sin_a)],
                                       lambda t, sub=sub, h=h: t.matmul(
                                           ps_v1[0:128, sub:sub + 1],
                                           sinm[0:64, sub * 128:(sub + 1) * 128],
                                           ccT_s[0:64, h:h + 1],
                                           start=True, stop=True), inc=(T, 1))
                    out3 = (dstT[:].rearrange("p (t hh) -> p t hh", hh=H)
                            [:, chunk * 4:chunk * 4 + 4, h:h + 1])
                    in3 = ps_v1[0:128, 0:4].rearrange("p (a b) -> p a b", b=1)
                    ev = sch.step("scalar", [(T, T.n)],
                                  lambda s, out3=out3, in3=in3: s.copy(out3, in3))
                    prev_sin_a = ev
        sch.step("vector", [(A, prev_sin_a)], lambda v: v.tensor_scalar(
            p2eT[:], p2eT[:], 20.0, None, ALU.min))
        p2c = sch.step("vector", [], lambda v: v.tensor_scalar(
            p2eT[:], p2eT[:], -20.0, None, ALU.max))
        p2e_done = sch.step("scalar", [(V, p2c)], lambda s: s.activation(
            p2eT[:], p2eT[:], AF.Exp))

        x_rel = []
        for ti in range(NT):
            w = [(V, x_rel[-2])] if len(x_rel) >= 2 else []
            ld = sch.step("sync", w, lambda s, ti=ti: s.dma_start(
                out=xt[ti % 2][:], in_=x3[ti * 128:(ti + 1) * 128, :]),
                inc=(LDp[ti % 2], 16))
            sch.step("vector", [(LDp[ti % 2], ld)], lambda v, ti=ti: v.tensor_reduce(
                sumc[:, ti:ti + 1], xt[ti % 2][:], AX.X, ALU.add))
            rel = sch.step("vector", [], lambda v, ti=ti: v.scalar_tensor_tensor(
                scr[:], xt[ti % 2][:], 0.0, xt[ti % 2][:], ALU.add, ALU.mult,
                accum_out=sqc[:, ti:ti + 1]))
            x_rel.append(rel)
        sch.step("vector", [], lambda v: v.tensor_scalar(
            mu3[:], sumc[:], 1.0 / C, None, ALU.mult))
        sch.step("vector", [], lambda v: v.tensor_scalar(
            sqc[:], sqc[:], 1.0 / C, None, ALU.mult))
        sch.step("vector", [], lambda v: v.tensor_tensor(
            scr_nt[:], mu3[:], mu3[:], ALU.mult))
        var_done = sch.step("vector", [], lambda v: v.tensor_tensor(
            sqc[:], sqc[:], scr_nt[:], ALU.subtract))
        std_done = sch.step("scalar", [(V, var_done)], lambda s: s.activation(
            rstd3[:], sqc[:], AF.Sqrt, bias=eps5_col[:, 0:1], scale=1.0))
        stats_done = sch.step("vector", [(A, std_done)], lambda v: v.reciprocal(
            rstd3[:], rstd3[:]))

        psbf_last = [0]
        xnb_T = [0]
        x4_store = {}
        prev_tile_v = 0
        prev_tile_a = 0
        prev_oT_T = 0
        for s_ix in range(BPC):
            cr_reset = sch.step("vector", [], lambda v: (
                v.memset(carry_sb[:], 0.0), v.memset(cden_sb[:], 0.0))[-1])
            for t_ix in range(NTS):
                ti = s_ix * NTS + t_ix
                w = [(V, x_rel[-2])] if len(x_rel) >= 2 else []
                ld = sch.step("sync", w, lambda s, ti=ti: s.dma_start(
                    out=xt[ti % 2][:], in_=x3[ti * 128:(ti + 1) * 128, :]),
                    inc=(LDp[ti % 2], 16))
                nw = [(LDp[ti % 2], ld), (V, stats_done)]
                if xnb_T[0]:
                    nw.append((T, xnb_T[0]))
                nrm = sch.step("vector", nw, lambda v, ti=ti: v.tensor_scalar(
                    xn_b[:], xt[ti % 2][:], mu3[:, ti:ti + 1], rstd3[:, ti:ti + 1],
                    ALU.subtract, ALU.mult))
                evA = []
                for c2 in range(KCH):
                    slot = c2 % 3
                    waits = [(V, nrm)] if c2 == 0 else []
                    if c2 >= 3:
                        waits.append((A, evA[c2 - 3]))
                    if c2 == 0:
                        waits.append((G, idf_done))
                        waits.append((A, psbf_last[0]))
                    sch.step("tensor", waits, lambda t, c2=c2, slot=slot: t.transpose(
                        ps_t[slot][0:128, 0:128],
                        xn_b[:, c2 * 128:(c2 + 1) * 128], ident[:]), inc=(T, 1))
                    ev = sch.step("scalar", [(T, T.n)], lambda s, c2=c2, slot=slot:
                                  s.copy(xnT[0:128, c2 * 128:(c2 + 1) * 128],
                                         ps_t[slot][0:128, 0:128]))
                    evA.append(ev)
                psbf_last[0] = evA[-1]
                xnT_done = evA[-1]
                xnb_T[0] = T.n

                def v_mm(t):
                    r = None
                    for c2 in range(KCH):
                        t.matmul(ps_v1[:, 0:512], xnT[:, c2 * 128:(c2 + 1) * 128],
                                 vw_b[:, c2 * 1024:c2 * 1024 + 512],
                                 start=(c2 == 0), stop=(c2 == KCH - 1))
                        r = t.matmul(ps_v2[:, 0:512], xnT[:, c2 * 128:(c2 + 1) * 128],
                                     vw_b[:, c2 * 1024 + 512:c2 * 1024 + 1024],
                                     start=(c2 == 0), stop=(c2 == KCH - 1))
                    return r
                vw_w = [(A, xnT_done), (W, w_all)]
                if prev_tile_v:
                    vw_w.append((V, prev_tile_v))
                vmm = sch.step("tensor", vw_w, v_mm, inc=(T, 1))
                vev = sch.step("scalar", [(T, vmm)], lambda s: s.copy(
                    v_b[:, 0:512], ps_v1[:, 0:512]))
                vev = sch.step("scalar", [], lambda s: s.copy(
                    v_b[:, 512:1024], ps_v2[:, 0:512]))

                def cde_mm(t):
                    r = None
                    for j in range(3):
                        for c2 in range(KCH):
                            r = t.matmul(ps_c[:, j * 16:(j + 1) * 16],
                                         xnT[:, c2 * 128:(c2 + 1) * 128],
                                         k123_b[:, c2 * 48 + j * 16:c2 * 48 + (j + 1) * 16],
                                         start=(j == 0 and c2 == 0),
                                         stop=(j == 2 and c2 == KCH - 1))
                    return r
                cw = [(A, xnT_done)]
                if prev_tile_v:
                    cw.append((V, prev_tile_v))
                cmm = sch.step("tensor", cw, cde_mm, inc=(T, 1))
                cev = sch.step("scalar", [(T, cmm)], lambda s: s.copy(
                    cde_s[:, 0:48], ps_c[:, 0:48]))

                sch.step("vector", [(A, cev)], lambda v, t_ix=t_ix: v.tensor_tensor(
                    cdeE[:, 0:16], cde_s[:, 0:16],
                    p1T[:, t_ix * H:(t_ix + 1) * H], ALU.add))
                sch.step("vector", [], lambda v: v.tensor_copy(
                    cdeE[:, 16:48], cde_s[:, 16:48]))
                sch.step("vector", [], lambda v: v.tensor_scalar(
                    cdeE[:], cdeE[:], 20.0, None, ALU.min))
                ce1 = sch.step("vector", [], lambda v: v.tensor_scalar(
                    cdeE[:], cdeE[:], -20.0, None, ALU.max))
                ce2 = sch.step("scalar", [(V, ce1), (A, p2e_done)], lambda s: s.activation(
                    cdeE[:], cdeE[:], AF.Exp))
                sv = None
                for h in range(H):
                    sv = sch.step("vector", [(A, ce2)] if h == 0 else [],
                                  lambda v, h=h: v.tensor_scalar(
                                      s_b[:, h * 64:(h + 1) * 64],
                                      v_b[:, h * 64:(h + 1) * 64],
                                      cdeE[:, h:h + 1], None, ALU.mult))
                crb = sch.step("vector", [], lambda v: v.tensor_copy(
                    crE_b[:], cdeE[:, 0:16]))
                cab = sch.step("vector", [], lambda v: v.tensor_copy(
                    carry_b[:], carry_sb[:]))
                cdb = sch.step("vector", [], lambda v: v.tensor_copy(
                    cden_b[:], cden_sb[:]))

                def pre_mm(t):
                    r = None
                    for h in range(H):
                        psd = ps_p1 if h < 8 else ps_p2
                        off = (h % 8) * 64
                        t.matmul(psd[:, off:off + 64], uptri[:],
                                 s_b[:, h * 64:(h + 1) * 64],
                                 start=(h % 8 == 0), stop=False)
                        r = t.matmul(psd[:, off:off + 64], ones_row[:],
                                     carry_b[0:1, h * 64:(h + 1) * 64],
                                     start=False, stop=(h % 8 == 7))
                    return r
                pw = [(V, cdb), (V, ut_done)]
                pmm = sch.step("tensor", pw, pre_mm, inc=(T, 1))
                def den_mm(t):
                    t.matmul(ps_c[:, 48:64], uptri[:], crE_b[:],
                             start=True, stop=False)
                    return t.matmul(ps_c[:, 48:64], ones_row[:], cden_b[0:1, :],
                                    start=False, stop=True)
                dmm = sch.step("tensor", [(V, cdb)], den_mm, inc=(T, 1))
                def csum_mm(t):
                    t.matmul(ps_v1[0:1, 0:512], ones_col[:], s_b[:, 0:512],
                             start=True, stop=True)
                    t.matmul(ps_v2[0:1, 0:512], ones_col[:], s_b[:, 512:1024],
                             start=True, stop=True)
                    return t.matmul(ps_c[0:1, 64:80], ones_col[:], crE_b[:],
                                    start=True, stop=True)
                smm = sch.step("tensor", [(T, pmm), (A, A.n)], csum_mm, inc=(T, 1))
                cup = sch.step("vector", [(T, smm)], lambda v: v.scalar_tensor_tensor(
                    carry_sb[0:1, 0:512], ps_v1[0:1, 0:512], 0.0,
                    carry_sb[0:1, 0:512], ALU.add, ALU.add))
                cup = sch.step("vector", [], lambda v: v.scalar_tensor_tensor(
                    carry_sb[0:1, 512:1024], ps_v2[0:1, 0:512], 0.0,
                    carry_sb[0:1, 512:1024], ALU.add, ALU.add))
                cup = sch.step("vector", [], lambda v: v.scalar_tensor_tensor(
                    cden_sb[0:1, :], ps_c[0:1, 64:80], 0.0,
                    cden_sb[0:1, :], ALU.add, ALU.add))

                sch.step("vector", [], lambda v, t_ix=t_ix: v.tensor_tensor(
                    c1t[:], p2eT[:, t_ix * H:(t_ix + 1) * H], cdeE[:, 32:48], ALU.mult))
                sch.step("vector", [(T, smm)], lambda v: v.tensor_copy(
                    denp[:], ps_c[:, 48:64]))
                sch.step("vector", [], lambda v: v.tensor_tensor(
                    dent[:], denp[:], c1t[:], ALU.mult))
                sch.step("vector", [], lambda v: v.tensor_tensor(
                    dent[:], dent[:], cdeE[:, 16:32], ALU.add))
                dr = sch.step("vector", [], lambda v: v.reciprocal(dent[:], dent[:]))
                ov = None
                for h in range(H):
                    psd = ps_p1 if h < 8 else ps_p2
                    off = (h % 8) * 64
                    sch.step("vector", [(T, pmm)] if h == 0 else [],
                             lambda v, h=h: v.tensor_scalar(
                                 vdiag[:, h * 64:(h + 1) * 64],
                                 v_b[:, h * 64:(h + 1) * 64],
                                 cdeE[:, 16 + h:17 + h], None, ALU.mult))
                    sch.step("vector", [], lambda v, h=h, psd=psd, off=off:
                             v.scalar_tensor_tensor(
                                 numt[:, h * 64:(h + 1) * 64],
                                 psd[:, off:off + 64], c1t[:, h:h + 1],
                                 vdiag[:, h * 64:(h + 1) * 64], ALU.mult, ALU.add))
                    ov = sch.step("vector", [], lambda v, h=h: v.tensor_scalar(
                        o_b[:, h * 64:(h + 1) * 64],
                        numt[:, h * 64:(h + 1) * 64],
                        dent[:, h:h + 1], None, ALU.mult))
                evO = []
                for h in range(H):
                    slot = h % 3
                    waits = [(V, ov)] if h == 0 else []
                    if h >= 3:
                        waits.append((A, evO[h - 3]))
                    if h == 0:
                        waits.append((A, psbf_last[0]))
                        if prev_oT_T:
                            waits.append((T, prev_oT_T))
                    sch.step("tensor", waits, lambda t, h=h, slot=slot: t.transpose(
                        ps_t[slot][0:64, 0:128],
                        o_b[:, h * 64:(h + 1) * 64], ident[:]), inc=(T, 1))
                    ev = sch.step("scalar", [(T, T.n)], lambda s, h=h, slot=slot:
                                  s.copy(oTp[h // 2][(h % 2) * 64:(h % 2) * 64 + 64, :],
                                         ps_t[slot][0:64, 0:128]))
                    evO.append(ev)
                psbf_last[0] = evO[-1]
                def o_mm(t):
                    r = None
                    for j in range(8):
                        t.matmul(ps_v1[:, 0:512], oTp[j][:],
                                 ow_b[:, j * C:j * C + 512],
                                 start=(j == 0), stop=(j == 7))
                        r = t.matmul(ps_v2[:, 0:512], oTp[j][:],
                                     ow_b[:, j * C + 512:j * C + 1024],
                                     start=(j == 0), stop=(j == 7))
                    return r
                omm = sch.step("tensor", [(A, evO[-1]), (V, cup)], o_mm, inc=(T, 1))
                prev_oT_T = omm
                x4w = [(T, omm)]
                if (ti - 2) in x4_store:
                    x4w.append((STp[ti % 2], x4_store[ti - 2]))
                sch.step("vector", x4w, lambda v, ti=ti: v.tensor_tensor(
                    x4t[ti % 2][:, 0:512], xt[ti % 2][:, 0:512],
                    ps_v1[:, 0:512], ALU.add))
                x4v = sch.step("vector", [], lambda v, ti=ti: v.tensor_tensor(
                    x4t[ti % 2][:, 512:1024], xt[ti % 2][:, 512:1024],
                    ps_v2[:, 0:512], ALU.add))
                x_rel.append(x4v)
                sch.step("vector", [], lambda v, ti=ti: v.tensor_reduce(
                    sumc[:, ti:ti + 1], x4t[ti % 2][:], AX.X, ALU.add))
                st2 = sch.step("vector", [], lambda v, ti=ti: v.scalar_tensor_tensor(
                    scr[:], x4t[ti % 2][:], 0.0, x4t[ti % 2][:], ALU.add, ALU.mult,
                    accum_out=sqc[:, ti:ti + 1]))
                prev_tile_v = st2
                stv = sch.step("sync", [(V, x4v)], lambda s, ti=ti: s.dma_start(
                    out=x4_out[ti * 128:(ti + 1) * 128, :], in_=x4t[ti % 2][:]),
                    inc=(STp[ti % 2], 16))
                x4_store[ti] = stv

        sch.step("vector", [], lambda v: v.tensor_scalar(
            mu4[:], sumc[:], 1.0 / C, None, ALU.mult))
        sch.step("vector", [], lambda v: v.tensor_scalar(
            sqc[:], sqc[:], 1.0 / C, None, ALU.mult))
        sch.step("vector", [], lambda v: v.tensor_tensor(
            scr_nt[:], mu4[:], mu4[:], ALU.mult))
        var4 = sch.step("vector", [], lambda v: v.tensor_tensor(
            sqc[:], sqc[:], scr_nt[:], ALU.subtract))
        sd4 = sch.step("scalar", [(V, var4)], lambda s: s.activation(
            rstd4[:], sqc[:], AF.Sqrt, bias=eps5_col[:, 0:1], scale=1.0))
        st4 = sch.step("vector", [(A, sd4)], lambda v: v.reciprocal(
            rstd4[:], rstd4[:]))
        for ti in range(NT):
            w = [(V, x_rel[-2])] if len(x_rel) >= 2 else []
            w = w + [(ST, ST.n), (STB, STB.n)]
            ld = sch.step("sync", w, lambda s, ti=ti: s.dma_start(
                out=xt[ti % 2][:], in_=x4_out[ti * 128:(ti + 1) * 128, :]),
                inc=(LDp[ti % 2], 16))
            nrm = sch.step("vector", [(LDp[ti % 2], ld), (V, st4)],
                           lambda v, ti=ti: v.tensor_scalar(
                               scr[:], xt[ti % 2][:], mu4[:, ti:ti + 1],
                               rstd4[:, ti:ti + 1], ALU.subtract, ALU.mult))
            x_rel.append(nrm)
            sch.step("vector", [], lambda v: v.tensor_reduce(
                colA[:], scr[:], AX.X, ALU.max, apply_absolute_value=True))
            m0u = sch.step("vector", [], lambda v: v.tensor_tensor(
                m0c[:], m0c[:], colA[:], ALU.max))
        m0t = sch.step("tensor", [(V, m0u), (G, idf_done)], lambda t: t.transpose(
            ps_c[0:1, 0:128], m0c[:], identf[:]), inc=(T, 1))
        m0e = sch.step("vector", [(T, m0t)], lambda v: v.tensor_copy(
            m0row[:], ps_c[0:1, 0:128]))
        m0r = sch.step("vector", [], lambda v: v.tensor_reduce(
            m0s[:], m0row[:], AX.X, ALU.max))
        sch.step("sync", [(V, m0r)], lambda s: s.dma_start(
            out=m0_out, in_=m0s[:]), inc=(ST, 16))
        sch.step("sync", [], lambda s: s.dma_start(
            out=mu4_out, in_=mu4[:]), inc=(ST, 16))
        sch.step("sync", [], lambda s: s.dma_start(
            out=rstd4_out, in_=rstd4[:]), inc=(ST, 16))
        sch.step("gpsimd", [(ST, ST.n), (STB, STB.n)], lambda g: None)
        sch.emit(block)
    return nc


_STAGE_B2 = None


def run_stage_b2(x3_full, inputs):
    global _STAGE_B2
    if _STAGE_B2 is None:
        _STAGE_B2 = build_stage_b2()
    k1 = np.asarray(inputs["L1_attn_k1"], np.float32)
    k2 = np.asarray(inputs["L1_attn_k2"], np.float32)
    k3 = np.asarray(inputs["L1_attn_k3"], np.float32)
    k123T = np.ascontiguousarray(np.concatenate([k1.T, k2.T, k3.T], axis=1))
    a1 = np.asarray(inputs["L1_attn_a1"], np.float32).reshape(H, 64)
    a2 = np.asarray(inputs["L1_attn_a2"], np.float32).reshape(H, 64)
    b1 = np.asarray(inputs["L1_attn_b1"], np.float32).reshape(H, 64)
    b2 = np.asarray(inputs["L1_attn_b2"], np.float32).reshape(H, 64)
    cc = np.asarray(inputs["L1_attn_c"], np.float32).reshape(H, 64)
    abR = np.zeros((1, 2 * H * 64), np.float32)
    abR[0, :H * 64] = a1.reshape(-1)
    abR[0, H * 64:] = a2.reshape(-1)
    bT = np.ascontiguousarray(np.concatenate([b1.T, b2.T], axis=1))
    ccT = np.ascontiguousarray(cc.T)
    vw = np.asarray(inputs["L1_attn_value_weight"], np.float32)
    vwF = np.ascontiguousarray(vw.transpose(1, 0, 2).reshape(C, H * 64))
    ow = np.asarray(inputs["L1_attn_output_weight"], np.float32)
    owS = np.ascontiguousarray(ow.transpose(0, 2, 1).reshape(H * 64, C))
    posT = np.arange(N, dtype=np.float32).reshape(1, N)
    in_maps = []
    for c in range(NCORES):
        x3c = np.ascontiguousarray(x3_full[c * BPC:(c + 1) * BPC].reshape(TPC, C))
        in_maps.append(dict(x3=x3c, k123T=k123T, abR=abR, bT=bT, ccT=ccT,
                            vwF=vwF, owS=owS, posT=posT,
                            utri=np.triu(np.ones((128, 128), np.float32))))
    res = run_bass_kernel_spmd(_STAGE_B2, in_maps, core_ids=list(range(NCORES)))
    x4 = np.stack([res.results[c]["x4_out"].reshape(BPC, N, C)
                   for c in range(NCORES)]).reshape(B, N, C)
    m0 = np.stack([res.results[c]["m0_out"].reshape(1) for c in range(NCORES)])
    mu4 = np.stack([res.results[c]["mu4_out"] for c in range(NCORES)])
    rstd4 = np.stack([res.results[c]["rstd4_out"] for c in range(NCORES)])
    return x4, m0, mu4, rstd4



def build_stage_l1ffn(mode):
    nc = bass.Bass("TRN2", target_bir_lowering=False)
    x4 = nc.dram_tensor("x4", [TPC, C], F32, kind="ExternalInput").ap()
    mu4c = nc.dram_tensor("mu4c", [128, NT], F32, kind="ExternalInput").ap()
    rstd4c = nc.dram_tensor("rstd4c", [128, NT], F32, kind="ExternalInput").ap()
    m_all = nc.dram_tensor("m_all", [1, 64], F32, kind="ExternalInput").ap()
    if mode == "c":
        wu_nat = nc.dram_tensor("wu_nat", [3072, C], F32, kind="ExternalInput").ap()
        wg_nat = nc.dram_tensor("wg_nat", [3072, C], F32, kind="ExternalInput").ap()
        wuT_out = nc.dram_tensor("wuT_out", [128, KCH * 3072], BF16,
                                 kind="ExternalOutput").ap()
        wgT_out = nc.dram_tensor("wgT_out", [128, KCH * 3072], BF16,
                                 kind="ExternalOutput").ap()
        m_out = nc.dram_tensor("m_out", [1, 2], F32, kind="ExternalOutput").ap()
    elif mode == "d":
        wuT_in = nc.dram_tensor("wuT_in", [128, KCH * 3072], BF16,
                                kind="ExternalInput").ap()
        wgT_in = nc.dram_tensor("wgT_in", [128, KCH * 3072], BF16,
                                kind="ExternalInput").ap()
        m_out = nc.dram_tensor("m_out", [1, 1], F32, kind="ExternalOutput").ap()
        upq_out = nc.dram_tensor("upq_out", [TPC, 3072], I16, kind="ExternalOutput").ap()
        gtq_out = nc.dram_tensor("gtq_out", [TPC, 3072], I16, kind="ExternalOutput").ap()
    elif mode == "e":
        upq_in = nc.dram_tensor("upq_in", [TPC, 3072], I16, kind="ExternalInput").ap()
        gtq_in = nc.dram_tensor("gtq_in", [TPC, 3072], I16, kind="ExternalInput").ap()
        m_out = nc.dram_tensor("m_out", [1, 1], F32, kind="ExternalOutput").ap()
    else:
        upq_in = nc.dram_tensor("upq_in", [TPC, 3072], I16, kind="ExternalInput").ap()
        gtq_in = nc.dram_tensor("gtq_in", [TPC, 3072], I16, kind="ExternalInput").ap()
        o_out = nc.dram_tensor("o_out", [TPC, 3072], BF16, kind="ExternalOutput").ap()

    QMAX = 32767.0
    NRC = 3072 // 128
    NDC = 6
    FR = 512 if mode in ("c", "d") else 3072

    with ExitStack() as ctx:
        sbt = lambda n, s, d=F32: ctx.enter_context(nc.sbuf_tensor(n, s, d))
        pst = lambda n, s, d=F32: ctx.enter_context(nc.psum_tensor(n, s, d))

        ones_rowf = sbt("ones_rowf", [1, 128], F32)
        ident = sbt("ident", [128, 128], BF16)
        identf = sbt("identf", [128, 128], F32)
        m_s = sbt("m_s", [1, 64], F32)
        mrow = sbt("mrow", [1, 1], F32)
        sc_sc = sbt("sc_sc", [1, 8])
        rrow = sbt("rrow", [1, 8])
        scB = sbt("scB", [128, 8])
        scB2 = sbt("scB2", [128, 8])
        mu4s = sbt("mu4s", [128, NT], F32)
        rstd4s = sbt("rstd4s", [128, NT], F32)
        xt = [sbt(f"xt{i}", [128, C]) for i in range(2)]
        if mode in ("c", "d"):
            xq_b = sbt("xq_b", [128, C], BF16)
            scr = sbt("scr", [128, C])
            xqT = sbt("xqT", [128, KCH * (NT * 128)], BF16)
            wuT = sbt("wuT", [128, KCH * 3072], BF16)
            wgT = sbt("wgT", [128, KCH * 3072], BF16)
        if mode == "c":
            wt_f = [sbt(f"wt_f{i}", [128, C]) for i in range(2)]
            wsc = sbt("wsc", [128, 32])
            wsc_r = sbt("wsc_r", [128, 32])
            wdq_b = sbt("wdq_b", [128, C], BF16)
        upf = sbt("upf", [128, FR])
        gtf = sbt("gtf", [128, FR])
        sigf = sbt("sigf", [128, FR])
        t3f = sbt("t3f", [128, FR])
        t4f = sbt("t4f", [128, FR])
        colm = [sbt(f"colm{i}", [128, 1]) for i in range(2)]
        m_run = sbt("m_run", [128, 2])
        mrow128 = sbt("mrow128", [1, 128])
        mres = sbt("mres", [1, 2])
        if mode == "f1":
            of_t = [sbt("of_t0", [128, 3072], BF16)] * 2
        if mode == "d":
            upq_t = [sbt("upq_t0", [128, 3072], I16)] * 2
            gtq_t = [sbt("gtq_t0", [128, 3072], I16)] * 2
        if mode in ("e", "f1"):
            upq_t = [sbt(f"upq_t{i}", [128, 3072], I16) for i in range(2)]
            gtq_t = [sbt(f"gtq_t{i}", [128, 3072], I16) for i in range(2)]

        ps_t = [pst(f"ps_t{i}", [128, 128], BF16) for i in range(3)]
        ps_u1 = pst("ps_u1", [128, 512])
        ps_u2 = pst("ps_u2", [128, 512])
        ps_g1 = pst("ps_g1", [128, 512])
        ps_g2 = pst("ps_g2", [128, 512])
        ps_m = pst("ps_m", [128, 512])

        S = {nm: Sem(ctx.enter_context(nc.semaphore(nm)))
             for nm in ("LD", "LDB", "WL", "WLB", "CN", "V", "A", "T", "G", "ST", "STB")}
        (LD, LDB, WL, WLB, CN, V, A, T, G, ST, STB) = (
            S[n] for n in ("LD", "LDB", "WL", "WLB", "CN", "V", "A", "T", "G", "ST", "STB"))
        LDp = [LD, LDB]
        WLp = [WL, WLB]
        STp = [ST, STB]
        block = ctx.enter_context(nc.Block())
        sch = Sched({"vector": V, "scalar": A, "gpsimd": G})

        sch.step("sync", [], lambda s: s.dma_start(out=m_s[:], in_=m_all), inc=(CN, 16))
        sch.step("sync", [], lambda s: s.dma_start(out=mu4s[:], in_=mu4c), inc=(CN, 16))
        sch.step("sync", [], lambda s: s.dma_start(out=rstd4s[:], in_=rstd4c), inc=(CN, 16))
        cn_all = CN.n
        consts_v = sch.step("vector", [], lambda v: (
            v.memset(ones_rowf[:], 1.0), v.memset(identf[:], 1.0),
            v.memset(sc_sc[:], 1.0), v.memset(rrow[:], 1.0),
            v.memset(m_run[:], 0.0))[-1])
        id_ms = sch.step("gpsimd", [], lambda g: g.memset(ident[:], 1.0))
        id_done = sch.step("gpsimd", [(G, id_ms)], lambda g: g.affine_select(
            ident[:], ident[:], pattern=[[-1, 128]], compare_op=ALU.is_equal,
            fill=0.0, base=0, channel_multiplier=1))
        idf_done = sch.step("gpsimd", [(G, id_done), (V, consts_v)],
                            lambda g: g.affine_select(
                                identf[:], identf[:], pattern=[[-1, 128]],
                                compare_op=ALU.is_equal, fill=0.0, base=0,
                                channel_multiplier=1))

        nsc = {"c": 1, "d": 3, "e": 4, "f1": 5}[mode]
        sc_done = None
        for j in range(nsc):
            sch.step("vector", [(CN, cn_all), (V, consts_v)] if j == 0 else [],
                     lambda v, j=j: v.tensor_reduce(
                         mrow[:], m_s[0:1, j * 8:(j + 1) * 8], AX.X, ALU.max))
            sch.step("vector", [], lambda v, j=j: v.tensor_scalar(
                sc_sc[0:1, j:j + 1], mrow[:], 1.0 / QMAX, None, ALU.mult))
            sc_done = sch.step("vector", [], lambda v, j=j: v.tensor_scalar(
                sc_sc[0:1, j:j + 1], sc_sc[0:1, j:j + 1], 1e-12, None, ALU.max))
        rc = sch.step("vector", [(V, sc_done)], lambda v: v.reciprocal(
            rrow[0:1, 0:nsc], sc_sc[0:1, 0:nsc]))
        bc1 = sch.step("tensor", [(V, rc)], lambda t: t.matmul(
            ps_m[:, 0:8], ones_rowf[:], sc_sc[:], start=True, stop=True), inc=(T, 1))
        bv1 = sch.step("vector", [(T, bc1)], lambda v: v.tensor_copy(
            scB[:], ps_m[:, 0:8]))
        bc2 = sch.step("tensor", [(V, bv1)], lambda t: t.matmul(
            ps_m[:, 0:8], ones_rowf[:], rrow[:], start=True, stop=True), inc=(T, 1))
        bv2 = sch.step("vector", [(T, bc2)], lambda v: v.tensor_copy(
            scB2[:], ps_m[:, 0:8]))
        s_col = lambda j: scB[:, j:j + 1]
        r_col = lambda j: scB2[:, j:j + 1]

        x_rel = []
        psbf_last = [0]
        xnb_T = [0]
        for ti in range(NT if mode in ("c", "d") else 0):
            w = [(V, x_rel[-2])] if len(x_rel) >= 2 else []
            ld = sch.step("sync", w, lambda s, ti=ti: s.dma_start(
                out=xt[ti % 2][:], in_=x4[ti * 128:(ti + 1) * 128, :]),
                inc=(LDp[ti % 2], 16))
            nw = [(LDp[ti % 2], ld), (V, bv2)]
            if xnb_T[0]:
                nw.append((T, xnb_T[0]))
            sch.step("vector", nw, lambda v, ti=ti: v.tensor_scalar(
                scr[:], xt[ti % 2][:], mu4s[:, ti:ti + 1], rstd4s[:, ti:ti + 1],
                ALU.subtract, ALU.mult))
            sch.step("vector", [], lambda v: v.tensor_scalar(
                scr[:], scr[:], r_col(0), MAGIC, ALU.mult, ALU.add))
            sch.step("vector", [], lambda v: v.tensor_scalar(
                scr[:], scr[:], MAGIC, QMAX, ALU.subtract, ALU.min))
            nrm = sch.step("vector", [], lambda v: v.tensor_scalar(
                xq_b[:], scr[:], -QMAX - 1.0, s_col(0), ALU.max, ALU.mult))
            x_rel.append(nrm)
            evA = []
            for c2 in range(KCH):
                slot = c2 % 3
                waits = [(V, nrm)] if c2 == 0 else []
                if c2 >= 3:
                    waits.append((A, evA[c2 - 3]))
                if c2 == 0:
                    waits.append((G, idf_done))
                    waits.append((A, psbf_last[0]))
                sch.step("tensor", waits, lambda t, c2=c2, slot=slot: t.transpose(
                    ps_t[slot][0:128, 0:128],
                    xq_b[:, c2 * 128:(c2 + 1) * 128], ident[:]), inc=(T, 1))
                ev = sch.step("scalar", [(T, T.n)], lambda s, c2=c2, slot=slot, ti=ti:
                              s.copy(xqT[0:128, c2 * (NT * 128) + ti * 128:
                                          c2 * (NT * 128) + ti * 128 + 128],
                                     ps_t[slot][0:128, 0:128]))
                evA.append(ev)
            psbf_last[0] = evA[-1]
            xnb_T[0] = T.n
        xqT_done = evA[-1] if mode in ("c", "d") else 0

        wdq_T = [0]
        def dequant_weight(w_nat, dstT, first_v):
            last_v = first_v
            last_ev = psbf_last[0]
            for rc_ in range(NRC):
                buf = wt_f[rc_ % 2]
                lw = [(V, last_v)] if last_v else []
                ldw = sch.step("sync", lw, lambda s, rc_=rc_, w_nat=w_nat, buf=buf:
                               s.dma_start(out=buf[:],
                                           in_=w_nat[rc_ * 128:(rc_ + 1) * 128, :]),
                               inc=(WLp[rc_ % 2], 16))
                dqw = [(WLp[rc_ % 2], ldw)]
                if wdq_T[0]:
                    dqw.append((T, wdq_T[0]))
                sch.step("vector", dqw, lambda v, buf=buf: v.tensor_reduce(
                    wsc[:], buf[:].rearrange("p (b k) -> p b k", k=32),
                    AX.X, ALU.max, apply_absolute_value=True))
                sch.step("vector", [], lambda v: v.tensor_scalar(
                    wsc[:], wsc[:], 1.0 / 7.0, 1e-12, ALU.mult, ALU.max))
                sch.step("vector", [], lambda v: v.reciprocal(wsc_r[:], wsc[:]))
                sc3 = bass.AP(wsc_r[:].tensor, 0, [[32, 128], [1, 32], [0, 32]])
                sc3b = bass.AP(wsc[:].tensor, 0, [[32, 128], [1, 32], [0, 32]])
                b3 = buf[:].rearrange("p (b k) -> p b k", k=32)
                sch.step("vector", [], lambda v, b3=b3, sc3=sc3: v.tensor_tensor(
                    b3, b3, sc3, ALU.mult))
                sch.step("vector", [], lambda v, buf=buf: v.tensor_scalar(
                    buf[:], buf[:], MAGIC, MAGIC, ALU.add, ALU.subtract))
                sch.step("vector", [], lambda v, buf=buf: v.tensor_scalar(
                    buf[:], buf[:], 7.0, -8.0, ALU.min, ALU.max))
                lv = sch.step("vector", [], lambda v, b3=b3, sc3b=sc3b: v.tensor_tensor(
                    wdq_b[:].rearrange("p (b k) -> p b k", k=32), b3, sc3b, ALU.mult))
                last_v = lv
                evB = []
                for c2 in range(KCH):
                    slot = c2 % 3
                    waits = [(V, lv)] if c2 == 0 else []
                    if c2 >= 3:
                        waits.append((A, evB[c2 - 3]))
                    if c2 == 0:
                        waits.append((A, psbf_last[0]))
                    sch.step("tensor", waits, lambda t, c2=c2, slot=slot: t.transpose(
                        ps_t[slot][0:128, 0:128],
                        wdq_b[:, c2 * 128:(c2 + 1) * 128], ident[:]), inc=(T, 1))
                    ev = sch.step("scalar", [(T, T.n)],
                                  lambda s, c2=c2, slot=slot, rc_=rc_, dstT=dstT:
                                  s.copy(dstT[0:128, c2 * 3072 + rc_ * 128:
                                              c2 * 3072 + rc_ * 128 + 128],
                                         ps_t[slot][0:128, 0:128]))
                    evB.append(ev)
                psbf_last[0] = evB[-1]
                wdq_T[0] = T.n
                last_ev = evB[-1]
            return last_v, last_ev

        if mode == "c":
            vU, aU = dequant_weight(wu_nat, wuT, 0)
            vG, aG = dequant_weight(wg_nat, wgT, vU)
            wT_done = aG
            sch.step("sync", [(A, wT_done)], lambda s: s.dma_start(
                out=wuT_out, in_=wuT[:]), inc=(ST, 16))
            sch.step("sync", [], lambda s: s.dma_start(
                out=wgT_out, in_=wgT[:]), inc=(ST, 16))
        elif mode == "d":
            for half in range(2):
                sch.step("sync", [], lambda s, half=half: s.dma_start(
                    out=wuT[:, half * (KCH * 1536):(half + 1) * (KCH * 1536)],
                    in_=wuT_in[:, half * (KCH * 1536):(half + 1) * (KCH * 1536)]),
                    inc=(WL, 16))
                sch.step("sync", [], lambda s, half=half: s.dma_start(
                    out=wgT[:, half * (KCH * 1536):(half + 1) * (KCH * 1536)],
                    in_=wgT_in[:, half * (KCH * 1536):(half + 1) * (KCH * 1536)]),
                    inc=(WL, 16))
            wld = WL.n
            wT_done = sch.step("scalar", [(WL, wld)], lambda s: s.memzero(
                mrow128[:]))

        frag_hist = []
        o_store = {}
        q_store = {}
        q_rel = []
        fv = 0
        for ti in range(NT):
            if mode in ("e", "f1"):
                wq = [(V, q_rel[-2])] if len(q_rel) >= 2 else []
                ldu = sch.step("sync", wq, lambda s, ti=ti: s.dma_start(
                    out=upq_t[ti % 2][:], in_=upq_in[ti * 128:(ti + 1) * 128, :]),
                    inc=(LDp[ti % 2], 16))
                ldg2 = sch.step("sync", [], lambda s, ti=ti: s.dma_start(
                    out=gtq_t[ti % 2][:], in_=gtq_in[ti * 128:(ti + 1) * 128, :]),
                    inc=(LDp[ti % 2], 16))
            for dc in range(NDC if mode in ("c", "d") else 1):
                pu = ps_u1 if dc % 2 == 0 else ps_u2
                pg = ps_g1 if dc % 2 == 0 else ps_g2
                def up_mm(t, ti=ti, dc=dc, pu=pu, pg=pg):
                    r = None
                    for c2 in range(KCH):
                        t.matmul(pu[:, 0:512],
                                 xqT[:, c2 * (NT * 128) + ti * 128:
                                     c2 * (NT * 128) + ti * 128 + 128],
                                 wuT[:, c2 * 3072 + dc * 512:c2 * 3072 + dc * 512 + 512],
                                 start=(c2 == 0), stop=(c2 == KCH - 1))
                        r = t.matmul(pg[:, 0:512],
                                     xqT[:, c2 * (NT * 128) + ti * 128:
                                         c2 * (NT * 128) + ti * 128 + 128],
                                     wgT[:, c2 * 3072 + dc * 512:c2 * 3072 + dc * 512 + 512],
                                     start=(c2 == 0), stop=(c2 == KCH - 1))
                    return r
                if mode in ("c", "d"):
                    mw = []
                    if ti == 0 and dc == 0:
                        mw = [(A, wT_done), (A, xqT_done)]
                    if len(frag_hist) >= 2:
                        mw.append((V, frag_hist[-2]))
                    umm = sch.step("tensor", mw, up_mm, inc=(T, 1))
                else:
                    umm = None
                if mode == "c":
                    sch.step("vector", [(T, umm)], lambda v, pu=pu: v.tensor_reduce(
                        colm[0][:], pu[:, 0:512], AX.X, ALU.max,
                        apply_absolute_value=True))
                    sch.step("vector", [], lambda v: v.tensor_tensor(
                        m_run[:, 0:1], m_run[:, 0:1], colm[0][:], ALU.max))
                    sch.step("vector", [], lambda v, pg=pg: v.tensor_reduce(
                        colm[1][:], pg[:, 0:512], AX.X, ALU.max,
                        apply_absolute_value=True))
                    fv = sch.step("vector", [], lambda v: v.tensor_tensor(
                        m_run[:, 1:2], m_run[:, 1:2], colm[1][:], ALU.max))
                else:
                    if mode == "d":
                        sch.step("vector", [(T, umm)], lambda v, pu=pu: v.tensor_scalar(
                            upf[:], pu[:, 0:512], r_col(1), MAGIC, ALU.mult, ALU.add))
                        sch.step("vector", [], lambda v: v.tensor_scalar(
                            upf[:], upf[:], MAGIC, QMAX, ALU.subtract, ALU.min))
                        sch.step("vector", [], lambda v: v.tensor_scalar(
                            upf[:], upf[:], -QMAX - 1.0, None, ALU.max))
                        qsw = [(STp[(ti - 1) % 2], q_store[ti - 1])]                             if (dc == 0 and (ti - 1) in q_store) else []
                        sch.step("vector", qsw, lambda v, ti=ti, dc=dc: v.tensor_copy(
                            upq_t[ti % 2][:, dc * 512:(dc + 1) * 512], upf[:]))
                        sch.step("vector", [], lambda v: v.tensor_scalar(
                            upf[:], upf[:], s_col(1), None, ALU.mult))
                        sch.step("vector", [], lambda v, pg=pg: v.tensor_scalar(
                            gtf[:], pg[:, 0:512], r_col(2), MAGIC, ALU.mult, ALU.add))
                        sch.step("vector", [], lambda v: v.tensor_scalar(
                            gtf[:], gtf[:], MAGIC, QMAX, ALU.subtract, ALU.min))
                        sch.step("vector", [], lambda v: v.tensor_scalar(
                            gtf[:], gtf[:], -QMAX - 1.0, None, ALU.max))
                        sch.step("vector", [], lambda v, ti=ti, dc=dc: v.tensor_copy(
                            gtq_t[ti % 2][:, dc * 512:(dc + 1) * 512], gtf[:]))
                        gq = sch.step("vector", [], lambda v: v.tensor_scalar(
                            gtf[:], gtf[:], s_col(2), None, ALU.mult))
                    else:
                        lw_ = [(LDp[ti % 2], ldg2)] if dc == 0 else []
                        sch.step("vector", lw_, lambda v, ti=ti, dc=dc: v.tensor_scalar(
                            upf[:], upq_t[ti % 2][:, dc * FR:(dc + 1) * FR],
                            s_col(1), None, ALU.mult))
                        gq = sch.step("vector", [], lambda v, ti=ti, dc=dc: v.tensor_scalar(
                            gtf[:], gtq_t[ti % 2][:, dc * FR:(dc + 1) * FR],
                            s_col(2), None, ALU.mult))
                    sg = sch.step("scalar", [(V, gq)], lambda s: s.activation(
                        sigf[:], gtf[:], AF.Sigmoid))
                    sch.step("vector", [(A, sg)], lambda v: v.tensor_scalar(
                        sigf[:], sigf[:], 65536.0, MAGIC, ALU.mult, ALU.add))
                    sch.step("vector", [], lambda v: v.tensor_scalar(
                        sigf[:], sigf[:], MAGIC, 65535.0, ALU.subtract, ALU.min))
                    sch.step("vector", [], lambda v: v.tensor_scalar(
                        sigf[:], sigf[:], 0.0, 1.0 / 65536.0, ALU.max, ALU.mult))
                    t3v = sch.step("vector", [], lambda v: v.tensor_tensor(
                        t3f[:], gtf[:], sigf[:], ALU.mult))
                    if mode == "d":
                        sch.step("vector", [], lambda v: v.tensor_reduce(
                            colm[0][:], t3f[:], AX.X, ALU.max,
                            apply_absolute_value=True))
                        fv = sch.step("vector", [], lambda v: v.tensor_tensor(
                            m_run[:, 0:1], m_run[:, 0:1], colm[0][:], ALU.max))
                    else:
                        sch.step("vector", [], lambda v: v.tensor_scalar(
                            t3f[:], t3f[:], r_col(3), MAGIC, ALU.mult, ALU.add))
                        sch.step("vector", [], lambda v: v.tensor_scalar(
                            t3f[:], t3f[:], MAGIC, QMAX, ALU.subtract, ALU.min))
                        sch.step("vector", [], lambda v: v.tensor_scalar(
                            t3f[:], t3f[:], -QMAX - 1.0, s_col(3), ALU.max, ALU.mult))
                        t4v = sch.step("vector", [], lambda v: v.tensor_tensor(
                            t4f[:], t3f[:], upf[:], ALU.mult))
                        if mode == "e":
                            sch.step("vector", [], lambda v: v.tensor_reduce(
                                colm[0][:], t4f[:], AX.X, ALU.max,
                                apply_absolute_value=True))
                            fv = sch.step("vector", [], lambda v: v.tensor_tensor(
                                m_run[:, 0:1], m_run[:, 0:1], colm[0][:], ALU.max))
                        else:
                            sch.step("vector", [], lambda v: v.tensor_scalar(
                                t4f[:], t4f[:], r_col(4), MAGIC, ALU.mult, ALU.add))
                            sch.step("vector", [], lambda v: v.tensor_scalar(
                                t4f[:], t4f[:], MAGIC, QMAX, ALU.subtract, ALU.min))
                            ow_ = [(STp[(ti - 1) % 2], o_store[ti - 1])] \
                                if (dc == 0 and (ti - 1) in o_store) else []
                            fv = sch.step("vector", ow_, lambda v, dc=dc, ti=ti:
                                          v.tensor_scalar(
                                              of_t[ti % 2][:, dc * FR:(dc + 1) * FR],
                                              t4f[:], -QMAX - 1.0, s_col(4),
                                              ALU.max, ALU.mult))
                frag_hist.append(fv)
            if mode in ("e", "f1"):
                q_rel.append(fv)
            if mode == "d":
                stq = sch.step("sync", [(V, fv)], lambda s, ti=ti: s.dma_start(
                    out=upq_out[ti * 128:(ti + 1) * 128, :], in_=upq_t[ti % 2][:]),
                    inc=(STp[ti % 2], 16))
                stq = sch.step("sync", [], lambda s, ti=ti: s.dma_start(
                    out=gtq_out[ti * 128:(ti + 1) * 128, :], in_=gtq_t[ti % 2][:]),
                    inc=(STp[ti % 2], 16))
                q_store[ti] = stq
            if mode == "f1":
                stv = sch.step("sync", [(V, fv)], lambda s, ti=ti: s.dma_start(
                    out=o_out[ti * 128:(ti + 1) * 128, :], in_=of_t[ti % 2][:]),
                    inc=(STp[ti % 2], 16))
                o_store[ti] = stv

        if mode != "f1":
            nout = 2 if mode == "c" else 1
            last = fv
            for j in range(nout):
                tj = sch.step("tensor", [(V, last), (G, idf_done)],
                              lambda t, j=j: t.transpose(
                                  ps_m[0:1, 0:128], m_run[:, j:j + 1], identf[:]),
                              inc=(T, 1))
                sch.step("vector", [(T, tj)], lambda v: v.tensor_copy(
                    mrow128[:], ps_m[0:1, 0:128]))
                last = sch.step("vector", [], lambda v, j=j: v.tensor_reduce(
                    mres[0:1, j:j + 1], mrow128[:], AX.X, ALU.max))
            sch.step("sync", [(V, last)], lambda s: s.dma_start(
                out=m_out, in_=mres[0:1, 0:nout]), inc=(ST, 16))
        sch.step("gpsimd", [(ST, ST.n), (STB, STB.n)], lambda g: None)
        sch.emit(block)
    return nc



_STAGE_A = None


def _stage_a_nc():
    global _STAGE_A
    if _STAGE_A is None:
        _STAGE_A = build_stage_a()
    return _STAGE_A


def run_stage_a(inputs):
    x = np.asarray(inputs["L0_attn_x"], np.float32)
    wqT = np.ascontiguousarray(np.asarray(inputs["L0_attn_Wq"], np.float32).T)
    wkT = np.ascontiguousarray(np.asarray(inputs["L0_attn_Wk"], np.float32).T)
    wvT = np.ascontiguousarray(np.asarray(inputs["L0_attn_Wv"], np.float32).T)
    bqkv = np.stack([np.asarray(inputs["L0_attn_bq"], np.float32),
                     np.asarray(inputs["L0_attn_bk"], np.float32),
                     np.asarray(inputs["L0_attn_bv"], np.float32)])
    gb = np.stack([np.asarray(inputs["L0_attn_gamma"], np.float32),
                   np.asarray(inputs["L0_attn_beta"], np.float32)])
    wg = np.asarray(inputs["L0_ffn_Wg"], np.float32)
    posT = np.arange(N, dtype=np.float32).reshape(1, N)
    jfrac = (np.arange(0, C, 2, dtype=np.float32) / np.float32(C)).reshape(1, C // 2)

    nc = _stage_a_nc()
    in_maps = []
    for c in range(NCORES):
        xa = np.ascontiguousarray(x[c * BPC:(c + 1) * BPC].reshape(TPC, C))
        in_maps.append(dict(xA=xa, wqT=wqT, wkT=wkT, wvT=wvT, bqkv=bqkv, gb=gb,
                            wg=wg, posT=posT, jfrac=jfrac))
    res = run_bass_kernel_spmd(nc, in_maps, core_ids=list(range(NCORES)))
    x2 = np.stack([res.results[c]["x2_out"].reshape(BPC, N, C)
                   for c in range(NCORES)]).reshape(B, N, C)
    probs = np.stack([res.results[c]["probs_out"].reshape(BPC * N, 8)
                      for c in range(NCORES)]).reshape(B * N, 8)
    return x2, probs


def build_stage_f2():
    nc = bass.Bass("TRN2", target_bir_lowering=False)
    o_in = nc.dram_tensor("o_in", [TPC, 3072], BF16, kind="ExternalInput").ap()
    x4 = nc.dram_tensor("x4", [TPC, C], F32, kind="ExternalInput").ap()
    wd_nat = nc.dram_tensor("wd_nat", [C, 3072], F32, kind="ExternalInput").ap()
    y_out = nc.dram_tensor("y_out", [TPC, C], F32, kind="ExternalOutput").ap()

    NI3 = 24

    with ExitStack() as ctx:
        sbt = lambda n, s, d=F32: ctx.enter_context(nc.sbuf_tensor(n, s, d))
        pst = lambda n, s, d=F32: ctx.enter_context(nc.psum_tensor(n, s, d))
        ident = sbt("ident", [128, 128], BF16)
        wt_f = [sbt(f"wt_f{i}", [128, 3072]) for i in range(2)]
        wsc = sbt("wsc", [128, 96])
        wsc_r = sbt("wsc_r", [128, 96])
        wdq_b = sbt("wdq_b", [128, 3072], BF16)
        wdT = sbt("wdT", [128, NI3 * C], BF16)
        ot_b = [sbt(f"ot_b{i}", [128, 3072], BF16) for i in range(2)]
        oT = sbt("oT", [128, NI3 * 128], BF16)
        xt = [sbt(f"xt{i}", [128, C]) for i in range(2)]
        yt = [sbt(f"yt{i}", [128, C]) for i in range(2)]
        ps_t = [pst(f"ps_t{i}", [128, 128], BF16) for i in range(3)]
        ps_y1 = pst("ps_y1", [128, 512])
        ps_y2 = pst("ps_y2", [128, 512])

        S = {nm: Sem(ctx.enter_context(nc.semaphore(nm)))
             for nm in ("LD", "LDB", "LO", "LOB", "WL", "WLB", "V", "A", "T", "G", "ST", "STB")}
        (LD, LDB, LO, LOB, WL, WLB, V, A, T, G, ST, STB) = (
            S[n] for n in ("LD", "LDB", "LO", "LOB", "WL", "WLB", "V", "A", "T", "G",
                           "ST", "STB"))
        WLp = [WL, WLB]
        LDp = [LD, LDB]
        LOp = [LO, LOB]
        STp = [ST, STB]
        block = ctx.enter_context(nc.Block())
        sch = Sched({"vector": V, "scalar": A, "gpsimd": G})

        id_ms = sch.step("gpsimd", [], lambda g: g.memset(ident[:], 1.0))
        id_done = sch.step("gpsimd", [(G, id_ms)], lambda g: g.affine_select(
            ident[:], ident[:], pattern=[[-1, 128]], compare_op=ALU.is_equal,
            fill=0.0, base=0, channel_multiplier=1))

        psbf_last = [0]
        wdq_T = [0]
        last_v = 0
        for rc_ in range(KCH):
            buf = wt_f[rc_ % 2]
            lw = [(V, last_v)] if rc_ >= 2 else []
            ldw = sch.step("sync", lw, lambda s, rc_=rc_, buf=buf: s.dma_start(
                out=buf[:], in_=wd_nat[rc_ * 128:(rc_ + 1) * 128, :]),
                inc=(WLp[rc_ % 2], 16))
            dqw = [(WLp[rc_ % 2], ldw)]
            if wdq_T[0]:
                dqw.append((T, wdq_T[0]))
            sch.step("vector", dqw, lambda v, buf=buf: v.tensor_reduce(
                wsc[:], buf[:].rearrange("p (b k) -> p b k", k=32),
                AX.X, ALU.max, apply_absolute_value=True))
            sch.step("vector", [], lambda v: v.tensor_scalar(
                wsc[:], wsc[:], 1.0 / 7.0, 1e-12, ALU.mult, ALU.max))
            sch.step("vector", [], lambda v: v.reciprocal(wsc_r[:], wsc[:]))
            sc3 = bass.AP(wsc_r[:].tensor, 0, [[96, 128], [1, 96], [0, 32]])
            sc3b = bass.AP(wsc[:].tensor, 0, [[96, 128], [1, 96], [0, 32]])
            b3 = buf[:].rearrange("p (b k) -> p b k", k=32)
            sch.step("vector", [], lambda v, b3=b3, sc3=sc3: v.tensor_tensor(
                b3, b3, sc3, ALU.mult))
            sch.step("vector", [], lambda v, buf=buf: v.tensor_scalar(
                buf[:], buf[:], MAGIC, MAGIC, ALU.add, ALU.subtract))
            sch.step("vector", [], lambda v, buf=buf: v.tensor_scalar(
                buf[:], buf[:], 7.0, -8.0, ALU.min, ALU.max))
            lv = sch.step("vector", [], lambda v, b3=b3, sc3b=sc3b: v.tensor_tensor(
                wdq_b[:].rearrange("p (b k) -> p b k", k=32), b3, sc3b, ALU.mult))
            last_v = lv
            evB = []
            for c2 in range(NI3):
                slot = c2 % 3
                waits = [(V, lv)] if c2 == 0 else []
                if c2 >= 3:
                    waits.append((A, evB[c2 - 3]))
                if c2 == 0:
                    waits.append((G, id_done))
                    waits.append((A, psbf_last[0]))
                sch.step("tensor", waits, lambda t, c2=c2, slot=slot: t.transpose(
                    ps_t[slot][0:128, 0:128],
                    wdq_b[:, c2 * 128:(c2 + 1) * 128], ident[:]), inc=(T, 1))
                ev = sch.step("scalar", [(T, T.n)],
                              lambda s, c2=c2, slot=slot, rc_=rc_: s.copy(
                                  wdT[0:128, c2 * C + rc_ * 128:
                                      c2 * C + rc_ * 128 + 128],
                                  ps_t[slot][0:128, 0:128]))
                evB.append(ev)
            psbf_last[0] = evB[-1]
            wdq_T[0] = T.n
        wT_done = psbf_last[0]

        o_rel = []
        x_rel = []
        oT_T = [0]
        y_store = {}
        for ti in range(NT):
            w = [(T, o_rel[-2])] if len(o_rel) >= 2 else []
            ldo = sch.step("sync", w, lambda s, ti=ti: s.dma_start(
                out=ot_b[ti % 2][:], in_=o_in[ti * 128:(ti + 1) * 128, :]),
                inc=(LOp[ti % 2], 16))
            wx = [(V, x_rel[-2])] if len(x_rel) >= 2 else []
            ldx = sch.step("sync", wx, lambda s, ti=ti: s.dma_start(
                out=xt[ti % 2][:], in_=x4[ti * 128:(ti + 1) * 128, :]),
                inc=(LDp[ti % 2], 16))
            evA = []
            for c2 in range(NI3):
                slot = c2 % 3
                waits = [(LOp[ti % 2], ldo)] if c2 == 0 else []
                if c2 >= 3:
                    waits.append((A, evA[c2 - 3]))
                if c2 == 0:
                    waits.append((A, psbf_last[0]))
                    if oT_T[0]:
                        waits.append((T, oT_T[0]))
                sch.step("tensor", waits, lambda t, c2=c2, slot=slot, ti=ti:
                         t.transpose(ps_t[slot][0:128, 0:128],
                                     ot_b[ti % 2][:, c2 * 128:(c2 + 1) * 128],
                                     ident[:]), inc=(T, 1))
                ev = sch.step("scalar", [(T, T.n)], lambda s, c2=c2, slot=slot:
                              s.copy(oT[0:128, c2 * 128:(c2 + 1) * 128],
                                     ps_t[slot][0:128, 0:128]))
                evA.append(ev)
            psbf_last[0] = evA[-1]
            o_rel.append(T.n)
            def y_mm(t, ti=ti):
                r = None
                for c2 in range(NI3):
                    t.matmul(ps_y1[:, 0:512], oT[:, c2 * 128:(c2 + 1) * 128],
                             wdT[:, c2 * C:c2 * C + 512],
                             start=(c2 == 0), stop=(c2 == NI3 - 1))
                    r = t.matmul(ps_y2[:, 0:512], oT[:, c2 * 128:(c2 + 1) * 128],
                                 wdT[:, c2 * C + 512:c2 * C + 1024],
                                 start=(c2 == 0), stop=(c2 == NI3 - 1))
                return r
            yw = [(A, evA[-1])]
            if ti > 0:
                yw.append((V, prev_yv))
            ymm = sch.step("tensor", yw, y_mm, inc=(T, 1))
            oT_T[0] = ymm
            y1w = [(T, ymm), (LDp[ti % 2], ldx)]
            if (ti - 2) in y_store:
                y1w.append((STp[ti % 2], y_store[ti - 2]))
            sch.step("vector", y1w, lambda v, ti=ti: v.tensor_tensor(
                yt[ti % 2][:, 0:512], xt[ti % 2][:, 0:512], ps_y1[:, 0:512], ALU.add))
            yv = sch.step("vector", [], lambda v, ti=ti: v.tensor_tensor(
                yt[ti % 2][:, 512:1024], xt[ti % 2][:, 512:1024],
                ps_y2[:, 0:512], ALU.add))
            prev_yv = yv
            x_rel.append(yv)
            stv = sch.step("sync", [(V, yv)], lambda s, ti=ti: s.dma_start(
                out=y_out[ti * 128:(ti + 1) * 128, :], in_=yt[ti % 2][:]),
                inc=(STp[ti % 2], 16))
            y_store[ti] = stv
        sch.step("gpsimd", [(ST, ST.n), (STB, STB.n)], lambda g: None)
        sch.emit(block)
    return nc



_L1FFN = {}
_STAGE_F2 = None


def run_stage_l1ffn(mode, x4_full, mu4, rstd4, m_parts, inputs, wrelay=None,
                    qrelay=None):
    if mode not in _L1FFN:
        _L1FFN[mode] = build_stage_l1ffn(mode)
    m_all = np.zeros((1, 64), np.float32)
    for j, mp in enumerate(m_parts):
        m_all[0, j * 8:j * 8 + len(mp)] = np.asarray(mp).reshape(-1)
    in_maps = []
    for c in range(NCORES):
        x4c = np.ascontiguousarray(x4_full[c * BPC:(c + 1) * BPC].reshape(TPC, C))
        im = dict(x4=x4c, mu4c=np.ascontiguousarray(mu4[c]),
                  rstd4c=np.ascontiguousarray(rstd4[c]), m_all=m_all)
        if mode == "c":
            im["wu_nat"] = np.ascontiguousarray(
                np.asarray(inputs["L1_ffn_w_up"], np.float32))
            im["wg_nat"] = np.ascontiguousarray(
                np.asarray(inputs["L1_ffn_w_gate"], np.float32))
        elif mode == "d":
            im["wuT_in"], im["wgT_in"] = wrelay
        else:
            im["upq_in"] = qrelay[0][c]
            im["gtq_in"] = qrelay[1][c]
        in_maps.append(im)
    res = run_bass_kernel_spmd(_L1FFN[mode], in_maps, core_ids=list(range(NCORES)))
    if mode == "f1":
        o = np.stack([np.asarray(res.results[c]["o_out"]) for c in range(NCORES)])
        return o
    if mode == "c":
        wrelay_out = (np.asarray(res.results[0]["wuT_out"]),
                      np.asarray(res.results[0]["wgT_out"]))
        m = np.stack([np.asarray(res.results[c]["m_out"]).reshape(-1)
                      for c in range(NCORES)])
        return m, wrelay_out
    if mode == "d":
        q = ([np.asarray(res.results[c]["upq_out"]) for c in range(NCORES)],
             [np.asarray(res.results[c]["gtq_out"]) for c in range(NCORES)])
        m = np.stack([np.asarray(res.results[c]["m_out"]).reshape(-1)
                      for c in range(NCORES)])
        return m, q
    return np.stack([np.asarray(res.results[c]["m_out"]).reshape(-1)
                     for c in range(NCORES)])


def run_stage_f2(o_shards, x4_full, inputs):
    global _STAGE_F2
    if _STAGE_F2 is None:
        _STAGE_F2 = build_stage_f2()
    wd = np.ascontiguousarray(np.asarray(inputs["L1_ffn_w_down"], np.float32))
    in_maps = []
    for c in range(NCORES):
        x4c = np.ascontiguousarray(x4_full[c * BPC:(c + 1) * BPC].reshape(TPC, C))
        in_maps.append(dict(o_in=np.ascontiguousarray(o_shards[c]), x4=x4c, wd_nat=wd))
    res = run_bass_kernel_spmd(_STAGE_F2, in_maps, core_ids=list(range(NCORES)))
    y = np.stack([res.results[c]["y_out"].reshape(BPC, N, C)
                  for c in range(NCORES)]).reshape(B, N, C)
    return y


def kernel(**inputs):
    x2, probs = run_stage_a(inputs)
    x3 = run_stage_b1(x2, probs, inputs)
    x4, m0, mu4, rstd4 = run_stage_b2(x3, inputs)
    m12, wrelay = run_stage_l1ffn("c", x4, mu4, rstd4, [m0], inputs)
    m1 = m12[:, 0]
    m2 = m12[:, 1]
    m3, qrelay = run_stage_l1ffn("d", x4, mu4, rstd4, [m0, m1, m2], inputs,
                                 wrelay)
    m3 = m3[:, 0]
    m4 = run_stage_l1ffn("e", x4, mu4, rstd4, [m0, m1, m2, m3], inputs,
                         qrelay=qrelay)[:, 0]
    o = run_stage_l1ffn("f1", x4, mu4, rstd4, [m0, m1, m2, m3, m4], inputs,
                        qrelay=qrelay)
    y = run_stage_f2(o, x4, inputs)
    return y
